# version 01
# baseline (speedup 1.0000x reference)
import contextlib
import numpy as np

import concourse.bass as bass
import concourse.mybir as mybir
from concourse.bass_utils import run_bass_kernel_spmd

F32 = mybir.dt.float32
BF16 = mybir.dt.bfloat16
AF = mybir.ActivationFunctionType
ALU = mybir.AluOpType
AX = mybir.AxisListType

D = 1024
T = 2048
H = 16
HD = 64
NE = 16
INTER = 512
NB = 2
NDT = 8
TC = 512
NCH = T // TC
EPS = 1e-5

ENGINES = ("sync", "gpsimd", "vector", "scalar", "tensor")


class Ch:
    def __init__(self, handle):
        self.h = handle
        self.n = 0


class Emitter:

    def __init__(self, sems):
        self.q = {e: [] for e in ENGINES}
        self.ch = {name: Ch(h) for name, h in sems.items()}
        self.bar = {e: self.ch["bar_" + e] for e in ENGINES}
        self.bank_free = {}
        self.bank_ready = {}

    def on(self, eng, fn):
        self.q[eng].append(fn)

    def mk_inc(self, ch, k=1):
        ch.n += k
        h = ch.h

        def apply(instr):
            instr.then_inc(h, k)
            return instr
        return apply

    def tok(self, ch):
        ap = self.mk_inc(ch, 1)
        return ap, (ch, ch.n)

    def wait_tok(self, eng, tok):
        ch, v = tok
        self.wait_val(eng, ch, v)

    def wait_now(self, eng, ch):
        self.wait_val(eng, ch, ch.n)

    def wait_val(self, eng, ch, v):
        h = ch.h
        if v > 0:
            self.on(eng, lambda e: e.wait_ge(h, v))

    def barrier(self, engines=ENGINES):
        dma_chs = {"sync": ("w", "o"), "gpsimd": ("g",)}
        for e in engines:
            for cn in dma_chs.get(e, ()):
                self.wait_now(e, self.ch[cn])
        toks = {}
        for e in engines:
            ap = self.mk_inc(self.bar[e], 1)
            self.on(e, lambda eng, ap=ap: ap(eng.nop()))
            toks[e] = self.bar[e].n
        for e in engines:
            for o in engines:
                if o != e:
                    self.wait_val(e, self.bar[o], toks[o])

    def bank_acquire(self, bank):
        if bank in self.bank_free:
            self.wait_tok("tensor", self.bank_free[bank])

    def mark_ready(self, bank, ch):
        ap, tk = self.tok(ch)
        self.bank_ready[bank] = tk
        return ap

    def wait_ready(self, eng, bank):
        self.wait_tok(eng, self.bank_ready[bank])

    def mark_free(self, bank, eng):
        chmap = {"vector": "v", "scalar": "s", "gpsimd": "g", "sync": "w"}
        ap, tk = self.tok(self.ch[chmap[eng]])
        self.bank_free[bank] = tk
        return ap

    def free_after(self, bank, eng):
        chmap = {"vector": "v", "scalar": "s", "gpsimd": "g", "sync": "w"}
        ch = self.ch[chmap[eng]]
        self.bank_free[bank] = (ch, ch.n)

    def run(self, block):
        for e in ENGINES:
            fns = self.q[e]

            def body(eng, fns=fns):
                for fn in fns:
                    fn(eng)
            getattr(block, e)(body)


def build_program(phases=("a", "f1", "b", "f2"), dumps=()):
    nc = bass.Bass("TRN2", target_bir_lowering=False)

    din = {}

    def I(name, shape, dt=F32):
        din[name] = nc.dram_tensor(name, list(shape), dt,
                                   kind="ExternalInput").ap()
        return din[name]

    x_in = I("x", (NB, D, T))
    I("wqkv", (8, 8, 128, 384))
    I("wout", (8, 8, 128, 128))
    for nm in ("wr", "wk", "wv", "wo"):
        I(nm, (8, 8, 128, 128))
    I("we1", (NE, 8, 128, 512))
    I("we3", (NE, 8, 128, 512))
    I("we2", (8, NE, 4, 128, 128))
    I("ws1", (8, 8, 128, 128))
    I("ws3", (8, 8, 128, 128))
    I("ws2", (8, 8, 128, 128))
    I("gwt", (128, 128))
    I("be2m", (16, 1024))
    I("gateb", (1, 16))
    I("cc", (128, 224))
    I("idf", (128, 128))
    I("ones128", (128, 128))
    I("onesrow", (1, 128))
    I("oneslo", (1, 128))
    I("oneshi", (1, 128))
    I("stair", (128, 896))
    I("hsel", (128, 2))
    I("exh", (16, 2048))
    I("hselt", (2, 128))

    y_out = nc.dram_tensor("y", [NB, D, T], F32, kind="ExternalOutput").ap()

    dstg = {}
    for nm, shp in (
        ("wqkv_b", (8, 8, 128, 384)), ("wout_b", (8, 8, 128, 128)),
        ("wr_b", (8, 8, 128, 128)), ("wk_b", (8, 8, 128, 128)),
        ("wv_b", (8, 8, 128, 128)), ("wo_b", (8, 8, 128, 128)),
        ("we1_b", (NE, 8, 128, 512)), ("we3_b", (NE, 8, 128, 512)),
        ("we2_b", (8, NE, 4, 128, 128)),
        ("ws1_b", (8, 8, 128, 128)), ("ws3_b", (8, 8, 128, 128)),
        ("ws2_b", (8, 8, 128, 128)),
    ):
        dstg[nm] = nc.dram_tensor(nm, list(shp), BF16, kind="Internal").ap()

    dump_outs = {}
    for nm in dumps:
        shp, dt = {"R": ([128, 16384], F32), "A1": ([128, 16384], BF16),
                   "A2": ([128, 16384], BF16), "A3": ([128, 6144], BF16),
                   "SCF": ([128, 4096], F32), "CT": ([128, 64], F32),
                   "SS": ([128, 8], F32), "SQ": ([128, 8], F32),
                   "DC": ([128, 64], F32)}[nm]
        dump_outs[nm] = nc.dram_tensor("dump_" + nm, shp, dt,
                                       kind="ExternalOutput").ap()

    es = contextlib.ExitStack()
    with es:
        sbt = lambda n, p, f, dt: es.enter_context(
            nc.sbuf_tensor(n, [p, f], dt))
        R = sbt("R", 128, 16384, F32)
        A1 = sbt("A1", 128, 16384, BF16)
        A2 = sbt("A2", 128, 16384, BF16)
        A3 = sbt("A3", 128, 6144, BF16)
        W1 = sbt("W1", 128, 8192, BF16)
        W2 = sbt("W2", 128, 8192, BF16)
        SCF = sbt("SCF", 128, 4096, F32)
        SC2 = sbt("SC2", 128, 512, F32)
        SCB = sbt("SCB", 128, 2048, BF16)
        GP = SCF[0:18, 3072:3584]
        GT = SCF[:, 2112:2184]
        CT = SCF[:, 2048:2112]
        CTROW = SCF[0:16, 3584:4096]
        CTB = sbt("CTB", 16, 512, BF16)
        CC = sbt("CC", 128, 224, F32)
        DC = sbt("DC", 128, 64, F32)
        GWT = sbt("GWT", 128, 128, F32)
        CS = sbt("CS", 128, 16, F32)
        BE2 = sbt("BE2", 16, 1024, BF16)
        GB = sbt("GB", 128, 16, F32)
        IDF = sbt("IDF", 128, 128, F32)
        IDB = sbt("IDB", 128, 128, BF16)
        ONES128B = sbt("ONES128B", 128, 128, BF16)
        ONESROWF = sbt("ONESROWF", 1, 128, F32)
        ONESROWB = sbt("ONESROWB", 1, 128, BF16)
        ONESLO = sbt("ONESLO", 1, 128, F32)
        ONESHI = sbt("ONESHI", 1, 128, F32)
        STAIRB = sbt("STAIRB", 128, 896, BF16)
        HSEL = sbt("HSEL", 128, 2, F32)
        EXHB = sbt("EXHB", 16, 2048, BF16)
        HSELT = sbt("HSELT", 2, 128, F32)
        SS = SCF[:, 2184:2192]
        SQ = SCF[:, 2192:2200]
        CAB = SCF[:, 2200:2216]
        SCL = SCF[:, 2216:2224]
        BIA = SCF[:, 2224:2232]

        PS = [es.enter_context(nc.psum_tensor("PB%d" % i, [128, 512], F32))
              for i in range(6)]
        PTB = es.enter_context(nc.psum_tensor("PTB", [128, 128], BF16))
        PTF = es.enter_context(nc.psum_tensor("PTF", [128, 128], F32))

        sems = {}
        for nm in ["w", "g", "t", "s", "v", "o"] + ["bar_" + e
                                                    for e in ENGINES]:
            sems[nm] = es.enter_context(nc.semaphore("sem_" + nm))

        block = es.enter_context(nc.Block())
        em = Emitter(sems)
        chW, chG, chT, chS, chV, chO = (em.ch[k] for k in
                                        ("w", "g", "t", "s", "v", "o"))

        def dma(eng, dst, src, ch):
            ap = em.mk_inc(ch, 16)
            em.on(eng, lambda e: ap(e.dma_start(out=dst, in_=src)))

        def s_act(out, in_, func, bias=0.0, scale=1.0, accum=None, apl=()):
            ap = em.mk_inc(chS, 1)

            def f(s):
                if accum is not None:
                    ins = s.activation(out=out, in_=in_, func=func, bias=bias,
                                       scale=scale, accum_out=accum)
                else:
                    ins = s.activation(out=out, in_=in_, func=func, bias=bias,
                                       scale=scale)
                ap(ins)
                for a in apl:
                    a(ins)
                return ins
            em.on("scalar", f)

        def v_op(fn, apl=()):
            ap = em.mk_inc(chV, 1)

            def f(v):
                ins = fn(v)
                ap(ins)
                for a in apl:
                    a(ins)
                return ins
            em.on("vector", f)

        import traceback

        def t_mm(out, lhsT, rhs, start, stop, apl=()):
            site = traceback.extract_stack(limit=3)[0]

            def f(t):
                try:
                    ins = t.matmul(out, lhsT, rhs, start=start, stop=stop)
                except Exception:
                    print("t_mm failed from %s:%d" % (site.filename,
                                                      site.lineno))
                    raise
                for a in apl:
                    a(ins)
                return ins
            em.on("tensor", f)

        def t_tr(out, in_, ident, apl=()):
            def f(t):
                ins = t.transpose(out, in_, ident)
                for a in apl:
                    a(ins)
                return ins
            em.on("tensor", f)

        def Rt(dt, c):
            return R[:, dt * 2048 + c * TC: dt * 2048 + c * TC + TC]

        dma("sync", CC[:], din["cc"][:], chW)
        dma("sync", GWT[:], din["gwt"][:], chW)
        dma("sync", IDF[:], din["idf"][:], chW)
        dma("sync", ONESROWF[:], din["onesrow"][:], chW)
        dma("sync", ONESLO[:], din["oneslo"][:], chW)
        dma("sync", ONESHI[:], din["oneshi"][:], chW)
        dma("sync", HSEL[:], din["hsel"][:], chW)
        dma("sync", HSELT[:], din["hselt"][:], chW)
        dma("sync", GB[:], din["gateb"].broadcast_to([128, 16]), chW)
        dma("gpsimd", STAIRB[:], din["stair"][:], chG)
        dma("gpsimd", ONES128B[:], din["ones128"][:], chG)
        dma("gpsimd", IDB[:], din["idf"][:], chG)
        dma("gpsimd", ONESROWB[:], din["onesrow"][:], chG)
        dma("gpsimd", BE2[:], din["be2m"][:], chG)
        dma("gpsimd", EXHB[:], din["exh"][:], chG)

        for e in ("scalar", "vector", "tensor"):
            em.wait_now(e, chW)
            em.wait_now(e, chG)

        s_act(DC[:, 0:8], CC[:, 24:32], AF.Exp)
        s_act(DC[:, 0:8], DC[:, 0:8], AF.Exp, scale=-1.0)
        s_act(DC[:, 8:16], CC[:, 32:40], AF.Exp)
        em.wait_now("vector", chS)

        def vsync0():
            ap = em.mk_inc(chV, 1)
            em.on("vector", lambda v, ap=ap: ap(v.engine_nop()))
            em.wait_now("vector", chV)

        v_op(lambda v: v.tensor_mul(DC[:, 16:24], DC[:, 0:8], DC[:, 8:16]))
        vsync0()
        v_op(lambda v: v.tensor_scalar(DC[:, 16:24], DC[:, 16:24], -1.0, 1.0,
                                       op0=ALU.mult, op1=ALU.add))
        vsync0()
        for j in range(3):
            v_op(lambda v, j=j: v.tensor_scalar(
                DC[:, 24 + 8 * j:32 + 8 * j], CC[:, 8 * j:8 * j + 8],
                -1.0, 1.0, op0=ALU.mult, op1=ALU.add))
        v_op(lambda v: v.memset(SCF[:, 0:1], 1.0))
        em.wait_now("tensor", chV)
        for kt in range(8):
            apl = [em.mark_ready(0, chT)] if kt == 7 else ()
            t_mm(PS[0][0:1, 0:16], SCF[:, 0:1], GWT[:, kt * 16:kt * 16 + 16],
                 start=(kt == 0), stop=(kt == 7), apl=apl)
        em.wait_ready("vector", 0)
        v_op(lambda v: v.tensor_copy(SCF[0:1, 4:20], PS[0][0:1, 0:16]))
        em.free_after(0, "vector")
        em.wait_now("tensor", chV)
        t_mm(PTF[:, 0:16], ONESROWF[:], SCF[0:1, 4:20], True, True,
             apl=[em.mark_ready(6, chT)])
        em.wait_ready("vector", 6)
        v_op(lambda v: v.tensor_copy(CS[:], PTF[:, 0:16]))
        em.free_after(6, "vector")

        state = {"flip": 0, "pend": [0, 0]}

        def stage(dst_ap, src_ap, cols):
            slot = state["flip"]
            state["flip"] = 1 - slot
            buf = W1[:, slot * 4096: slot * 4096 + cols]
            shp = list(src_ap.shape)
            if len(shp) == 3:
                buf = buf.rearrange("p (a c) -> p a c", a=shp[1])
            elif len(shp) == 4:
                buf = buf.rearrange("p (a b c) -> p a b c", a=shp[1],
                                    b=shp[2])
            if state["pend"][slot]:
                em.wait_val("gpsimd", chG, state["pend"][slot])
            ap1 = em.mk_inc(chG, 16)
            em.on("gpsimd", lambda g, b=buf, s=src_ap: ap1(
                g.dma_start(out=b, in_=s)))
            v1 = chG.n
            em.wait_val("gpsimd", chG, v1)
            ap2 = em.mk_inc(chG, 16)
            em.on("gpsimd", lambda g, b=buf, d=dst_ap: ap2(
                g.dma_start(out=d, in_=b)))
            state["pend"][slot] = chG.n

        for s in range(8):
            stage(dstg["wqkv_b"][s].rearrange("kt p c -> p kt c"),
                  din["wqkv"][s].rearrange("kt p c -> p kt c"), 3072)
        for nm in ("wout", "wr", "wk", "wv", "wo"):
            for eh in range(4):
                stage(dstg[nm + "_b"][2 * eh:2 * eh + 2].rearrange(
                          "e kt p c -> p e kt c"),
                      din[nm][2 * eh:2 * eh + 2].rearrange(
                          "e kt p c -> p e kt c"), 2048)
        for nm in ("we1", "we3"):
            for e in range(NE):
                stage(dstg[nm + "_b"][e].rearrange("kt p c -> p kt c"),
                      din[nm][e].rearrange("kt p c -> p kt c"), 4096)
        for dt in range(8):
            for half in range(2):
                stage(dstg["we2_b"][dt, 8 * half:8 * half + 8].rearrange(
                          "e it p c -> p e it c"),
                      din["we2"][dt, 8 * half:8 * half + 8].rearrange(
                          "e it p c -> p e it c"), 4096)
        for nm in ("ws1", "ws3", "ws2"):
            for ih in range(4):
                stage(dstg[nm + "_b"][2 * ih:2 * ih + 2].rearrange(
                          "i kt p c -> p i kt c"),
                      din[nm][2 * ih:2 * ih + 2].rearrange(
                          "i kt p c -> p i kt c"), 2048)
        em.barrier()

        scb_guard = [0, 0]
        scb_ffn_guard = [0, 0]

        def ln_chunk(c, dest_tile_fn):
            for bank in (0, 1):
                em.bank_acquire(bank)
            for dt in range(NDT):
                par = dt % 2
                xb = SCB[:, par * 1024: par * 1024 + 512]
                x2b = SCB[:, par * 1024 + 512: par * 1024 + 1024]
                if scb_guard[par]:
                    em.wait_val("scalar", chT, scb_guard[par])
                if scb_ffn_guard[par]:
                    em.wait_val("scalar", chV, scb_ffn_guard[par])
                s_act(xb, Rt(dt, c), AF.Copy)
                s_act(x2b, Rt(dt, c), AF.Square)
                em.wait_now("tensor", chS)
                apl0 = [em.mark_ready(0, chT)] if dt == NDT - 1 else ()
                t_mm(PS[0][0:1, 0:512], ONES128B[:, 0:1], xb,
                     start=(dt == 0), stop=(dt == NDT - 1), apl=apl0)
                if dt == NDT - 1:
                    apl1 = [em.mark_ready(1, chT)]
                else:
                    apl1 = [em.mk_inc(chT, 1)]
                t_mm(PS[1][0:1, 0:512], ONES128B[:, 0:1], x2b,
                     start=(dt == 0), stop=(dt == NDT - 1), apl=apl1)
                scb_guard[par] = chT.n
            em.wait_ready("vector", 0)
            em.wait_ready("vector", 1)
            arow = SCF[0:1, 0:512]
            brow = SCF[0:1, 512:1024]
            vrow = SCF[0:1, 1024:1536]
            mrow = SCF[0:1, 1536:2048]
            v_op(lambda v: v.tensor_scalar_mul(mrow, PS[0][0:1, 0:512],
                                               1.0 / D))
            em.free_after(0, "vector")
            v_op(lambda v: v.tensor_scalar_mul(vrow, PS[1][0:1, 0:512],
                                               1.0 / D))
            em.free_after(1, "vector")
            v_op(lambda v: v.tensor_mul(arow, mrow, mrow))
            v_op(lambda v: v.tensor_sub(vrow, vrow, arow))
            em.wait_now("scalar", chV)
            s_act(arow, vrow, AF.Ln, bias=CC[0:1, 208:209])
            s_act(arow, arow, AF.Exp, scale=-0.5)
            em.wait_now("vector", chS)
            v_op(lambda v: v.tensor_mul(brow, mrow, arow))
            v_op(lambda v: v.tensor_scalar_mul(brow, brow, -1.0))
            em.bank_acquire(2)
            em.bank_acquire(3)
            em.wait_now("tensor", chV)
            t_mm(PS[2][:, 0:512], ONESROWF[:], arow, True, True,
                 apl=[em.mark_ready(2, chT)])
            t_mm(PS[3][:, 0:512], ONESROWF[:], brow, True, True,
                 apl=[em.mark_ready(3, chT)])
            em.wait_ready("vector", 2)
            em.wait_ready("vector", 3)
            t1 = SC2[:, 0:512]
            for dt in range(NDT):
                v_op(lambda v, dt=dt: v.tensor_mul(t1, Rt(dt, c),
                                                   PS[2][:, 0:512]))
                v_op(lambda v, dt=dt: v.tensor_add(dest_tile_fn(dt), t1,
                                                   PS[3][:, 0:512]))
                if dt == NDT - 1:
                    em.free_after(2, "vector")
                    em.free_after(3, "vector")

        def phase_attn_a(b):
            for c in range(NCH):
                ln_chunk(c, lambda dt, c=c: A1[:, dt * 2048 + c * TC:
                                               dt * 2048 + c * TC + TC])
            em.barrier(("tensor", "vector", "scalar", "sync"))

            VT = W2[:, 0:1040]
            PT = [W2[:, 1040 + i * 512: 1040 + (i + 1) * 512]
                  for i in range(4)]
            wslab = [W1[:, 0:3072], W1[:, 4096:4096 + 3072]]
            slab_owner = [0, 0]
            for s in range(8):
                slot = s % 2
                if slab_owner[slot]:
                    em.wait_val("sync", chT, slab_owner[slot])
                dma("sync",
                    wslab[slot].rearrange("p (kt c) -> p kt c", kt=8),
                    dstg["wqkv_b"][s].rearrange("kt p c -> p kt c"), chW)
                em.wait_now("tensor", chW)
                for mt in range(3):
                    for c in range(NCH):
                        bank = 4 + (mt * NCH + c) % 2
                        em.bank_acquire(bank)
                        for kt in range(8):
                            last = kt == 7
                            apl = [em.mark_ready(bank, chT)] if last else ()
                            t_mm(PS[bank][:, 0:512],
                                 wslab[slot][:, kt * 384 + mt * 128:
                                             kt * 384 + mt * 128 + 128],
                                 A1[:, kt * 2048 + c * TC:
                                    kt * 2048 + c * TC + TC],
                                 start=(kt == 0), stop=last, apl=apl)
                        em.wait_ready("scalar", bank)
                        dst = A3[:, mt * 2048 + c * TC:
                                 mt * 2048 + c * TC + TC]
                        s_act(dst, PS[bank][:, 0:512], AF.Copy)
                        em.free_after(bank, "scalar")
                slab_owner[slot] = chT.n

                pt_guard = [0, 0, 0, 0]
                for par in range(2):
                    h = 2 * s + par
                    rows = slice(par * 64, par * 64 + 64)
                    em.wait_now("tensor", chS)
                    em.wait_now("vector", chT)
                    for kt in range(16):
                        em.bank_acquire("ptb")
                        t_tr(PTB[:, 0:64],
                             A3[rows, 2 * 2048 + kt * 128:
                                2 * 2048 + kt * 128 + 128],
                             IDB[rows, par * 64: par * 64 + 64],
                             apl=[em.mark_ready("ptb", chT)])
                        em.wait_ready("vector", "ptb")
                        v_op(lambda v, kt=kt: v.tensor_copy(
                            VT[:, kt * 65: kt * 65 + 64], PTB[:, 0:64]))
                        em.free_after("ptb", "vector")
                        v_op(lambda v, kt=kt: v.memset(
                            VT[:, kt * 65 + 64: kt * 65 + 65], 1.0))
                    em.wait_now("tensor", chV)
                    for qc in range(NCH):
                        nkt = 4 * (qc + 1)
                        em.bank_acquire(4)
                        for kt in range(nkt):
                            sbank = 2 + kt % 2
                            em.bank_acquire(sbank)
                            t_mm(PS[sbank][:, 0:512],
                                 A3[rows, 1 * 2048 + kt * 128:
                                    1 * 2048 + kt * 128 + 128],
                                 A3[rows, 0 * 2048 + qc * TC:
                                    0 * 2048 + qc * TC + TC],
                                 start=True, stop=True,
                                 apl=[em.mark_ready(sbank, chT)])
                            em.wait_ready("scalar", sbank)
                            pt = PT[kt % 4]
                            if pt_guard[kt % 4]:
                                em.wait_val("scalar", chT, pt_guard[kt % 4])
                            s_act(pt, PS[sbank][:, 0:512], AF.Exp,
                                  scale=0.125)
                            em.free_after(sbank, "scalar")
                            dd = kt - 4 * qc
                            if dd >= 0:
                                off = 384 - 128 * dd
                                em.wait_now("vector", chS)
                                v_op(lambda v, pt=pt, off=off: v.tensor_mul(
                                    pt, pt, STAIRB[:, off:off + 512]))
                                em.wait_now("tensor", chV)
                            else:
                                em.wait_now("tensor", chS)
                            if kt == nkt - 1:
                                apl2 = [em.mark_ready(4, chT)]
                            else:
                                apl2 = [em.mk_inc(chT, 1)]
                            t_mm(PS[4][0:65, 0:512],
                                 VT[:, kt * 65: kt * 65 + 65], pt,
                                 start=(kt == 0), stop=(kt == nkt - 1),
                                 apl=apl2)
                            pt_guard[kt % 4] = chT.n
                        em.wait_ready("scalar", 4)
                        odst = A2[rows, s * 2048 + qc * TC:
                                  s * 2048 + qc * TC + TC]
                        s_act(odst, PS[4][0:64, 0:512], AF.Copy)
                        drow = SCF[0:1, par * 2048 + qc * TC:
                                   par * 2048 + qc * TC + TC]
                        s_act(drow, PS[4][64:65, 0:512], AF.Copy)
                        em.free_after(4, "scalar")
                em.wait_now("tensor", chS)
                for qc in range(NCH):
                    em.bank_acquire(5)
                    t_mm(PS[5][:, 0:512], ONESLO[:],
                         SCF[0:1, qc * TC: qc * TC + TC],
                         start=True, stop=False)
                    t_mm(PS[5][:, 0:512], ONESHI[:],
                         SCF[0:1, 2048 + qc * TC: 2048 + qc * TC + TC],
                         start=False, stop=True,
                         apl=[em.mark_ready(5, chT)])
                    em.wait_ready("scalar", 5)
                    rec = SC2[:, 0:512]
                    s_act(rec, PS[5][:, 0:512], AF.Ln)
                    em.free_after(5, "scalar")
                    s_act(rec, rec, AF.Exp, scale=-1.0)
                    em.wait_now("vector", chS)
                    ot = A2[:, s * 2048 + qc * TC: s * 2048 + qc * TC + TC]
                    v_op(lambda v, ot=ot: v.tensor_mul(ot, ot, rec))

            em.barrier(("tensor", "vector", "scalar", "sync"))
            wslot = [W1[:, 0:1024], W1[:, 4096:4096 + 1024]]
            owner = [0, 0]
            for et in range(8):
                sl = et % 2
                if owner[sl]:
                    em.wait_val("sync", chT, owner[sl])
                dma("sync",
                    wslot[sl].rearrange("p (kt c) -> p kt c", kt=8),
                    dstg["wout_b"][et].rearrange("kt p c -> p kt c"), chW)
                em.wait_now("tensor", chW)
                for c in range(NCH):
                    bank = 4 + c % 2
                    em.bank_acquire(bank)
                    for kt in range(8):
                        last = kt == 7
                        apl = [em.mark_ready(bank, chT)] if last else ()
                        t_mm(PS[bank][:, 0:512],
                             wslot[sl][:, kt * 128: kt * 128 + 128],
                             A2[:, kt * 2048 + c * TC:
                                kt * 2048 + c * TC + TC],
                             start=(kt == 0), stop=last, apl=apl)
                    em.wait_ready("vector", bank)
                    v_op(lambda v, et=et, c=c, bank=bank: v.tensor_add(
                        Rt(et, c), Rt(et, c), PS[bank][:, 0:512]))
                    em.free_after(bank, "vector")
                owner[sl] = chT.n
            em.barrier()


        def phase_ffn(b):
            eslot_guard = [0, 0, 0]
            w2hi_guard = [0]
            for c in range(NCH):
                ln_chunk(c, lambda dt: A3[:, dt * 512: dt * 512 + 512])
                em.bank_acquire(0)
                for kt in range(8):
                    apl = [em.mark_ready(0, chT)] if kt == 7 else ()
                    t_mm(PS[0][0:16, 0:512], GWT[:, kt * 16: kt * 16 + 16],
                         Rt(kt, c), start=(kt == 0), stop=(kt == 7), apl=apl)
                em.wait_ready("scalar", 0)
                GPS = SCF[0:18, 3072:3584]
                s_act(GPS[0:16, :], PS[0][0:16, 0:512], AF.Copy)
                em.free_after(0, "scalar")
                arow = SCF[0:1, 0:512]
                brow = SCF[0:1, 512:1024]
                em.wait_now("gpsimd", chV)
                em.wait_now("gpsimd", chS)
                dma("gpsimd", SCF[16:17, 3072:3584], arow, chG)
                dma("gpsimd", SCF[17:18, 3072:3584], brow, chG)
                em.wait_now("tensor", chG)

                def vsync():
                    ap = em.mk_inc(chV, 1)
                    em.on("vector", lambda v, ap=ap: ap(v.engine_nop()))
                    em.wait_now("vector", chV)

                WGS = []
                for j in range(4):
                    base = 2240 + j * 192
                    W = {}
                    o = SCF[:, base:base + 192]
                    W["gt"] = o[:, 0:18]
                    W["t0"] = o[:, 18:34]
                    W["sc"] = o[:, 34:50]
                    W["p0"] = o[:, 50:66]
                    W["s2"] = o[:, 66:82]
                    W["gm"] = o[:, 82:90]
                    W["gs"] = o[:, 90:94]
                    W["keep"] = o[:, 94:98]
                    W["ke"] = o[:, 98:114]
                    W["s2k"] = o[:, 114:130]
                    W["e1"] = o[:, 130:146]
                    W["e2"] = o[:, 146:162]
                    W["eqg"] = o[:, 162:166]
                    W["tmp"] = o[:, 166:182]
                    W["nmx"] = o[:, 182:183]
                    W["sume"] = o[:, 183:184]
                    W["rcp"] = o[:, 184:185]
                    W["r1"] = o[:, 185:186]
                    W["r2"] = o[:, 186:187]
                    W["m1"] = o[:, 187:188]
                    W["m2"] = o[:, 188:189]
                    WGS.append(W)
                for j in range(4):
                    em.bank_acquire("ptf")
                    em.wait_now("tensor", chS)
                    t_tr(PTF[:, 0:18], GPS[0:18, j * 128: j * 128 + 128],
                         IDF[0:18, 0:18], apl=[em.mark_ready("ptf", chT)])
                    em.wait_ready("vector", "ptf")
                    v_op(lambda v, j=j: v.tensor_copy(WGS[j]["gt"],
                                                      PTF[:, 0:18]))
                    em.free_after("ptf", "vector")
                vsync()
                for j in range(4):
                    W = WGS[j]
                    v_op(lambda v, W=W: v.tensor_scalar(
                        W["t0"], W["gt"][:, 0:16], W["gt"][:, 16:17], None,
                        op0=ALU.mult))
                vsync()
                for j in range(4):
                    W = WGS[j]
                    v_op(lambda v, W=W: v.scalar_tensor_tensor(
                        W["sc"], CS[:], W["gt"][:, 17:18], W["t0"],
                        op0=ALU.mult, op1=ALU.add))
                vsync()
                for j in range(4):
                    W = WGS[j]
                    v_op(lambda v, W=W: v.reduce_max(W["nmx"], W["sc"],
                                                     axis=AX.X, negate=True))
                vsync()
                em.wait_now("scalar", chV)
                for j in range(4):
                    W = WGS[j]
                    s_act(W["p0"], W["sc"], AF.Exp, bias=W["nmx"],
                          accum=W["sume"])
                for j in range(4):
                    W = WGS[j]
                    s_act(W["rcp"], W["sume"], AF.Ln)
                for j in range(4):
                    W = WGS[j]
                    s_act(W["rcp"], W["rcp"], AF.Exp, scale=-1.0)
                em.wait_now("vector", chS)
                for j in range(4):
                    W = WGS[j]
                    v_op(lambda v, W=W: v.tensor_scalar(
                        W["p0"], W["p0"], W["rcp"], None, op0=ALU.mult))
                vsync()
                for j in range(4):
                    W = WGS[j]
                    v_op(lambda v, W=W: v.tensor_add(W["s2"], W["p0"], GB[:]))
                vsync()
                for gi in range(4):
                    for j in range(4):
                        W = WGS[j]
                        v_op(lambda v, W=W, gi=gi: v.reduce_max(
                            W["gm"][:, gi:gi + 1],
                            W["s2"][:, 4 * gi:4 * gi + 4], axis=AX.X))
                vsync()
                for gi in range(4):
                    for j in range(4):
                        W = WGS[j]
                        v_op(lambda v, W=W, gi=gi: v.tensor_scalar(
                            W["eqg"], W["s2"][:, 4 * gi:4 * gi + 4],
                            W["gm"][:, gi:gi + 1], None, op0=ALU.is_equal))
                    vsync()
                    for j in range(4):
                        W = WGS[j]
                        v_op(lambda v, W=W, gi=gi: v.scalar_tensor_tensor(
                            W["tmp"][:, 0:4], W["eqg"], -1e30,
                            W["s2"][:, 4 * gi:4 * gi + 4],
                            op0=ALU.mult, op1=ALU.add))
                    vsync()
                    for j in range(4):
                        W = WGS[j]
                        v_op(lambda v, W=W, gi=gi: v.reduce_max(
                            W["gm"][:, 4 + gi:5 + gi], W["tmp"][:, 0:4],
                            axis=AX.X))
                    vsync()
                for j in range(4):
                    W = WGS[j]
                    v_op(lambda v, W=W: v.tensor_add(
                        W["gs"], W["gm"][:, 0:4], W["gm"][:, 4:8]))
                vsync()
                for j in range(4):
                    W = WGS[j]
                    v_op(lambda v, W=W: v.reduce_max(W["r1"], W["gs"],
                                                     axis=AX.X))
                vsync()
                for j in range(4):
                    W = WGS[j]
                    v_op(lambda v, W=W: v.tensor_scalar(
                        W["eqg"], W["gs"], W["r1"], None, op0=ALU.is_equal))
                vsync()
                for j in range(4):
                    W = WGS[j]
                    v_op(lambda v, W=W: v.scalar_tensor_tensor(
                        W["tmp"][:, 0:4], W["eqg"], -1e30, W["gs"],
                        op0=ALU.mult, op1=ALU.add))
                vsync()
                for j in range(4):
                    W = WGS[j]
                    v_op(lambda v, W=W: v.reduce_max(W["r2"],
                                                     W["tmp"][:, 0:4],
                                                     axis=AX.X))
                vsync()
                for j in range(4):
                    W = WGS[j]
                    v_op(lambda v, W=W: v.tensor_scalar(
                        W["keep"], W["gs"], W["r2"], None, op0=ALU.is_ge))
                vsync()
                for j in range(4):
                    W = WGS[j]
                    v_op(lambda v, W=W: v.memset(W["ke"], 1.0))
                vsync()
                for gi in range(4):
                    for j in range(4):
                        W = WGS[j]
                        v_op(lambda v, W=W, gi=gi: v.tensor_scalar(
                            W["ke"][:, 4 * gi:4 * gi + 4],
                            W["ke"][:, 4 * gi:4 * gi + 4],
                            W["keep"][:, gi:gi + 1], None, op0=ALU.mult))
                    vsync()
                for j in range(4):
                    W = WGS[j]
                    v_op(lambda v, W=W: v.tensor_scalar(
                        W["ke"], W["ke"], 1e30, -1e30,
                        op0=ALU.mult, op1=ALU.add))
                vsync()
                for j in range(4):
                    W = WGS[j]
                    v_op(lambda v, W=W: v.tensor_add(W["s2k"], W["s2"],
                                                     W["ke"]))
                vsync()
                for j in range(4):
                    W = WGS[j]
                    v_op(lambda v, W=W: v.reduce_max(W["m1"], W["s2k"],
                                                     axis=AX.X))
                vsync()
                for j in range(4):
                    W = WGS[j]
                    v_op(lambda v, W=W: v.tensor_scalar(
                        W["e1"], W["s2k"], W["m1"], None, op0=ALU.is_equal))
                vsync()
                for j in range(4):
                    W = WGS[j]
                    v_op(lambda v, W=W: v.scalar_tensor_tensor(
                        W["tmp"], W["e1"], -1e30, W["s2k"],
                        op0=ALU.mult, op1=ALU.add))
                vsync()
                for j in range(4):
                    W = WGS[j]
                    v_op(lambda v, W=W: v.reduce_max(W["m2"], W["tmp"],
                                                     axis=AX.X))
                vsync()
                for j in range(4):
                    W = WGS[j]
                    v_op(lambda v, W=W: v.tensor_scalar(
                        W["e2"], W["tmp"], W["m2"], None, op0=ALU.is_equal))
                vsync()
                for j in range(4):
                    W = WGS[j]
                    v_op(lambda v, W=W: v.tensor_add(W["e1"], W["e1"],
                                                     W["e2"]))
                vsync()
                for j in range(4):
                    W = WGS[j]
                    v_op(lambda v, W=W, j=j: v.tensor_mul(
                        CT[:, j * 16: j * 16 + 16], W["p0"], W["e1"]))
                vsync()
                em.wait_now("tensor", chV)
                for j in range(4):
                    em.bank_acquire("ptf")
                    t_tr(PTF[0:16, 0:128], CT[:, j * 16: j * 16 + 16],
                         IDF[:, 0:128], apl=[em.mark_ready("ptf", chT)])
                    em.wait_ready("scalar", "ptf")
                    s_act(CTROW[0:16, j * 128: j * 128 + 128],
                          PTF[0:16, 0:128], AF.Copy)
                    em.free_after("ptf", "scalar")
                em.wait_now("vector", chS)
                v_op(lambda v: v.tensor_copy(CTB[:], CTROW[:]))

                shguard = [0, 0]
                for it in range(8):
                    sl = it % 2
                    base = 4096 + sl * 1024
                    if it < 2 and w2hi_guard[0]:
                        em.wait_val("sync", chT, w2hi_guard[0])
                    if shguard[sl]:
                        em.wait_val("sync", chT, shguard[sl])
                    dma("sync", W2[:, base:base + 1024].rearrange(
                        "p (kt c) -> p kt c", kt=8),
                        dstg["ws1_b"][it].rearrange("kt p c -> p kt c"), chW)
                    dma("sync", W2[:, 6144 + sl * 1024: 7168 + sl * 1024]
                        .rearrange("p (kt c) -> p kt c", kt=8),
                        dstg["ws3_b"][it].rearrange("kt p c -> p kt c"), chW)
                    em.wait_now("tensor", chW)
                    em.bank_acquire(4)
                    em.bank_acquire(5)
                    for kt in range(8):
                        apl = [em.mark_ready(4, chT)] if kt == 7 else ()
                        t_mm(PS[4][:, 0:512],
                             W2[:, base + kt * 128: base + kt * 128 + 128],
                             A3[:, kt * 512: kt * 512 + 512],
                             start=(kt == 0), stop=(kt == 7), apl=apl)
                    for kt in range(8):
                        apl = [em.mark_ready(5, chT)] if kt == 7 else ()
                        t_mm(PS[5][:, 0:512],
                             W2[:, 6144 + sl * 1024 + kt * 128:
                                6144 + sl * 1024 + kt * 128 + 128],
                             A3[:, kt * 512: kt * 512 + 512],
                             start=(kt == 0), stop=(kt == 7), apl=apl)
                    shguard[sl] = chT.n
                    em.wait_ready("scalar", 4)
                    sa = SCB[:, (it % 2) * 1024: (it % 2) * 1024 + 512]
                    if scb_ffn_guard[it % 2]:
                        em.wait_val("scalar", chV, scb_ffn_guard[it % 2])
                    if scb_guard[it % 2]:
                        em.wait_val("scalar", chT, scb_guard[it % 2])
                    s_act(sa, PS[4][:, 0:512], AF.Silu,
                          bias=CC[:, 56 + it:57 + it])
                    em.free_after(4, "scalar")
                    em.wait_ready("vector", 5)
                    em.wait_now("vector", chS)
                    v_op(lambda v, it=it, sa=sa: v.scalar_tensor_tensor(
                        W2[:, it * 512: it * 512 + 512], PS[5][:, 0:512],
                        CC[:, 72 + it:73 + it], sa,
                        op0=ALU.add, op1=ALU.mult))
                    em.free_after(5, "vector")
                    scb_ffn_guard[it % 2] = chV.n
                eslot_guard[2] = max(eslot_guard[2], chT.n)

                slots = [W1[:, 0:4096], W1[:, 4096:8192], W2[:, 4096:8192]]

                def slot_of(li):
                    return slots[li % 3]

                for e in range(NE):
                    for half, nm in ((0, "we1_b"), (1, "we3_b")):
                        li = 2 * e + half
                        if eslot_guard[li % 3]:
                            em.wait_val("sync", chT, eslot_guard[li % 3])
                        dma("sync", slot_of(li).rearrange(
                            "p (kt c) -> p kt c", kt=8),
                            dstg[nm][e].rearrange("kt p c -> p kt c"), chW)
                    em.wait_now("tensor", chW)
                    em.bank_acquire(3)
                    em.wait_now("tensor", chV)
                    t_mm(PS[3][:, 0:512],
                         EXHB[:, e * 128: e * 128 + 128],
                         CTB[:, 0:512], start=True, stop=True,
                         apl=[em.mark_ready(3, chT)])
                    em.wait_ready("scalar", 3)
                    bcb = A3[:, 4096:4608]
                    em.wait_now("scalar", chV)
                    s_act(bcb, PS[3][:, 0:512], AF.Copy)
                    em.free_after(3, "scalar")
                    s1 = slot_of(2 * e)
                    s3 = slot_of(2 * e + 1)
                    for it in range(4):
                        em.bank_acquire(4)
                        for kt in range(8):
                            apl = [em.mark_ready(4, chT)] if kt == 7 else ()
                            t_mm(PS[4][:, 0:512],
                                 s1[:, kt * 512 + it * 128:
                                    kt * 512 + it * 128 + 128],
                                 A3[:, kt * 512: kt * 512 + 512],
                                 start=(kt == 0), stop=(kt == 7), apl=apl)
                        em.bank_acquire(5)
                        for kt in range(8):
                            apl = [em.mark_ready(5, chT)] if kt == 7 else ()
                            t_mm(PS[5][:, 0:512],
                                 s3[:, kt * 512 + it * 128:
                                    kt * 512 + it * 128 + 128],
                                 A3[:, kt * 512: kt * 512 + 512],
                                 start=(kt == 0), stop=(kt == 7), apl=apl)
                        if it == 3:
                            eslot_guard[(2 * e) % 3] = chT.n
                            eslot_guard[(2 * e + 1) % 3] = chT.n
                        par2 = it % 2
                        sa = SCB[:, par2 * 1024: par2 * 1024 + 512]
                        sb = SCB[:, par2 * 1024 + 512: par2 * 1024 + 1024]
                        em.wait_ready("scalar", 4)
                        if scb_ffn_guard[par2]:
                            em.wait_val("scalar", chV, scb_ffn_guard[par2])
                        if scb_guard[par2]:
                            em.wait_val("scalar", chT, scb_guard[par2])
                        ei = e * 4 + it
                        s_act(sa, PS[4][:, 0:512], AF.Silu,
                              bias=CC[:, 80 + ei:81 + ei])
                        em.free_after(4, "scalar")
                        em.wait_ready("scalar", 5)
                        s_act(sb, PS[5][:, 0:512], AF.Identity,
                              bias=CC[:, 144 + ei:145 + ei])
                        em.free_after(5, "scalar")
                        em.wait_now("vector", chS)
                        n = e * 4 + it
                        if n < 32:
                            chdst = A1[:, n * 512: n * 512 + 512]
                        else:
                            chdst = A2[:, (n - 32) * 512: (n - 32) * 512 + 512]
                        v_op(lambda v, sa=sa, bcb=bcb: v.tensor_mul(
                            sa, sa, bcb))
                        v_op(lambda v, chdst=chdst, sa=sa, sb=sb:
                             v.tensor_mul(chdst, sa, sb))
                        scb_ffn_guard[par2] = chV.n

                em.wait_now("tensor", chV)
                yo_all_guard = [0]
                for dt in range(8):
                    if dt == 0:
                        for gsl in range(3):
                            if eslot_guard[gsl]:
                                em.wait_val("sync", chT, eslot_guard[gsl])
                    elif yo_all_guard[0]:
                        em.wait_val("sync", chT, yo_all_guard[0])
                    dma("sync",
                        W2[:, 4096:8192].rearrange(
                            "p (e it c) -> p e it c", e=8, it=4),
                        dstg["we2_b"][dt, 0:8].rearrange(
                            "e it p c -> p e it c"), chW)
                    dma("sync",
                        W1[:, 2048:6144].rearrange(
                            "p (e it c) -> p e it c", e=8, it=4),
                        dstg["we2_b"][dt, 8:16].rearrange(
                            "e it p c -> p e it c"), chW)
                    dma("sync", W1[:, 0:1024]
                        .rearrange("p (it c) -> p it c", it=8),
                        dstg["ws2_b"][dt].rearrange("it p c -> p it c"), chW)
                    em.wait_now("tensor", chW)
                    em.bank_acquire(2)
                    first = True
                    for n in range(64):
                        if n < 32:
                            chsrc = A1[:, n * 512: n * 512 + 512]
                            lh = W2[:, 4096 + n * 128: 4096 + n * 128 + 128]
                        else:
                            chsrc = A2[:, (n - 32) * 512:
                                       (n - 32) * 512 + 512]
                            lh = W1[:, 2048 + (n - 32) * 128:
                                    2048 + (n - 32) * 128 + 128]
                        t_mm(PS[2][:, 0:512], lh, chsrc,
                             start=first, stop=False)
                        first = False
                    t_mm(PS[2][:, 0:512], BE2[:, dt * 128: dt * 128 + 128],
                         CTB[:, 0:512], start=False, stop=False)
                    for it in range(8):
                        last = it == 7
                        apl = [em.mark_ready(2, chT)] if last else ()
                        t_mm(PS[2][:, 0:512],
                             W1[:, it * 128: it * 128 + 128],
                             W2[:, it * 512: it * 512 + 512],
                             start=False, stop=last, apl=apl)
                    yo_all_guard[0] = chT.n
                    em.wait_ready("vector", 2)
                    v_op(lambda v, dt=dt, c=c: v.scalar_tensor_tensor(
                        Rt(dt, c), PS[2][:, 0:512], CC[:, 64 + dt:65 + dt],
                        Rt(dt, c), op0=ALU.add, op1=ALU.add))
                    em.free_after(2, "vector")
                eslot_guard[0] = chT.n
                eslot_guard[1] = chT.n
                eslot_guard[2] = chT.n
                w2hi_guard[0] = chT.n
            em.barrier(("tensor", "vector", "scalar", "sync"))

        def phase_attn_b(b):
            TB = 256
            NCB = T // TB

            def vsync():
                ap = em.mk_inc(chV, 1)
                em.on("vector", lambda v, ap=ap: ap(v.engine_nop()))
                em.wait_now("vector", chV)

            sdummy = SCF[0:1, 2238:2239]

            def ssync():
                ap = em.mk_inc(chS, 1)
                em.on("scalar", lambda s, ap=ap: ap(
                    s.activation(out=sdummy, in_=sdummy, func=AF.Copy)))
                em.wait_now("scalar", chS)

            v_op(lambda v: v.memset(CAB[:], 0.0))
            v_op(lambda v: v.memset(SS[:], 0.0))
            v_op(lambda v: v.memset(SQ[:], 0.0))
            vsync()
            for c in range(NCH):
                ln_chunk(c, lambda dt, c=c: A1[:, dt * 2048 + c * TC:
                                               dt * 2048 + c * TC + TC])
            em.barrier(("tensor", "vector", "scalar", "sync", "gpsimd"))

            EK = SCF[:, 0:256]
            A_ = SCF[:, 256:512]
            SA = SCF[:, 512:800]
            SB = SCF[:, 800:1088]
            WREP = SCF[:, 1088:1344]
            NN = SCF[:, 1344:1600]
            DD = SCF[:, 1600:1856]
            RR = SCF[:, 1856:2112]
            ss_tmp = SCF[:, 2232:2233]
            sq_tmp = SCF[:, 2233:2234]

            wslot_guard = [0, 0]
            for cb in range(NCB):
                em.wait_now("vector", chT)
                for mi, moff in ((0, 0), (1, 2048), (2, 4096)):
                    for dt in range(NDT):
                        xt = A1[:, dt * 2048 + cb * TB:
                                dt * 2048 + cb * TB + TB]
                        dst = A3[:, moff + dt * TB: moff + dt * TB + TB]
                        v_op(lambda v, xt=xt, dst=dst, mi=mi, dt=dt:
                             v.tensor_scalar(dst, xt,
                                             CC[:, 8 * mi + dt:8 * mi + dt
                                                + 1], None, op0=ALU.mult))
                for mi, moff in ((0, 0), (1, 2048), (2, 4096)):
                    for dt in range(NDT):
                        dst = A3[:, moff + dt * TB: moff + dt * TB + TB]
                        if cb == 0:
                            xp = A1[:, dt * 2048: dt * 2048 + TB - 1]
                            v_op(lambda v, xp=xp, dst=dst, mi=mi, dt=dt:
                                 v.scalar_tensor_tensor(
                                     dst[:, 1:TB], xp,
                                     DC[:, 24 + 8 * mi + dt:
                                        25 + 8 * mi + dt],
                                     dst[:, 1:TB],
                                     op0=ALU.mult, op1=ALU.add))
                        else:
                            xp = A1[:, dt * 2048 + cb * TB - 1:
                                    dt * 2048 + cb * TB + TB - 1]
                            v_op(lambda v, xp=xp, dst=dst, mi=mi, dt=dt:
                                 v.scalar_tensor_tensor(
                                     dst, xp,
                                     DC[:, 24 + 8 * mi + dt:
                                        25 + 8 * mi + dt],
                                     dst, op0=ALU.mult, op1=ALU.add))
                vsync()
                for g in range(8):
                    sl = g % 2
                    base = sl * 3072
                    if wslot_guard[sl]:
                        em.wait_val("sync", chT, wslot_guard[sl])
                    for wi, nm in ((0, "wr_b"), (1, "wk_b"), (2, "wv_b")):
                        dma("sync", W1[:, base + wi * 1024:
                                       base + wi * 1024 + 1024].rearrange(
                                           "p (kt c) -> p kt c", kt=8),
                            dstg[nm][g].rearrange("kt p c -> p kt c"), chW)
                    em.wait_now("tensor", chW)
                    em.wait_now("tensor", chV)
                    for bank, wi, src_off in ((3, 0, 0), (4, 1, 2048),
                                              (5, 2, 4096)):
                        em.bank_acquire(bank)
                        for kt in range(8):
                            apl = ([em.mark_ready(bank, chT)]
                                   if kt == 7 else ())
                            t_mm(PS[bank][:, 0:TB],
                                 W1[:, base + wi * 1024 + kt * 128:
                                    base + wi * 1024 + kt * 128 + 128],
                                 A3[:, src_off + kt * TB:
                                    src_off + kt * TB + TB],
                                 start=(kt == 0), stop=(kt == 7), apl=apl)
                    wslot_guard[sl] = chT.n
                    em.wait_ready("scalar", 4)
                    em.wait_now("scalar", chV)
                    s_act(EK, PS[4][:, 0:TB], AF.Exp)
                    em.free_after(4, "scalar")
                    em.wait_ready("scalar", 3)
                    s_act(RR, PS[3][:, 0:TB], AF.Sigmoid)
                    em.free_after(3, "scalar")
                    em.wait_ready("vector", 5)
                    em.wait_now("vector", chS)
                    v_op(lambda v: v.memset(WREP, 1.0))
                    v_op(lambda v, g=g: v.tensor_mul(A_, EK, PS[5][:, 0:TB]))
                    em.free_after(5, "vector")
                    v_op(lambda v, g=g: v.tensor_scalar(
                        WREP, WREP, DC[:, g:g + 1], None, op0=ALU.mult))
                    vsync()
                    v_op(lambda v, g=g: v.tensor_tensor_scan(
                        SA[:, 1:257], WREP, A_, CAB[:, g:g + 1],
                        op0=ALU.mult, op1=ALU.add))
                    v_op(lambda v, g=g: v.tensor_tensor_scan(
                        SB[:, 1:257], WREP, EK, CAB[:, 8 + g:9 + g],
                        op0=ALU.mult, op1=ALU.add))
                    v_op(lambda v, g=g: v.tensor_copy(SA[:, 0:1],
                                                      CAB[:, g:g + 1]))
                    v_op(lambda v, g=g: v.tensor_copy(SB[:, 0:1],
                                                      CAB[:, 8 + g:9 + g]))
                    vsync()
                    v_op(lambda v, g=g: v.tensor_copy(CAB[:, g:g + 1],
                                                      SA[:, 256:257]))
                    v_op(lambda v, g=g: v.tensor_copy(CAB[:, 8 + g:9 + g],
                                                      SB[:, 256:257]))
                    v_op(lambda v, g=g: v.tensor_scalar(
                        EK, SA[:, 1:257], DC[:, 8 + g:9 + g], None,
                        op0=ALU.mult))
                    v_op(lambda v, g=g: v.tensor_scalar(
                        A_, SB[:, 1:257], DC[:, 8 + g:9 + g], 1e-9,
                        op0=ALU.mult, op1=ALU.add))
                    vsync()
                    v_op(lambda v, g=g: v.scalar_tensor_tensor(
                        NN, SA[:, 0:256], DC[:, 16 + g:17 + g], EK,
                        op0=ALU.mult, op1=ALU.add))
                    v_op(lambda v, g=g: v.scalar_tensor_tensor(
                        DD, SB[:, 0:256], DC[:, 16 + g:17 + g], A_,
                        op0=ALU.mult, op1=ALU.add))
                    vsync()
                    em.wait_now("scalar", chV)
                    s_act(DD, DD, AF.Ln)
                    ssync()
                    s_act(DD, DD, AF.Exp, scale=-1.0)
                    em.wait_now("vector", chS)
                    v_op(lambda v: v.tensor_mul(NN, NN, DD))
                    vsync()
                    v_op(lambda v: v.tensor_mul(A_, NN, RR))
                    vsync()
                    odst = A2[:, g * 2048 + cb * TB: g * 2048 + cb * TB + TB]
                    v_op(lambda v, odst=odst: v.tensor_scalar(
                        odst, A_, 1.0, None, op0=ALU.mult, op1=ALU.add,
                        accum_out=ss_tmp))
                    v_op(lambda v: v.scalar_tensor_tensor(
                        DD, A_, 1.0, A_, op0=ALU.mult, op1=ALU.mult,
                        accum_out=sq_tmp))
                    vsync()
                    v_op(lambda v, g=g: v.tensor_add(
                        SS[:, g:g + 1], SS[:, g:g + 1], ss_tmp))
                    v_op(lambda v, g=g: v.tensor_add(
                        SQ[:, g:g + 1], SQ[:, g:g + 1], sq_tmp))
                    vsync()

            em.barrier(("tensor", "vector", "scalar", "sync"))
            em.bank_acquire("ptf")
            t_mm(PTF[0:2, 0:8], HSEL[:], SS[:], start=True, stop=True)
            t_mm(PTF[0:2, 8:16], HSEL[:], SQ[:], start=True, stop=True,
                 apl=[em.mark_ready("ptf", chT)])
            em.wait_ready("scalar", "ptf")
            HSROW = SCF[0:2, 2240:2256]
            s_act(HSROW, PTF[0:2, 0:16], AF.Copy)
            em.free_after("ptf", "scalar")
            hm = SCF[0:2, 2256:2264]
            hv = SCF[0:2, 2264:2272]
            em.wait_now("vector", chS)
            NINV = 1.0 / (64.0 * 2048.0)
            v_op(lambda v: v.tensor_scalar(hm, HSROW[0:2, 0:8], NINV, None,
                                           op0=ALU.mult))
            v_op(lambda v: v.tensor_scalar(hv, HSROW[0:2, 8:16], NINV, None,
                                           op0=ALU.mult))
            vsync()
            hm2 = SCF[0:2, 2272:2280]
            v_op(lambda v: v.tensor_mul(hm2, hm, hm))
            vsync()
            v_op(lambda v: v.tensor_sub(hv, hv, hm2))
            vsync()
            em.wait_now("scalar", chV)
            s_act(hv, hv, AF.Ln, bias=CC[0:2, 208:209])
            ssync()
            s_act(hv, hv, AF.Exp, scale=-0.5)
            em.wait_now("tensor", chS)
            em.bank_acquire("ptf")
            t_mm(PTF[:, 0:8], HSELT[:], hm, start=True, stop=True)
            t_mm(PTF[:, 8:16], HSELT[:], hv, start=True, stop=True,
                 apl=[em.mark_ready("ptf", chT)])
            em.wait_ready("vector", "ptf")
            v_op(lambda v: v.tensor_mul(SCL[:], PTF[:, 8:16],
                                        CC[:, 40:48]))
            em.free_after("ptf", "vector")
            vsync()
            tb8 = SCF[:, 2280:2288]
            v_op(lambda v: v.tensor_mul(tb8, PTF[:, 0:8], SCL[:]))
            vsync()
            v_op(lambda v: v.tensor_sub(BIA[:], CC[:, 48:56], tb8))
            vsync()
            for g in range(8):
                for c in range(NCH):
                    src_t = A2[:, g * 2048 + c * TC: g * 2048 + c * TC + TC]
                    dst_t = A1[:, g * 2048 + c * TC: g * 2048 + c * TC + TC]
                    v_op(lambda v, src_t=src_t, dst_t=dst_t, g=g:
                         v.tensor_scalar(dst_t, src_t, SCL[:, g:g + 1],
                                         BIA[:, g:g + 1], op0=ALU.mult,
                                         op1=ALU.add))
            vsync()
            woguard = [0, 0]
            for et in range(8):
                sl = et % 2
                if woguard[sl]:
                    em.wait_val("sync", chT, woguard[sl])
                dma("sync", W2[:, sl * 1024: sl * 1024 + 1024].rearrange(
                    "p (kt c) -> p kt c", kt=8),
                    dstg["wo_b"][et].rearrange("kt p c -> p kt c"), chW)
                em.wait_now("tensor", chW)
                em.wait_now("tensor", chV)
                for c in range(NCH):
                    bank = 4 + c % 2
                    em.bank_acquire(bank)
                    for kt in range(8):
                        last = kt == 7
                        apl = [em.mark_ready(bank, chT)] if last else ()
                        t_mm(PS[bank][:, 0:512],
                             W2[:, sl * 1024 + kt * 128:
                                sl * 1024 + kt * 128 + 128],
                             A1[:, kt * 2048 + c * TC:
                                kt * 2048 + c * TC + TC],
                             start=(kt == 0), stop=last, apl=apl)
                    em.wait_ready("vector", bank)
                    v_op(lambda v, et=et, c=c, bank=bank: v.tensor_add(
                        Rt(et, c), Rt(et, c), PS[bank][:, 0:512]))
                    em.free_after(bank, "vector")
                woguard[sl] = chT.n
            em.barrier()

        for b in range(NB):
            dma("gpsimd", R[:].rearrange("p (dt t) -> p dt t", dt=8),
                x_in[b].rearrange("(dt p) t -> p dt t", p=128), chG)
            em.barrier()
            if "a" in phases:
                phase_attn_a(b)
            if "f1" in phases:
                phase_ffn(b)
            if "b" in phases:
                phase_attn_b(b)
            if "f2" in phases:
                phase_ffn(b)
            ap = em.mk_inc(chO, 16)
            dstp = y_out[b].rearrange("(dt p) t -> p dt t", p=128)
            em.on("sync", lambda e, d=dstp, ap=ap: ap(
                e.dma_start(out=d,
                            in_=R[:].rearrange("p (dt t) -> p dt t", dt=8))))
            em.barrier()

        for nm, out_ap in dump_outs.items():
            buf = {"R": R, "A1": A1, "A2": A2, "A3": A3, "SCF": SCF,
                   "CT": CT, "SS": SS, "SQ": SQ, "DC": DC}[nm]
            ap = em.mk_inc(chO, 16)
            em.on("sync", lambda e, bu=buf, o=out_ap, ap=ap: ap(
                e.dma_start(out=o, in_=bu[:])))
        em.wait_now("sync", chO)

        em.run(block)
    return nc



def _prep_shared_inputs(kw):
    f = np.float32
    out = {}
    wqkv = np.asarray(kw["L0_attn_a_w_qkv"], f)
    cols = []
    for s in range(8):
        for part in range(3):
            for hh in (2 * s, 2 * s + 1):
                base = part * 1024 + hh * 64
                cols.append(np.arange(base, base + 64))
    perm = np.concatenate(cols)
    wq = wqkv[:, perm].reshape(8, 128, 8, 384)
    out["wqkv"] = np.ascontiguousarray(wq.transpose(2, 0, 1, 3))

    def tile_oi(w):
        return np.ascontiguousarray(
            w.reshape(8, 128, 8, 128).transpose(2, 0, 1, 3))

    out["wout"] = tile_oi(np.asarray(kw["L0_attn_a_w_out"], f))
    out["wr"] = tile_oi(np.asarray(kw["L0_attn_b_W_r"], f).T)
    out["wk"] = tile_oi(np.asarray(kw["L0_attn_b_W_k"], f).T)
    out["wv"] = tile_oi(np.asarray(kw["L0_attn_b_W_v"], f).T)
    out["wo"] = tile_oi(np.asarray(kw["L0_attn_b_W_o"], f).T)
    out["we1"] = np.ascontiguousarray(
        np.asarray(kw["L0_ffn_We1"], f).reshape(16, 8, 128, 512))
    out["we3"] = np.ascontiguousarray(
        np.asarray(kw["L0_ffn_We3"], f).reshape(16, 8, 128, 512))
    out["we2"] = np.ascontiguousarray(
        np.asarray(kw["L0_ffn_We2"], f).reshape(16, 4, 128, 8, 128)
        .transpose(3, 0, 1, 2, 4))
    out["ws1"] = tile_oi(np.asarray(kw["L0_ffn_ws1"], f))
    out["ws3"] = tile_oi(np.asarray(kw["L0_ffn_ws3"], f))
    out["ws2"] = tile_oi(np.asarray(kw["L0_ffn_ws2"], f))
    gwt = np.asarray(kw["L0_ffn_gate_w"], f).T
    out["gwt"] = np.ascontiguousarray(
        gwt.reshape(8, 128, 16).transpose(1, 0, 2).reshape(128, 128))
    out["be2m"] = np.asarray(kw["L0_ffn_be2"], f)
    out["gateb"] = np.asarray(kw["L0_ffn_gate_b"], f).reshape(1, 16)

    cc = np.zeros((128, 224), f)

    def pack8(vec):
        return np.asarray(vec, f).reshape(8, 128).T

    cc[:, 0:8] = pack8(kw["L0_attn_b_time_mix_r"])
    cc[:, 8:16] = pack8(kw["L0_attn_b_time_mix_k"])
    cc[:, 16:24] = pack8(kw["L0_attn_b_time_mix_v"])
    cc[:, 24:32] = pack8(np.asarray(kw["L0_attn_b_time_decay"], f).reshape(-1))
    cc[:, 32:40] = pack8(np.asarray(kw["L0_attn_b_time_first"], f).reshape(-1))
    cc[:, 40:48] = pack8(kw["L0_attn_b_gn_weight"])
    cc[:, 48:56] = pack8(kw["L0_attn_b_gn_bias"])
    cc[:, 56:64] = pack8(kw["L0_ffn_bs1"])
    cc[:, 64:72] = pack8(kw["L0_ffn_bs2"])
    cc[:, 72:80] = pack8(kw["L0_ffn_bs3"])
    be1 = np.asarray(kw["L0_ffn_be1"], f)
    be3 = np.asarray(kw["L0_ffn_be3"], f)
    cc[:, 80:144] = be1.reshape(16, 4, 128).transpose(2, 0, 1).reshape(128, 64)
    cc[:, 144:208] = be3.reshape(16, 4, 128).transpose(2, 0, 1).reshape(128, 64)
    cc[:, 208] = 1e-5
    cc[:, 209] = 1e-9
    out["cc"] = cc

    out["idf"] = np.eye(128, dtype=f)
    out["ones128"] = np.ones((128, 128), f)
    out["onesrow"] = np.ones((1, 128), f)
    lo = np.zeros((1, 128), f)
    lo[0, 0:64] = 1.0
    out["oneslo"] = lo
    hi = np.zeros((1, 128), f)
    hi[0, 64:128] = 1.0
    out["oneshi"] = hi
    out["stair"] = (np.arange(896)[None, :] >=
                    (np.arange(128)[:, None] + 384)).astype(f)
    out["exh"] = np.kron(np.eye(16, dtype=f), np.ones((1, 128), f))
    hsel = np.zeros((128, 2), f)
    hsel[0:64, 0] = 1.0
    hsel[64:128, 1] = 1.0
    out["hsel"] = hsel
    out["hselt"] = np.ascontiguousarray(hsel.T)
    return out


def _run(inputs, phases=("a", "f1", "b", "f2"), dumps=(), cores=8):
    x = np.asarray(inputs["L0_attn_a_x"], np.float32)
    shared = _prep_shared_inputs(inputs)
    nc = build_program(phases=phases, dumps=dumps)
    in_maps = []
    for core in range(cores):
        m = dict(shared)
        m["x"] = np.ascontiguousarray(
            x[core * NB:(core + 1) * NB].transpose(0, 2, 1))
        in_maps.append(m)
    res = run_bass_kernel_spmd(nc, in_maps, core_ids=list(range(cores)))
    return res


def kernel(**inputs):
    res = _run(inputs)
    outs = []
    for core in range(8):
        y = np.asarray(res.results[core]["y"])
        outs.append(np.ascontiguousarray(y.transpose(0, 2, 1)))
    return np.concatenate(outs, axis=0)


# version 02
# speedup vs baseline: 1.2234x; 1.2234x over previous
import contextlib
import numpy as np

import concourse.bass as bass
import concourse.mybir as mybir
from concourse.bass_utils import run_bass_kernel_spmd

F32 = mybir.dt.float32
BF16 = mybir.dt.bfloat16
AF = mybir.ActivationFunctionType
ALU = mybir.AluOpType
AX = mybir.AxisListType

D = 1024
T = 2048
H = 16
HD = 64
NE = 16
INTER = 512
NB = 2
NDT = 8
TC = 512
NCH = T // TC
EPS = 1e-5

ENGINES = ("sync", "gpsimd", "vector", "scalar", "tensor")


class Ch:
    def __init__(self, handle):
        self.h = handle
        self.n = 0


class Emitter:

    def __init__(self, sems):
        self.q = {e: [] for e in ENGINES}
        self.ch = {name: Ch(h) for name, h in sems.items()}
        self.bar = {e: self.ch["bar_" + e] for e in ENGINES}
        self.bank_free = {}
        self.bank_ready = {}

    def on(self, eng, fn):
        self.q[eng].append(fn)

    def mk_inc(self, ch, k=1):
        ch.n += k
        h = ch.h

        def apply(instr):
            instr.then_inc(h, k)
            return instr
        return apply

    def tok(self, ch):
        ap = self.mk_inc(ch, 1)
        return ap, (ch, ch.n)

    def wait_tok(self, eng, tok):
        ch, v = tok
        self.wait_val(eng, ch, v)

    def wait_now(self, eng, ch):
        self.wait_val(eng, ch, ch.n)

    def wait_val(self, eng, ch, v):
        h = ch.h
        if v > 0:
            self.on(eng, lambda e: e.wait_ge(h, v))

    def barrier(self, engines=ENGINES):
        dma_chs = {"sync": ("w", "o"), "gpsimd": ("g",)}
        for e in engines:
            for cn in dma_chs.get(e, ()):
                self.wait_now(e, self.ch[cn])
        toks = {}
        for e in engines:
            ap = self.mk_inc(self.bar[e], 1)
            self.on(e, lambda eng, ap=ap: ap(eng.nop()))
            toks[e] = self.bar[e].n
        for e in engines:
            for o in engines:
                if o != e:
                    self.wait_val(e, self.bar[o], toks[o])

    def bank_acquire(self, bank):
        if bank in self.bank_free:
            self.wait_tok("tensor", self.bank_free[bank])

    def mark_ready(self, bank, ch):
        ap, tk = self.tok(ch)
        self.bank_ready[bank] = tk
        return ap

    def wait_ready(self, eng, bank):
        self.wait_tok(eng, self.bank_ready[bank])

    def mark_free(self, bank, eng):
        chmap = {"vector": "v", "scalar": "s", "gpsimd": "g", "sync": "w"}
        ap, tk = self.tok(self.ch[chmap[eng]])
        self.bank_free[bank] = tk
        return ap

    def free_after(self, bank, eng):
        chmap = {"vector": "v", "scalar": "s", "gpsimd": "g", "sync": "w"}
        ch = self.ch[chmap[eng]]
        self.bank_free[bank] = (ch, ch.n)

    def run(self, block):
        for e in ENGINES:
            fns = self.q[e]

            def body(eng, fns=fns):
                for fn in fns:
                    fn(eng)
            getattr(block, e)(body)


def build_program(phases=("a", "f1", "b", "f2"), dumps=()):
    nc = bass.Bass("TRN2", target_bir_lowering=False)

    din = {}

    def I(name, shape, dt=F32):
        din[name] = nc.dram_tensor(name, list(shape), dt,
                                   kind="ExternalInput").ap()
        return din[name]

    x_in = I("x", (NB, D, T))
    I("wqkv", (8, 8, 128, 384))
    I("wout", (8, 8, 128, 128))
    for nm in ("wr", "wk", "wv", "wo"):
        I(nm, (8, 8, 128, 128))
    I("we1", (NE, 8, 128, 512))
    I("we3", (NE, 8, 128, 512))
    I("we2", (8, NE, 4, 128, 128))
    I("ws1", (8, 8, 128, 128))
    I("ws3", (8, 8, 128, 128))
    I("ws2", (8, 8, 128, 128))
    I("gwt", (128, 128))
    I("be2m", (16, 1024))
    I("gateb", (1, 16))
    I("cc", (128, 224))
    I("idf", (128, 128))
    I("ones128", (128, 128))
    I("onesrow", (1, 128))
    I("oneslo", (1, 128))
    I("oneshi", (1, 128))
    I("stair", (128, 896))
    I("hsel", (128, 2))
    I("exh", (16, 2048))
    I("hselt", (2, 128))

    y_out = nc.dram_tensor("y", [NB, D, T], F32, kind="ExternalOutput").ap()

    dstg = {}
    for nm, shp in (
        ("wqkv_b", (8, 8, 128, 384)), ("wout_b", (8, 8, 128, 128)),
        ("wr_b", (8, 8, 128, 128)), ("wk_b", (8, 8, 128, 128)),
        ("wv_b", (8, 8, 128, 128)), ("wo_b", (8, 8, 128, 128)),
        ("we1_b", (NE, 8, 128, 512)), ("we3_b", (NE, 8, 128, 512)),
        ("we2_b", (8, NE, 4, 128, 128)),
        ("ws1_b", (8, 8, 128, 128)), ("ws3_b", (8, 8, 128, 128)),
        ("ws2_b", (8, 8, 128, 128)),
    ):
        dstg[nm] = nc.dram_tensor(nm, list(shp), BF16, kind="Internal").ap()

    dump_outs = {}
    for nm in dumps:
        shp, dt = {"R": ([128, 16384], F32), "A1": ([128, 16384], BF16),
                   "A2": ([128, 16384], BF16), "A3": ([128, 6144], BF16),
                   "SCF": ([128, 4096], F32), "CT": ([128, 64], F32),
                   "SS": ([128, 8], F32), "SQ": ([128, 8], F32),
                   "DC": ([128, 64], F32)}[nm]
        dump_outs[nm] = nc.dram_tensor("dump_" + nm, shp, dt,
                                       kind="ExternalOutput").ap()

    es = contextlib.ExitStack()
    with es:
        sbt = lambda n, p, f, dt: es.enter_context(
            nc.sbuf_tensor(n, [p, f], dt))
        R = sbt("R", 128, 16384, F32)
        A1 = sbt("A1", 128, 16384, BF16)
        A2 = sbt("A2", 128, 16384, BF16)
        A3 = sbt("A3", 128, 6144, BF16)
        W1 = sbt("W1", 128, 8192, BF16)
        W2 = sbt("W2", 128, 8192, BF16)
        SCF = sbt("SCF", 128, 4096, F32)
        SC2 = sbt("SC2", 128, 512, F32)
        SCB = sbt("SCB", 128, 2048, BF16)
        GP = SCF[0:18, 3072:3584]
        GT = SCF[:, 2112:2184]
        CT = SCF[:, 2048:2112]
        CTROW = SCF[0:16, 3584:4096]
        CTB = sbt("CTB", 16, 512, BF16)
        CC = sbt("CC", 128, 224, F32)
        DC = sbt("DC", 128, 64, F32)
        GWT = sbt("GWT", 128, 128, F32)
        CS = sbt("CS", 128, 16, F32)
        BE2 = sbt("BE2", 16, 1024, BF16)
        GB = sbt("GB", 128, 16, F32)
        IDF = sbt("IDF", 128, 128, F32)
        IDB = sbt("IDB", 128, 128, BF16)
        ONES128B = sbt("ONES128B", 128, 128, BF16)
        ONESROWF = sbt("ONESROWF", 1, 128, F32)
        ONESROWB = sbt("ONESROWB", 1, 128, BF16)
        ONESLO = sbt("ONESLO", 1, 128, F32)
        ONESHI = sbt("ONESHI", 1, 128, F32)
        STAIRB = sbt("STAIRB", 128, 896, BF16)
        HSEL = sbt("HSEL", 128, 2, F32)
        EXHB = sbt("EXHB", 16, 2048, BF16)
        HSELT = sbt("HSELT", 2, 128, F32)
        SS = SCF[:, 2184:2192]
        SQ = SCF[:, 2192:2200]
        CAB = SCF[:, 2200:2216]
        SCL = SCF[:, 2216:2224]
        BIA = SCF[:, 2224:2232]

        PS = [es.enter_context(nc.psum_tensor("PB%d" % i, [128, 512], F32))
              for i in range(6)]
        PTB = es.enter_context(nc.psum_tensor("PTB", [128, 128], BF16))
        PTF = es.enter_context(nc.psum_tensor("PTF", [128, 128], F32))

        sems = {}
        for nm in ["w", "g", "t", "s", "v", "o"] + ["bar_" + e
                                                    for e in ENGINES]:
            sems[nm] = es.enter_context(nc.semaphore("sem_" + nm))

        block = es.enter_context(nc.Block())
        em = Emitter(sems)
        chW, chG, chT, chS, chV, chO = (em.ch[k] for k in
                                        ("w", "g", "t", "s", "v", "o"))

        def dma(eng, dst, src, ch):
            ap = em.mk_inc(ch, 16)
            em.on(eng, lambda e: ap(e.dma_start(out=dst, in_=src)))

        def s_act(out, in_, func, bias=0.0, scale=1.0, accum=None, apl=()):
            ap = em.mk_inc(chS, 1)

            def f(s):
                if accum is not None:
                    ins = s.activation(out=out, in_=in_, func=func, bias=bias,
                                       scale=scale, accum_out=accum)
                else:
                    ins = s.activation(out=out, in_=in_, func=func, bias=bias,
                                       scale=scale)
                ap(ins)
                for a in apl:
                    a(ins)
                return ins
            em.on("scalar", f)

        def v_op(fn, apl=()):
            ap = em.mk_inc(chV, 1)

            def f(v):
                ins = fn(v)
                ap(ins)
                for a in apl:
                    a(ins)
                return ins
            em.on("vector", f)

        import traceback

        def t_mm(out, lhsT, rhs, start, stop, apl=()):
            site = traceback.extract_stack(limit=3)[0]

            def f(t):
                try:
                    ins = t.matmul(out, lhsT, rhs, start=start, stop=stop)
                except Exception:
                    print("t_mm failed from %s:%d" % (site.filename,
                                                      site.lineno))
                    raise
                for a in apl:
                    a(ins)
                return ins
            em.on("tensor", f)

        def t_tr(out, in_, ident, apl=()):
            def f(t):
                ins = t.transpose(out, in_, ident)
                for a in apl:
                    a(ins)
                return ins
            em.on("tensor", f)

        def Rt(dt, c):
            return R[:, dt * 2048 + c * TC: dt * 2048 + c * TC + TC]

        dma("sync", CC[:], din["cc"][:], chW)
        dma("sync", GWT[:], din["gwt"][:], chW)
        dma("sync", IDF[:], din["idf"][:], chW)
        dma("sync", ONESROWF[:], din["onesrow"][:], chW)
        dma("sync", ONESLO[:], din["oneslo"][:], chW)
        dma("sync", ONESHI[:], din["oneshi"][:], chW)
        dma("sync", HSEL[:], din["hsel"][:], chW)
        dma("sync", HSELT[:], din["hselt"][:], chW)
        dma("sync", GB[:], din["gateb"].broadcast_to([128, 16]), chW)
        dma("gpsimd", STAIRB[:], din["stair"][:], chG)
        dma("gpsimd", ONES128B[:], din["ones128"][:], chG)
        dma("gpsimd", IDB[:], din["idf"][:], chG)
        dma("gpsimd", ONESROWB[:], din["onesrow"][:], chG)
        dma("gpsimd", BE2[:], din["be2m"][:], chG)
        dma("gpsimd", EXHB[:], din["exh"][:], chG)

        for e in ("scalar", "vector", "tensor"):
            em.wait_now(e, chW)
            em.wait_now(e, chG)

        s_act(DC[:, 0:8], CC[:, 24:32], AF.Exp)
        s_act(DC[:, 0:8], DC[:, 0:8], AF.Exp, scale=-1.0)
        s_act(DC[:, 8:16], CC[:, 32:40], AF.Exp)
        em.wait_now("vector", chS)

        def vsync0():
            ap = em.mk_inc(chV, 1)
            em.on("vector", lambda v, ap=ap: ap(v.engine_nop()))
            em.wait_now("vector", chV)

        v_op(lambda v: v.tensor_mul(DC[:, 16:24], DC[:, 0:8], DC[:, 8:16]))
        vsync0()
        v_op(lambda v: v.tensor_scalar(DC[:, 16:24], DC[:, 16:24], -1.0, 1.0,
                                       op0=ALU.mult, op1=ALU.add))
        vsync0()
        for j in range(3):
            v_op(lambda v, j=j: v.tensor_scalar(
                DC[:, 24 + 8 * j:32 + 8 * j], CC[:, 8 * j:8 * j + 8],
                -1.0, 1.0, op0=ALU.mult, op1=ALU.add))
        v_op(lambda v: v.memset(SCF[:, 0:1], 1.0))
        em.wait_now("tensor", chV)
        for kt in range(8):
            apl = [em.mark_ready(0, chT)] if kt == 7 else ()
            t_mm(PS[0][0:1, 0:16], SCF[:, 0:1], GWT[:, kt * 16:kt * 16 + 16],
                 start=(kt == 0), stop=(kt == 7), apl=apl)
        em.wait_ready("vector", 0)
        v_op(lambda v: v.tensor_copy(SCF[0:1, 4:20], PS[0][0:1, 0:16]))
        em.free_after(0, "vector")
        em.wait_now("tensor", chV)
        t_mm(PTF[:, 0:16], ONESROWF[:], SCF[0:1, 4:20], True, True,
             apl=[em.mark_ready(6, chT)])
        em.wait_ready("vector", 6)
        v_op(lambda v: v.tensor_copy(CS[:], PTF[:, 0:16]))
        em.free_after(6, "vector")

        state = {"flip": 0, "pend": [0, 0]}

        def stage(dst_ap, src_ap, cols):
            slot = state["flip"]
            state["flip"] = 1 - slot
            buf = W1[:, slot * 4096: slot * 4096 + cols]
            shp = list(src_ap.shape)
            if len(shp) == 3:
                buf = buf.rearrange("p (a c) -> p a c", a=shp[1])
            elif len(shp) == 4:
                buf = buf.rearrange("p (a b c) -> p a b c", a=shp[1],
                                    b=shp[2])
            if state["pend"][slot]:
                em.wait_val("gpsimd", chG, state["pend"][slot])
            ap1 = em.mk_inc(chG, 16)
            em.on("gpsimd", lambda g, b=buf, s=src_ap: ap1(
                g.dma_start(out=b, in_=s)))
            v1 = chG.n
            em.wait_val("gpsimd", chG, v1)
            ap2 = em.mk_inc(chG, 16)
            em.on("gpsimd", lambda g, b=buf, d=dst_ap: ap2(
                g.dma_start(out=d, in_=b)))
            state["pend"][slot] = chG.n

        for s in range(8):
            stage(dstg["wqkv_b"][s].rearrange("kt p c -> p kt c"),
                  din["wqkv"][s].rearrange("kt p c -> p kt c"), 3072)
        for nm in ("wout", "wr", "wk", "wv", "wo"):
            for eh in range(4):
                stage(dstg[nm + "_b"][2 * eh:2 * eh + 2].rearrange(
                          "e kt p c -> p e kt c"),
                      din[nm][2 * eh:2 * eh + 2].rearrange(
                          "e kt p c -> p e kt c"), 2048)
        for nm in ("we1", "we3"):
            for e in range(NE):
                stage(dstg[nm + "_b"][e].rearrange("kt p c -> p kt c"),
                      din[nm][e].rearrange("kt p c -> p kt c"), 4096)
        for dt in range(8):
            for half in range(2):
                stage(dstg["we2_b"][dt, 8 * half:8 * half + 8].rearrange(
                          "e it p c -> p e it c"),
                      din["we2"][dt, 8 * half:8 * half + 8].rearrange(
                          "e it p c -> p e it c"), 4096)
        for nm in ("ws1", "ws3", "ws2"):
            for ih in range(4):
                stage(dstg[nm + "_b"][2 * ih:2 * ih + 2].rearrange(
                          "i kt p c -> p i kt c"),
                      din[nm][2 * ih:2 * ih + 2].rearrange(
                          "i kt p c -> p i kt c"), 2048)
        em.barrier()

        scb_guard = [0, 0]
        scb_ffn_guard = [0, 0]

        def ln_chunk(c, dest_tile_fn):
            for bank in (0, 1):
                em.bank_acquire(bank)
            for dt in range(NDT):
                par = dt % 2
                xb = SCB[:, par * 1024: par * 1024 + 512]
                x2b = SCB[:, par * 1024 + 512: par * 1024 + 1024]
                if scb_guard[par]:
                    em.wait_val("scalar", chT, scb_guard[par])
                if scb_ffn_guard[par]:
                    em.wait_val("scalar", chV, scb_ffn_guard[par])
                s_act(xb, Rt(dt, c), AF.Copy)
                s_act(x2b, Rt(dt, c), AF.Square)
                em.wait_now("tensor", chS)
                apl0 = [em.mark_ready(0, chT)] if dt == NDT - 1 else ()
                t_mm(PS[0][0:1, 0:512], ONES128B[:, 0:1], xb,
                     start=(dt == 0), stop=(dt == NDT - 1), apl=apl0)
                if dt == NDT - 1:
                    apl1 = [em.mark_ready(1, chT)]
                else:
                    apl1 = [em.mk_inc(chT, 1)]
                t_mm(PS[1][0:1, 0:512], ONES128B[:, 0:1], x2b,
                     start=(dt == 0), stop=(dt == NDT - 1), apl=apl1)
                scb_guard[par] = chT.n
            em.wait_ready("vector", 0)
            em.wait_ready("vector", 1)
            arow = SCF[0:1, 0:512]
            brow = SCF[0:1, 512:1024]
            vrow = SCF[0:1, 1024:1536]
            mrow = SCF[0:1, 1536:2048]
            v_op(lambda v: v.tensor_scalar_mul(mrow, PS[0][0:1, 0:512],
                                               1.0 / D))
            em.free_after(0, "vector")
            v_op(lambda v: v.tensor_scalar_mul(vrow, PS[1][0:1, 0:512],
                                               1.0 / D))
            em.free_after(1, "vector")
            v_op(lambda v: v.tensor_mul(arow, mrow, mrow))
            v_op(lambda v: v.tensor_sub(vrow, vrow, arow))
            em.wait_now("scalar", chV)
            s_act(arow, vrow, AF.Ln, bias=CC[0:1, 208:209])
            s_act(arow, arow, AF.Exp, scale=-0.5)
            em.wait_now("vector", chS)
            v_op(lambda v: v.tensor_mul(brow, mrow, arow))
            v_op(lambda v: v.tensor_scalar_mul(brow, brow, -1.0))
            em.bank_acquire(2)
            em.bank_acquire(3)
            em.wait_now("tensor", chV)
            t_mm(PS[2][:, 0:512], ONESROWF[:], arow, True, True,
                 apl=[em.mark_ready(2, chT)])
            t_mm(PS[3][:, 0:512], ONESROWF[:], brow, True, True,
                 apl=[em.mark_ready(3, chT)])
            em.wait_ready("vector", 2)
            em.wait_ready("vector", 3)
            t1 = SC2[:, 0:512]
            for dt in range(NDT):
                v_op(lambda v, dt=dt: v.tensor_mul(t1, Rt(dt, c),
                                                   PS[2][:, 0:512]))
                v_op(lambda v, dt=dt: v.tensor_add(dest_tile_fn(dt), t1,
                                                   PS[3][:, 0:512]))
                if dt == NDT - 1:
                    em.free_after(2, "vector")
                    em.free_after(3, "vector")

        def phase_attn_a(b):
            for c in range(NCH):
                ln_chunk(c, lambda dt, c=c: A1[:, dt * 2048 + c * TC:
                                               dt * 2048 + c * TC + TC])
            em.barrier(("tensor", "vector", "scalar", "sync"))

            VT = W2[:, 0:1040]
            PT = [W2[:, 1040 + i * 512: 1040 + (i + 1) * 512]
                  for i in range(4)]
            wslab = [W1[:, 0:3072], W1[:, 4096:4096 + 3072]]
            slab_owner = [0, 0]
            for s in range(8):
                slot = s % 2
                if slab_owner[slot]:
                    em.wait_val("sync", chT, slab_owner[slot])
                dma("sync",
                    wslab[slot].rearrange("p (kt c) -> p kt c", kt=8),
                    dstg["wqkv_b"][s].rearrange("kt p c -> p kt c"), chW)
                em.wait_now("tensor", chW)
                for mt in range(3):
                    for c in range(NCH):
                        bank = 4 + (mt * NCH + c) % 2
                        em.bank_acquire(bank)
                        for kt in range(8):
                            last = kt == 7
                            apl = [em.mark_ready(bank, chT)] if last else ()
                            t_mm(PS[bank][:, 0:512],
                                 wslab[slot][:, kt * 384 + mt * 128:
                                             kt * 384 + mt * 128 + 128],
                                 A1[:, kt * 2048 + c * TC:
                                    kt * 2048 + c * TC + TC],
                                 start=(kt == 0), stop=last, apl=apl)
                        em.wait_ready("scalar", bank)
                        dst = A3[:, mt * 2048 + c * TC:
                                 mt * 2048 + c * TC + TC]
                        s_act(dst, PS[bank][:, 0:512], AF.Copy)
                        em.free_after(bank, "scalar")
                slab_owner[slot] = chT.n

                pt_guard = [0, 0, 0, 0]
                for par in range(2):
                    h = 2 * s + par
                    rows = slice(par * 64, par * 64 + 64)
                    em.wait_now("tensor", chS)
                    em.wait_now("vector", chT)
                    for kt in range(16):
                        em.bank_acquire("ptb")
                        t_tr(PTB[:, 0:64],
                             A3[rows, 2 * 2048 + kt * 128:
                                2 * 2048 + kt * 128 + 128],
                             IDB[rows, par * 64: par * 64 + 64],
                             apl=[em.mark_ready("ptb", chT)])
                        em.wait_ready("vector", "ptb")
                        v_op(lambda v, kt=kt: v.tensor_copy(
                            VT[:, kt * 65: kt * 65 + 64], PTB[:, 0:64]))
                        em.free_after("ptb", "vector")
                        v_op(lambda v, kt=kt: v.memset(
                            VT[:, kt * 65 + 64: kt * 65 + 65], 1.0))
                    em.wait_now("tensor", chV)
                    for qc in range(NCH):
                        nkt = 4 * (qc + 1)
                        em.bank_acquire(4)
                        for kt in range(nkt):
                            sbank = 2 + kt % 2
                            em.bank_acquire(sbank)
                            t_mm(PS[sbank][:, 0:512],
                                 A3[rows, 1 * 2048 + kt * 128:
                                    1 * 2048 + kt * 128 + 128],
                                 A3[rows, 0 * 2048 + qc * TC:
                                    0 * 2048 + qc * TC + TC],
                                 start=True, stop=True,
                                 apl=[em.mark_ready(sbank, chT)])
                            em.wait_ready("scalar", sbank)
                            pt = PT[kt % 4]
                            if pt_guard[kt % 4]:
                                em.wait_val("scalar", chT, pt_guard[kt % 4])
                            s_act(pt, PS[sbank][:, 0:512], AF.Exp,
                                  scale=0.125)
                            em.free_after(sbank, "scalar")
                            dd = kt - 4 * qc
                            if dd >= 0:
                                off = 384 - 128 * dd
                                em.wait_now("vector", chS)
                                v_op(lambda v, pt=pt, off=off: v.tensor_mul(
                                    pt, pt, STAIRB[:, off:off + 512]))
                                em.wait_now("tensor", chV)
                            else:
                                em.wait_now("tensor", chS)
                            if kt == nkt - 1:
                                apl2 = [em.mark_ready(4, chT)]
                            else:
                                apl2 = [em.mk_inc(chT, 1)]
                            t_mm(PS[4][0:65, 0:512],
                                 VT[:, kt * 65: kt * 65 + 65], pt,
                                 start=(kt == 0), stop=(kt == nkt - 1),
                                 apl=apl2)
                            pt_guard[kt % 4] = chT.n
                        em.wait_ready("scalar", 4)
                        odst = A2[rows, s * 2048 + qc * TC:
                                  s * 2048 + qc * TC + TC]
                        s_act(odst, PS[4][0:64, 0:512], AF.Copy)
                        drow = SCF[0:1, par * 2048 + qc * TC:
                                   par * 2048 + qc * TC + TC]
                        s_act(drow, PS[4][64:65, 0:512], AF.Copy)
                        em.free_after(4, "scalar")
                em.wait_now("tensor", chS)
                for qc in range(NCH):
                    em.bank_acquire(5)
                    t_mm(PS[5][:, 0:512], ONESLO[:],
                         SCF[0:1, qc * TC: qc * TC + TC],
                         start=True, stop=False)
                    t_mm(PS[5][:, 0:512], ONESHI[:],
                         SCF[0:1, 2048 + qc * TC: 2048 + qc * TC + TC],
                         start=False, stop=True,
                         apl=[em.mark_ready(5, chT)])
                    em.wait_ready("scalar", 5)
                    rec = SC2[:, 0:512]
                    s_act(rec, PS[5][:, 0:512], AF.Ln)
                    em.free_after(5, "scalar")
                    s_act(rec, rec, AF.Exp, scale=-1.0)
                    em.wait_now("vector", chS)
                    ot = A2[:, s * 2048 + qc * TC: s * 2048 + qc * TC + TC]
                    v_op(lambda v, ot=ot: v.tensor_mul(ot, ot, rec))

            em.barrier(("tensor", "vector", "scalar", "sync"))
            wslot = [W1[:, 0:1024], W1[:, 4096:4096 + 1024]]
            owner = [0, 0]
            for et in range(8):
                sl = et % 2
                if owner[sl]:
                    em.wait_val("sync", chT, owner[sl])
                dma("sync",
                    wslot[sl].rearrange("p (kt c) -> p kt c", kt=8),
                    dstg["wout_b"][et].rearrange("kt p c -> p kt c"), chW)
                em.wait_now("tensor", chW)
                for c in range(NCH):
                    bank = 4 + c % 2
                    em.bank_acquire(bank)
                    for kt in range(8):
                        last = kt == 7
                        apl = [em.mark_ready(bank, chT)] if last else ()
                        t_mm(PS[bank][:, 0:512],
                             wslot[sl][:, kt * 128: kt * 128 + 128],
                             A2[:, kt * 2048 + c * TC:
                                kt * 2048 + c * TC + TC],
                             start=(kt == 0), stop=last, apl=apl)
                    em.wait_ready("vector", bank)
                    v_op(lambda v, et=et, c=c, bank=bank: v.tensor_add(
                        Rt(et, c), Rt(et, c), PS[bank][:, 0:512]))
                    em.free_after(bank, "vector")
                owner[sl] = chT.n
            em.barrier()


        def phase_ffn(b):
            eslot_guard = [0, 0, 0]
            w2hi_guard = [0]
            for c in range(NCH):
                ln_chunk(c, lambda dt: A3[:, dt * 512: dt * 512 + 512])
                em.bank_acquire(0)
                for kt in range(8):
                    apl = [em.mark_ready(0, chT)] if kt == 7 else ()
                    t_mm(PS[0][0:16, 0:512], GWT[:, kt * 16: kt * 16 + 16],
                         Rt(kt, c), start=(kt == 0), stop=(kt == 7), apl=apl)
                em.wait_ready("scalar", 0)
                GPS = SCF[0:18, 3072:3584]
                s_act(GPS[0:16, :], PS[0][0:16, 0:512], AF.Copy)
                em.free_after(0, "scalar")
                arow = SCF[0:1, 0:512]
                brow = SCF[0:1, 512:1024]
                em.wait_now("gpsimd", chV)
                em.wait_now("gpsimd", chS)
                dma("gpsimd", SCF[16:17, 3072:3584], arow, chG)
                dma("gpsimd", SCF[17:18, 3072:3584], brow, chG)
                em.wait_now("tensor", chG)

                def vsync():
                    ap = em.mk_inc(chV, 1)
                    em.on("vector", lambda v, ap=ap: ap(v.engine_nop()))
                    em.wait_now("vector", chV)

                WGS = []
                for j in range(4):
                    base = 2240 + j * 192
                    W = {}
                    o = SCF[:, base:base + 192]
                    W["gt"] = o[:, 0:18]
                    W["t0"] = o[:, 18:34]
                    W["sc"] = o[:, 34:50]
                    W["p0"] = o[:, 50:66]
                    W["s2"] = o[:, 66:82]
                    W["gm"] = o[:, 82:90]
                    W["gs"] = o[:, 90:94]
                    W["keep"] = o[:, 94:98]
                    W["ke"] = o[:, 98:114]
                    W["s2k"] = o[:, 114:130]
                    W["e1"] = o[:, 130:146]
                    W["e2"] = o[:, 146:162]
                    W["eqg"] = o[:, 162:166]
                    W["tmp"] = o[:, 166:182]
                    W["nmx"] = o[:, 182:183]
                    W["sume"] = o[:, 183:184]
                    W["rcp"] = o[:, 184:185]
                    W["r1"] = o[:, 185:186]
                    W["r2"] = o[:, 186:187]
                    W["m1"] = o[:, 187:188]
                    W["m2"] = o[:, 188:189]
                    WGS.append(W)
                for j in range(4):
                    em.bank_acquire("ptf")
                    em.wait_now("tensor", chS)
                    t_tr(PTF[:, 0:18], GPS[0:18, j * 128: j * 128 + 128],
                         IDF[0:18, 0:18], apl=[em.mark_ready("ptf", chT)])
                    em.wait_ready("vector", "ptf")
                    v_op(lambda v, j=j: v.tensor_copy(WGS[j]["gt"],
                                                      PTF[:, 0:18]))
                    em.free_after("ptf", "vector")
                vsync()
                for j in range(4):
                    W = WGS[j]
                    v_op(lambda v, W=W: v.tensor_scalar(
                        W["t0"], W["gt"][:, 0:16], W["gt"][:, 16:17], None,
                        op0=ALU.mult))
                vsync()
                for j in range(4):
                    W = WGS[j]
                    v_op(lambda v, W=W: v.scalar_tensor_tensor(
                        W["sc"], CS[:], W["gt"][:, 17:18], W["t0"],
                        op0=ALU.mult, op1=ALU.add))
                vsync()
                for j in range(4):
                    W = WGS[j]
                    v_op(lambda v, W=W: v.reduce_max(W["nmx"], W["sc"],
                                                     axis=AX.X, negate=True))
                vsync()
                em.wait_now("scalar", chV)
                for j in range(4):
                    W = WGS[j]
                    s_act(W["p0"], W["sc"], AF.Exp, bias=W["nmx"],
                          accum=W["sume"])
                for j in range(4):
                    W = WGS[j]
                    s_act(W["rcp"], W["sume"], AF.Ln)
                for j in range(4):
                    W = WGS[j]
                    s_act(W["rcp"], W["rcp"], AF.Exp, scale=-1.0)
                em.wait_now("vector", chS)
                for j in range(4):
                    W = WGS[j]
                    v_op(lambda v, W=W: v.tensor_scalar(
                        W["p0"], W["p0"], W["rcp"], None, op0=ALU.mult))
                vsync()
                for j in range(4):
                    W = WGS[j]
                    v_op(lambda v, W=W: v.tensor_add(W["s2"], W["p0"], GB[:]))
                vsync()
                for gi in range(4):
                    for j in range(4):
                        W = WGS[j]
                        v_op(lambda v, W=W, gi=gi: v.reduce_max(
                            W["gm"][:, gi:gi + 1],
                            W["s2"][:, 4 * gi:4 * gi + 4], axis=AX.X))
                vsync()
                for gi in range(4):
                    for j in range(4):
                        W = WGS[j]
                        v_op(lambda v, W=W, gi=gi: v.tensor_scalar(
                            W["eqg"], W["s2"][:, 4 * gi:4 * gi + 4],
                            W["gm"][:, gi:gi + 1], None, op0=ALU.is_equal))
                    vsync()
                    for j in range(4):
                        W = WGS[j]
                        v_op(lambda v, W=W, gi=gi: v.scalar_tensor_tensor(
                            W["tmp"][:, 0:4], W["eqg"], -1e30,
                            W["s2"][:, 4 * gi:4 * gi + 4],
                            op0=ALU.mult, op1=ALU.add))
                    vsync()
                    for j in range(4):
                        W = WGS[j]
                        v_op(lambda v, W=W, gi=gi: v.reduce_max(
                            W["gm"][:, 4 + gi:5 + gi], W["tmp"][:, 0:4],
                            axis=AX.X))
                    vsync()
                for j in range(4):
                    W = WGS[j]
                    v_op(lambda v, W=W: v.tensor_add(
                        W["gs"], W["gm"][:, 0:4], W["gm"][:, 4:8]))
                vsync()
                for j in range(4):
                    W = WGS[j]
                    v_op(lambda v, W=W: v.reduce_max(W["r1"], W["gs"],
                                                     axis=AX.X))
                vsync()
                for j in range(4):
                    W = WGS[j]
                    v_op(lambda v, W=W: v.tensor_scalar(
                        W["eqg"], W["gs"], W["r1"], None, op0=ALU.is_equal))
                vsync()
                for j in range(4):
                    W = WGS[j]
                    v_op(lambda v, W=W: v.scalar_tensor_tensor(
                        W["tmp"][:, 0:4], W["eqg"], -1e30, W["gs"],
                        op0=ALU.mult, op1=ALU.add))
                vsync()
                for j in range(4):
                    W = WGS[j]
                    v_op(lambda v, W=W: v.reduce_max(W["r2"],
                                                     W["tmp"][:, 0:4],
                                                     axis=AX.X))
                vsync()
                for j in range(4):
                    W = WGS[j]
                    v_op(lambda v, W=W: v.tensor_scalar(
                        W["keep"], W["gs"], W["r2"], None, op0=ALU.is_ge))
                vsync()
                for j in range(4):
                    W = WGS[j]
                    v_op(lambda v, W=W: v.memset(W["ke"], 1.0))
                vsync()
                for gi in range(4):
                    for j in range(4):
                        W = WGS[j]
                        v_op(lambda v, W=W, gi=gi: v.tensor_scalar(
                            W["ke"][:, 4 * gi:4 * gi + 4],
                            W["ke"][:, 4 * gi:4 * gi + 4],
                            W["keep"][:, gi:gi + 1], None, op0=ALU.mult))
                    vsync()
                for j in range(4):
                    W = WGS[j]
                    v_op(lambda v, W=W: v.tensor_scalar(
                        W["ke"], W["ke"], 1e30, -1e30,
                        op0=ALU.mult, op1=ALU.add))
                vsync()
                for j in range(4):
                    W = WGS[j]
                    v_op(lambda v, W=W: v.tensor_add(W["s2k"], W["s2"],
                                                     W["ke"]))
                vsync()
                for j in range(4):
                    W = WGS[j]
                    v_op(lambda v, W=W: v.reduce_max(W["m1"], W["s2k"],
                                                     axis=AX.X))
                vsync()
                for j in range(4):
                    W = WGS[j]
                    v_op(lambda v, W=W: v.tensor_scalar(
                        W["e1"], W["s2k"], W["m1"], None, op0=ALU.is_equal))
                vsync()
                for j in range(4):
                    W = WGS[j]
                    v_op(lambda v, W=W: v.scalar_tensor_tensor(
                        W["tmp"], W["e1"], -1e30, W["s2k"],
                        op0=ALU.mult, op1=ALU.add))
                vsync()
                for j in range(4):
                    W = WGS[j]
                    v_op(lambda v, W=W: v.reduce_max(W["m2"], W["tmp"],
                                                     axis=AX.X))
                vsync()
                for j in range(4):
                    W = WGS[j]
                    v_op(lambda v, W=W: v.tensor_scalar(
                        W["e2"], W["tmp"], W["m2"], None, op0=ALU.is_equal))
                vsync()
                for j in range(4):
                    W = WGS[j]
                    v_op(lambda v, W=W: v.tensor_add(W["e1"], W["e1"],
                                                     W["e2"]))
                vsync()
                for j in range(4):
                    W = WGS[j]
                    v_op(lambda v, W=W, j=j: v.tensor_mul(
                        CT[:, j * 16: j * 16 + 16], W["p0"], W["e1"]))
                vsync()
                em.wait_now("tensor", chV)
                for j in range(4):
                    em.bank_acquire("ptf")
                    t_tr(PTF[0:16, 0:128], CT[:, j * 16: j * 16 + 16],
                         IDF[:, 0:128], apl=[em.mark_ready("ptf", chT)])
                    em.wait_ready("scalar", "ptf")
                    s_act(CTROW[0:16, j * 128: j * 128 + 128],
                          PTF[0:16, 0:128], AF.Copy)
                    em.free_after("ptf", "scalar")
                em.wait_now("vector", chS)
                v_op(lambda v: v.tensor_copy(CTB[:], CTROW[:]))

                shtok = {}

                def sh_load(it):
                    sl = it % 2
                    base = 4096 + sl * 1024
                    if it < 2:
                        if w2hi_guard[0]:
                            em.wait_val("sync", chT, w2hi_guard[0])
                    elif shguard[sl]:
                        em.wait_val("sync", chT, shguard[sl])
                    dma("sync", W2[:, base:base + 1024].rearrange(
                        "p (kt c) -> p kt c", kt=8),
                        dstg["ws1_b"][it].rearrange("kt p c -> p kt c"), chW)
                    dma("sync", W2[:, 6144 + sl * 1024: 7168 + sl * 1024]
                        .rearrange("p (kt c) -> p kt c", kt=8),
                        dstg["ws3_b"][it].rearrange("kt p c -> p kt c"), chW)
                    shtok[it] = chW.n

                shguard = [0, 0]
                sh_load(0)
                for it in range(8):
                    sl = it % 2
                    base = 4096 + sl * 1024
                    if it + 1 < 8:
                        sh_load(it + 1)
                    em.wait_val("tensor", chW, shtok[it])
                    b1, b3 = (0, 1) if it % 2 == 0 else (4, 5)
                    em.bank_acquire(b1)
                    em.bank_acquire(b3)
                    for kt in range(8):
                        apl = [em.mark_ready(b1, chT)] if kt == 7 else ()
                        t_mm(PS[b1][:, 0:512],
                             W2[:, base + kt * 128: base + kt * 128 + 128],
                             A3[:, kt * 512: kt * 512 + 512],
                             start=(kt == 0), stop=(kt == 7), apl=apl)
                    for kt in range(8):
                        apl = [em.mark_ready(b3, chT)] if kt == 7 else ()
                        t_mm(PS[b3][:, 0:512],
                             W2[:, 6144 + sl * 1024 + kt * 128:
                                6144 + sl * 1024 + kt * 128 + 128],
                             A3[:, kt * 512: kt * 512 + 512],
                             start=(kt == 0), stop=(kt == 7), apl=apl)
                    shguard[sl] = chT.n
                    em.wait_ready("scalar", b1)
                    sa = SCB[:, (it % 2) * 1024: (it % 2) * 1024 + 512]
                    if scb_ffn_guard[it % 2]:
                        em.wait_val("scalar", chV, scb_ffn_guard[it % 2])
                    if scb_guard[it % 2]:
                        em.wait_val("scalar", chT, scb_guard[it % 2])
                    s_act(sa, PS[b1][:, 0:512], AF.Silu,
                          bias=CC[:, 56 + it:57 + it])
                    em.free_after(b1, "scalar")
                    em.wait_ready("vector", b3)
                    em.wait_now("vector", chS)
                    v_op(lambda v, it=it, sa=sa, b3=b3: v.scalar_tensor_tensor(
                        W2[:, it * 512: it * 512 + 512], PS[b3][:, 0:512],
                        CC[:, 72 + it:73 + it], sa,
                        op0=ALU.add, op1=ALU.mult))
                    em.free_after(b3, "vector")
                    scb_ffn_guard[it % 2] = chV.n
                eslot_guard[2] = max(eslot_guard[2], chT.n)

                slots = [W1[:, 0:4096], W1[:, 4096:8192], W2[:, 4096:8192]]
                etok = {}

                def e_load(li, nm, e2):
                    if eslot_guard[li % 3]:
                        em.wait_val("sync", chT, eslot_guard[li % 3])
                    dma("sync", slots[li % 3].rearrange(
                        "p (kt c) -> p kt c", kt=8),
                        dstg[nm][e2].rearrange("kt p c -> p kt c"), chW)
                    etok[li] = chW.n

                em.wait_now("tensor", chV)
                e_load(0, "we1_b", 0)
                e_load(1, "we3_b", 0)
                h1banks = (0, 1, 4, 5)
                h3banks = (2, 3)
                for e in range(NE):
                    if e + 1 < NE:
                        e_load(2 * (e + 1), "we1_b", e + 1)
                    em.bank_acquire(2)
                    t_mm(PS[2][:, 0:512],
                         EXHB[:, e * 128: e * 128 + 128],
                         CTB[:, 0:512], start=True, stop=True,
                         apl=[em.mark_ready(2, chT)])
                    em.wait_ready("scalar", 2)
                    bcb = A3[:, 4096:4608]
                    em.wait_now("scalar", chV)
                    s_act(bcb, PS[2][:, 0:512], AF.Copy)
                    em.free_after(2, "scalar")
                    s1 = slots[(2 * e) % 3]
                    s3 = slots[(2 * e + 1) % 3]
                    em.wait_val("tensor", chW, etok[2 * e])
                    for it in range(4):
                        b1 = h1banks[it]
                        em.bank_acquire(b1)
                        for kt in range(8):
                            apl = [em.mark_ready(b1, chT)] if kt == 7 else ()
                            t_mm(PS[b1][:, 0:512],
                                 s1[:, kt * 512 + it * 128:
                                    kt * 512 + it * 128 + 128],
                                 A3[:, kt * 512: kt * 512 + 512],
                                 start=(kt == 0), stop=(kt == 7), apl=apl)
                    eslot_guard[(2 * e) % 3] = chT.n
                    if e + 1 < NE:
                        e_load(2 * (e + 1) + 1, "we3_b", e + 1)
                    em.wait_val("tensor", chW, etok[2 * e + 1])
                    for it in range(4):
                        b1 = h1banks[it]
                        b3 = h3banks[it % 2]
                        em.bank_acquire(b3)
                        for kt in range(8):
                            apl = [em.mark_ready(b3, chT)] if kt == 7 else ()
                            t_mm(PS[b3][:, 0:512],
                                 s3[:, kt * 512 + it * 128:
                                    kt * 512 + it * 128 + 128],
                                 A3[:, kt * 512: kt * 512 + 512],
                                 start=(kt == 0), stop=(kt == 7), apl=apl)
                        par2 = it % 2
                        sa = SCB[:, par2 * 1024: par2 * 1024 + 512]
                        sb = SCB[:, par2 * 1024 + 512: par2 * 1024 + 1024]
                        em.wait_ready("scalar", b1)
                        if scb_ffn_guard[par2]:
                            em.wait_val("scalar", chV, scb_ffn_guard[par2])
                        if scb_guard[par2]:
                            em.wait_val("scalar", chT, scb_guard[par2])
                        ei = e * 4 + it
                        s_act(sa, PS[b1][:, 0:512], AF.Silu,
                              bias=CC[:, 80 + ei:81 + ei])
                        em.free_after(b1, "scalar")
                        em.wait_ready("scalar", b3)
                        s_act(sb, PS[b3][:, 0:512], AF.Identity,
                              bias=CC[:, 144 + ei:145 + ei])
                        em.free_after(b3, "scalar")
                        em.wait_now("vector", chS)
                        n = e * 4 + it
                        if n < 32:
                            chdst = A1[:, n * 512: n * 512 + 512]
                        else:
                            chdst = A2[:, (n - 32) * 512: (n - 32) * 512 + 512]
                        v_op(lambda v, sa=sa, bcb=bcb: v.tensor_mul(
                            sa, sa, bcb))
                        v_op(lambda v, chdst=chdst, sa=sa, sb=sb:
                             v.tensor_mul(chdst, sa, sb))
                        scb_ffn_guard[par2] = chV.n
                    eslot_guard[(2 * e + 1) % 3] = chT.n

                em.wait_now("tensor", chV)
                ybuf = [W2[:, 4096:8192], W1[:, 2048:6144]]
                ybuf_guard = [eslot_guard[2],
                              max(eslot_guard[0], eslot_guard[1])]
                ws2_guard = [max(eslot_guard[0], eslot_guard[1]), 0]
                ytok = {}
                wtok = {}

                def y_load(s):
                    dt2, half = s // 2, s % 2
                    if ybuf_guard[s % 2]:
                        em.wait_val("sync", chT, ybuf_guard[s % 2])
                    dma("sync",
                        ybuf[s % 2].rearrange(
                            "p (e it c) -> p e it c", e=8, it=4),
                        dstg["we2_b"][dt2, 8 * half:8 * half + 8]
                        .rearrange("e it p c -> p e it c"), chW)
                    ytok[s] = chW.n

                def ws2_load(dt2):
                    sl = dt2 % 2
                    if ws2_guard[sl]:
                        em.wait_val("sync", chT, ws2_guard[sl])
                    dma("sync", W1[:, sl * 1024: sl * 1024 + 1024]
                        .rearrange("p (it c) -> p it c", it=8),
                        dstg["ws2_b"][dt2].rearrange("it p c -> p it c"), chW)
                    wtok[dt2] = chW.n

                y_load(0)
                ws2_load(0)
                y_load(1)
                for dt in range(8):
                    em.bank_acquire(2)
                    for s in (2 * dt, 2 * dt + 1):
                        if s % 2 == 1 and dt + 1 < 8:
                            ws2_load(dt + 1)
                        em.wait_val("tensor", chW, ytok[s])
                        for n0 in range(32):
                            n = (s % 2) * 32 + n0
                            if n < 32:
                                chsrc = A1[:, n * 512: n * 512 + 512]
                            else:
                                chsrc = A2[:, (n - 32) * 512:
                                           (n - 32) * 512 + 512]
                            apl = [em.mk_inc(chT, 1)] if n0 == 31 else ()
                            t_mm(PS[2][:, 0:512],
                                 ybuf[s % 2][:, n0 * 128: n0 * 128 + 128],
                                 chsrc, start=(s % 2 == 0 and n0 == 0),
                                 stop=False, apl=apl)
                        ybuf_guard[s % 2] = chT.n
                        if s + 2 < 16:
                            y_load(s + 2)
                    t_mm(PS[2][:, 0:512], BE2[:, dt * 128: dt * 128 + 128],
                         CTB[:, 0:512], start=False, stop=False)
                    em.wait_val("tensor", chW, wtok[dt])
                    for it in range(8):
                        last = it == 7
                        apl = [em.mark_ready(2, chT)] if last else ()
                        t_mm(PS[2][:, 0:512],
                             W1[:, (dt % 2) * 1024 + it * 128:
                                (dt % 2) * 1024 + it * 128 + 128],
                             W2[:, it * 512: it * 512 + 512],
                             start=False, stop=last, apl=apl)
                    ws2_guard[dt % 2] = chT.n
                    em.wait_ready("vector", 2)
                    v_op(lambda v, dt=dt, c=c: v.scalar_tensor_tensor(
                        Rt(dt, c), PS[2][:, 0:512], CC[:, 64 + dt:65 + dt],
                        Rt(dt, c), op0=ALU.add, op1=ALU.add))
                    em.free_after(2, "vector")
                eslot_guard[0] = chT.n
                eslot_guard[1] = chT.n
                eslot_guard[2] = chT.n
                w2hi_guard[0] = chT.n
            em.barrier(("tensor", "vector", "scalar", "sync"))

        def phase_attn_b(b):
            TB = 256
            NCB = T // TB

            def vsync():
                ap = em.mk_inc(chV, 1)
                em.on("vector", lambda v, ap=ap: ap(v.engine_nop()))
                em.wait_now("vector", chV)

            sdummy = SCF[0:1, 2238:2239]

            def ssync():
                ap = em.mk_inc(chS, 1)
                em.on("scalar", lambda s, ap=ap: ap(
                    s.activation(out=sdummy, in_=sdummy, func=AF.Copy)))
                em.wait_now("scalar", chS)

            v_op(lambda v: v.memset(CAB[:], 0.0))
            v_op(lambda v: v.memset(SS[:], 0.0))
            v_op(lambda v: v.memset(SQ[:], 0.0))
            vsync()
            for c in range(NCH):
                ln_chunk(c, lambda dt, c=c: A1[:, dt * 2048 + c * TC:
                                               dt * 2048 + c * TC + TC])
            em.barrier(("tensor", "vector", "scalar", "sync", "gpsimd"))

            EK = SCF[:, 0:256]
            A_ = SCF[:, 256:512]
            SA = SCF[:, 512:800]
            SB = SCF[:, 800:1088]
            WREP = SCF[:, 1088:1344]
            NN = SCF[:, 1344:1600]
            DD = SCF[:, 1600:1856]
            RR = SCF[:, 1856:2112]
            ss_tmp = SCF[:, 2232:2233]
            sq_tmp = SCF[:, 2233:2234]

            wslot_guard = [0, 0]
            for cb in range(NCB):
                em.wait_now("vector", chT)
                for mi, moff in ((0, 0), (1, 2048), (2, 4096)):
                    for dt in range(NDT):
                        xt = A1[:, dt * 2048 + cb * TB:
                                dt * 2048 + cb * TB + TB]
                        dst = A3[:, moff + dt * TB: moff + dt * TB + TB]
                        v_op(lambda v, xt=xt, dst=dst, mi=mi, dt=dt:
                             v.tensor_scalar(dst, xt,
                                             CC[:, 8 * mi + dt:8 * mi + dt
                                                + 1], None, op0=ALU.mult))
                for mi, moff in ((0, 0), (1, 2048), (2, 4096)):
                    for dt in range(NDT):
                        dst = A3[:, moff + dt * TB: moff + dt * TB + TB]
                        if cb == 0:
                            xp = A1[:, dt * 2048: dt * 2048 + TB - 1]
                            v_op(lambda v, xp=xp, dst=dst, mi=mi, dt=dt:
                                 v.scalar_tensor_tensor(
                                     dst[:, 1:TB], xp,
                                     DC[:, 24 + 8 * mi + dt:
                                        25 + 8 * mi + dt],
                                     dst[:, 1:TB],
                                     op0=ALU.mult, op1=ALU.add))
                        else:
                            xp = A1[:, dt * 2048 + cb * TB - 1:
                                    dt * 2048 + cb * TB + TB - 1]
                            v_op(lambda v, xp=xp, dst=dst, mi=mi, dt=dt:
                                 v.scalar_tensor_tensor(
                                     dst, xp,
                                     DC[:, 24 + 8 * mi + dt:
                                        25 + 8 * mi + dt],
                                     dst, op0=ALU.mult, op1=ALU.add))
                vsync()
                for g in range(8):
                    sl = g % 2
                    base = sl * 3072
                    if wslot_guard[sl]:
                        em.wait_val("sync", chT, wslot_guard[sl])
                    for wi, nm in ((0, "wr_b"), (1, "wk_b"), (2, "wv_b")):
                        dma("sync", W1[:, base + wi * 1024:
                                       base + wi * 1024 + 1024].rearrange(
                                           "p (kt c) -> p kt c", kt=8),
                            dstg[nm][g].rearrange("kt p c -> p kt c"), chW)
                    em.wait_now("tensor", chW)
                    em.wait_now("tensor", chV)
                    for bank, wi, src_off in ((3, 0, 0), (4, 1, 2048),
                                              (5, 2, 4096)):
                        em.bank_acquire(bank)
                        for kt in range(8):
                            apl = ([em.mark_ready(bank, chT)]
                                   if kt == 7 else ())
                            t_mm(PS[bank][:, 0:TB],
                                 W1[:, base + wi * 1024 + kt * 128:
                                    base + wi * 1024 + kt * 128 + 128],
                                 A3[:, src_off + kt * TB:
                                    src_off + kt * TB + TB],
                                 start=(kt == 0), stop=(kt == 7), apl=apl)
                    wslot_guard[sl] = chT.n
                    em.wait_ready("scalar", 4)
                    em.wait_now("scalar", chV)
                    s_act(EK, PS[4][:, 0:TB], AF.Exp)
                    em.free_after(4, "scalar")
                    em.wait_ready("scalar", 3)
                    s_act(RR, PS[3][:, 0:TB], AF.Sigmoid)
                    em.free_after(3, "scalar")
                    em.wait_ready("vector", 5)
                    em.wait_now("vector", chS)
                    v_op(lambda v: v.memset(WREP, 1.0))
                    v_op(lambda v, g=g: v.tensor_mul(A_, EK, PS[5][:, 0:TB]))
                    em.free_after(5, "vector")
                    v_op(lambda v, g=g: v.tensor_scalar(
                        WREP, WREP, DC[:, g:g + 1], None, op0=ALU.mult))
                    vsync()
                    v_op(lambda v, g=g: v.tensor_tensor_scan(
                        SA[:, 1:257], WREP, A_, CAB[:, g:g + 1],
                        op0=ALU.mult, op1=ALU.add))
                    v_op(lambda v, g=g: v.tensor_tensor_scan(
                        SB[:, 1:257], WREP, EK, CAB[:, 8 + g:9 + g],
                        op0=ALU.mult, op1=ALU.add))
                    v_op(lambda v, g=g: v.tensor_copy(SA[:, 0:1],
                                                      CAB[:, g:g + 1]))
                    v_op(lambda v, g=g: v.tensor_copy(SB[:, 0:1],
                                                      CAB[:, 8 + g:9 + g]))
                    vsync()
                    v_op(lambda v, g=g: v.tensor_copy(CAB[:, g:g + 1],
                                                      SA[:, 256:257]))
                    v_op(lambda v, g=g: v.tensor_copy(CAB[:, 8 + g:9 + g],
                                                      SB[:, 256:257]))
                    v_op(lambda v, g=g: v.tensor_scalar(
                        EK, SA[:, 1:257], DC[:, 8 + g:9 + g], None,
                        op0=ALU.mult))
                    v_op(lambda v, g=g: v.tensor_scalar(
                        A_, SB[:, 1:257], DC[:, 8 + g:9 + g], 1e-9,
                        op0=ALU.mult, op1=ALU.add))
                    vsync()
                    v_op(lambda v, g=g: v.scalar_tensor_tensor(
                        NN, SA[:, 0:256], DC[:, 16 + g:17 + g], EK,
                        op0=ALU.mult, op1=ALU.add))
                    v_op(lambda v, g=g: v.scalar_tensor_tensor(
                        DD, SB[:, 0:256], DC[:, 16 + g:17 + g], A_,
                        op0=ALU.mult, op1=ALU.add))
                    vsync()
                    em.wait_now("scalar", chV)
                    s_act(DD, DD, AF.Ln)
                    ssync()
                    s_act(DD, DD, AF.Exp, scale=-1.0)
                    em.wait_now("vector", chS)
                    v_op(lambda v: v.tensor_mul(NN, NN, DD))
                    vsync()
                    v_op(lambda v: v.tensor_mul(A_, NN, RR))
                    vsync()
                    odst = A2[:, g * 2048 + cb * TB: g * 2048 + cb * TB + TB]
                    v_op(lambda v, odst=odst: v.tensor_scalar(
                        odst, A_, 1.0, None, op0=ALU.mult, op1=ALU.add,
                        accum_out=ss_tmp))
                    v_op(lambda v: v.scalar_tensor_tensor(
                        DD, A_, 1.0, A_, op0=ALU.mult, op1=ALU.mult,
                        accum_out=sq_tmp))
                    vsync()
                    v_op(lambda v, g=g: v.tensor_add(
                        SS[:, g:g + 1], SS[:, g:g + 1], ss_tmp))
                    v_op(lambda v, g=g: v.tensor_add(
                        SQ[:, g:g + 1], SQ[:, g:g + 1], sq_tmp))
                    vsync()

            em.barrier(("tensor", "vector", "scalar", "sync"))
            em.bank_acquire("ptf")
            t_mm(PTF[0:2, 0:8], HSEL[:], SS[:], start=True, stop=True)
            t_mm(PTF[0:2, 8:16], HSEL[:], SQ[:], start=True, stop=True,
                 apl=[em.mark_ready("ptf", chT)])
            em.wait_ready("scalar", "ptf")
            HSROW = SCF[0:2, 2240:2256]
            s_act(HSROW, PTF[0:2, 0:16], AF.Copy)
            em.free_after("ptf", "scalar")
            hm = SCF[0:2, 2256:2264]
            hv = SCF[0:2, 2264:2272]
            em.wait_now("vector", chS)
            NINV = 1.0 / (64.0 * 2048.0)
            v_op(lambda v: v.tensor_scalar(hm, HSROW[0:2, 0:8], NINV, None,
                                           op0=ALU.mult))
            v_op(lambda v: v.tensor_scalar(hv, HSROW[0:2, 8:16], NINV, None,
                                           op0=ALU.mult))
            vsync()
            hm2 = SCF[0:2, 2272:2280]
            v_op(lambda v: v.tensor_mul(hm2, hm, hm))
            vsync()
            v_op(lambda v: v.tensor_sub(hv, hv, hm2))
            vsync()
            em.wait_now("scalar", chV)
            s_act(hv, hv, AF.Ln, bias=CC[0:2, 208:209])
            ssync()
            s_act(hv, hv, AF.Exp, scale=-0.5)
            em.wait_now("tensor", chS)
            em.bank_acquire("ptf")
            t_mm(PTF[:, 0:8], HSELT[:], hm, start=True, stop=True)
            t_mm(PTF[:, 8:16], HSELT[:], hv, start=True, stop=True,
                 apl=[em.mark_ready("ptf", chT)])
            em.wait_ready("vector", "ptf")
            v_op(lambda v: v.tensor_mul(SCL[:], PTF[:, 8:16],
                                        CC[:, 40:48]))
            em.free_after("ptf", "vector")
            vsync()
            tb8 = SCF[:, 2280:2288]
            v_op(lambda v: v.tensor_mul(tb8, PTF[:, 0:8], SCL[:]))
            vsync()
            v_op(lambda v: v.tensor_sub(BIA[:], CC[:, 48:56], tb8))
            vsync()
            for g in range(8):
                for c in range(NCH):
                    src_t = A2[:, g * 2048 + c * TC: g * 2048 + c * TC + TC]
                    dst_t = A1[:, g * 2048 + c * TC: g * 2048 + c * TC + TC]
                    v_op(lambda v, src_t=src_t, dst_t=dst_t, g=g:
                         v.tensor_scalar(dst_t, src_t, SCL[:, g:g + 1],
                                         BIA[:, g:g + 1], op0=ALU.mult,
                                         op1=ALU.add))
            vsync()
            woguard = [0, 0]
            for et in range(8):
                sl = et % 2
                if woguard[sl]:
                    em.wait_val("sync", chT, woguard[sl])
                dma("sync", W2[:, sl * 1024: sl * 1024 + 1024].rearrange(
                    "p (kt c) -> p kt c", kt=8),
                    dstg["wo_b"][et].rearrange("kt p c -> p kt c"), chW)
                em.wait_now("tensor", chW)
                em.wait_now("tensor", chV)
                for c in range(NCH):
                    bank = 4 + c % 2
                    em.bank_acquire(bank)
                    for kt in range(8):
                        last = kt == 7
                        apl = [em.mark_ready(bank, chT)] if last else ()
                        t_mm(PS[bank][:, 0:512],
                             W2[:, sl * 1024 + kt * 128:
                                sl * 1024 + kt * 128 + 128],
                             A1[:, kt * 2048 + c * TC:
                                kt * 2048 + c * TC + TC],
                             start=(kt == 0), stop=last, apl=apl)
                    em.wait_ready("vector", bank)
                    v_op(lambda v, et=et, c=c, bank=bank: v.tensor_add(
                        Rt(et, c), Rt(et, c), PS[bank][:, 0:512]))
                    em.free_after(bank, "vector")
                woguard[sl] = chT.n
            em.barrier()

        for b in range(NB):
            dma("gpsimd", R[:].rearrange("p (dt t) -> p dt t", dt=8),
                x_in[b].rearrange("(dt p) t -> p dt t", p=128), chG)
            em.barrier()
            if "a" in phases:
                phase_attn_a(b)
            if "f1" in phases:
                phase_ffn(b)
            if "b" in phases:
                phase_attn_b(b)
            if "f2" in phases:
                phase_ffn(b)
            ap = em.mk_inc(chO, 16)
            dstp = y_out[b].rearrange("(dt p) t -> p dt t", p=128)
            em.on("sync", lambda e, d=dstp, ap=ap: ap(
                e.dma_start(out=d,
                            in_=R[:].rearrange("p (dt t) -> p dt t", dt=8))))
            em.barrier()

        for nm, out_ap in dump_outs.items():
            buf = {"R": R, "A1": A1, "A2": A2, "A3": A3, "SCF": SCF,
                   "CT": CT, "SS": SS, "SQ": SQ, "DC": DC}[nm]
            ap = em.mk_inc(chO, 16)
            em.on("sync", lambda e, bu=buf, o=out_ap, ap=ap: ap(
                e.dma_start(out=o, in_=bu[:])))
        em.wait_now("sync", chO)

        em.run(block)
    return nc



def _prep_shared_inputs(kw):
    f = np.float32
    out = {}
    wqkv = np.asarray(kw["L0_attn_a_w_qkv"], f)
    cols = []
    for s in range(8):
        for part in range(3):
            for hh in (2 * s, 2 * s + 1):
                base = part * 1024 + hh * 64
                cols.append(np.arange(base, base + 64))
    perm = np.concatenate(cols)
    wq = wqkv[:, perm].reshape(8, 128, 8, 384)
    out["wqkv"] = np.ascontiguousarray(wq.transpose(2, 0, 1, 3))

    def tile_oi(w):
        return np.ascontiguousarray(
            w.reshape(8, 128, 8, 128).transpose(2, 0, 1, 3))

    out["wout"] = tile_oi(np.asarray(kw["L0_attn_a_w_out"], f))
    out["wr"] = tile_oi(np.asarray(kw["L0_attn_b_W_r"], f).T)
    out["wk"] = tile_oi(np.asarray(kw["L0_attn_b_W_k"], f).T)
    out["wv"] = tile_oi(np.asarray(kw["L0_attn_b_W_v"], f).T)
    out["wo"] = tile_oi(np.asarray(kw["L0_attn_b_W_o"], f).T)
    out["we1"] = np.ascontiguousarray(
        np.asarray(kw["L0_ffn_We1"], f).reshape(16, 8, 128, 512))
    out["we3"] = np.ascontiguousarray(
        np.asarray(kw["L0_ffn_We3"], f).reshape(16, 8, 128, 512))
    out["we2"] = np.ascontiguousarray(
        np.asarray(kw["L0_ffn_We2"], f).reshape(16, 4, 128, 8, 128)
        .transpose(3, 0, 1, 2, 4))
    out["ws1"] = tile_oi(np.asarray(kw["L0_ffn_ws1"], f))
    out["ws3"] = tile_oi(np.asarray(kw["L0_ffn_ws3"], f))
    out["ws2"] = tile_oi(np.asarray(kw["L0_ffn_ws2"], f))
    gwt = np.asarray(kw["L0_ffn_gate_w"], f).T
    out["gwt"] = np.ascontiguousarray(
        gwt.reshape(8, 128, 16).transpose(1, 0, 2).reshape(128, 128))
    out["be2m"] = np.asarray(kw["L0_ffn_be2"], f)
    out["gateb"] = np.asarray(kw["L0_ffn_gate_b"], f).reshape(1, 16)

    cc = np.zeros((128, 224), f)

    def pack8(vec):
        return np.asarray(vec, f).reshape(8, 128).T

    cc[:, 0:8] = pack8(kw["L0_attn_b_time_mix_r"])
    cc[:, 8:16] = pack8(kw["L0_attn_b_time_mix_k"])
    cc[:, 16:24] = pack8(kw["L0_attn_b_time_mix_v"])
    cc[:, 24:32] = pack8(np.asarray(kw["L0_attn_b_time_decay"], f).reshape(-1))
    cc[:, 32:40] = pack8(np.asarray(kw["L0_attn_b_time_first"], f).reshape(-1))
    cc[:, 40:48] = pack8(kw["L0_attn_b_gn_weight"])
    cc[:, 48:56] = pack8(kw["L0_attn_b_gn_bias"])
    cc[:, 56:64] = pack8(kw["L0_ffn_bs1"])
    cc[:, 64:72] = pack8(kw["L0_ffn_bs2"])
    cc[:, 72:80] = pack8(kw["L0_ffn_bs3"])
    be1 = np.asarray(kw["L0_ffn_be1"], f)
    be3 = np.asarray(kw["L0_ffn_be3"], f)
    cc[:, 80:144] = be1.reshape(16, 4, 128).transpose(2, 0, 1).reshape(128, 64)
    cc[:, 144:208] = be3.reshape(16, 4, 128).transpose(2, 0, 1).reshape(128, 64)
    cc[:, 208] = 1e-5
    cc[:, 209] = 1e-9
    out["cc"] = cc

    out["idf"] = np.eye(128, dtype=f)
    out["ones128"] = np.ones((128, 128), f)
    out["onesrow"] = np.ones((1, 128), f)
    lo = np.zeros((1, 128), f)
    lo[0, 0:64] = 1.0
    out["oneslo"] = lo
    hi = np.zeros((1, 128), f)
    hi[0, 64:128] = 1.0
    out["oneshi"] = hi
    out["stair"] = (np.arange(896)[None, :] >=
                    (np.arange(128)[:, None] + 384)).astype(f)
    out["exh"] = np.kron(np.eye(16, dtype=f), np.ones((1, 128), f))
    hsel = np.zeros((128, 2), f)
    hsel[0:64, 0] = 1.0
    hsel[64:128, 1] = 1.0
    out["hsel"] = hsel
    out["hselt"] = np.ascontiguousarray(hsel.T)
    return out


def _run(inputs, phases=("a", "f1", "b", "f2"), dumps=(), cores=8):
    x = np.asarray(inputs["L0_attn_a_x"], np.float32)
    shared = _prep_shared_inputs(inputs)
    nc = build_program(phases=phases, dumps=dumps)
    in_maps = []
    for core in range(cores):
        m = dict(shared)
        m["x"] = np.ascontiguousarray(
            x[core * NB:(core + 1) * NB].transpose(0, 2, 1))
        in_maps.append(m)
    res = run_bass_kernel_spmd(nc, in_maps, core_ids=list(range(cores)))
    return res


def kernel(**inputs):
    res = _run(inputs)
    outs = []
    for core in range(8):
        y = np.asarray(res.results[core]["y"])
        outs.append(np.ascontiguousarray(y.transpose(0, 2, 1)))
    return np.concatenate(outs, axis=0)


# version 03
# speedup vs baseline: 1.3613x; 1.1127x over previous
import contextlib
import numpy as np

import concourse.bass as bass
import concourse.mybir as mybir
from concourse.bass_utils import run_bass_kernel_spmd

F32 = mybir.dt.float32
BF16 = mybir.dt.bfloat16
AF = mybir.ActivationFunctionType
ALU = mybir.AluOpType
AX = mybir.AxisListType

D = 1024
T = 2048
H = 16
HD = 64
NE = 16
INTER = 512
NB = 2
NDT = 8
TC = 512
NCH = T // TC
EPS = 1e-5

ENGINES = ("sync", "gpsimd", "vector", "scalar", "tensor")


class Ch:
    def __init__(self, handle):
        self.h = handle
        self.n = 0


class Emitter:

    def __init__(self, sems):
        self.q = {e: [] for e in ENGINES}
        self.ch = {name: Ch(h) for name, h in sems.items()}
        self.bar = {e: self.ch["bar_" + e] for e in ENGINES}
        self.bank_free = {}
        self.bank_ready = {}

    def on(self, eng, fn):
        self.q[eng].append(fn)

    def mk_inc(self, ch, k=1):
        ch.n += k
        h = ch.h

        def apply(instr):
            instr.then_inc(h, k)
            return instr
        return apply

    def tok(self, ch):
        ap = self.mk_inc(ch, 1)
        return ap, (ch, ch.n)

    def wait_tok(self, eng, tok):
        ch, v = tok
        self.wait_val(eng, ch, v)

    def wait_now(self, eng, ch):
        self.wait_val(eng, ch, ch.n)

    def wait_val(self, eng, ch, v):
        h = ch.h
        if v > 0:
            self.on(eng, lambda e: e.wait_ge(h, v))

    def barrier(self, engines=ENGINES):
        dma_chs = {"sync": ("w", "o"), "gpsimd": ("g",)}
        for e in engines:
            for cn in dma_chs.get(e, ()):
                self.wait_now(e, self.ch[cn])
        toks = {}
        for e in engines:
            ap = self.mk_inc(self.bar[e], 1)
            self.on(e, lambda eng, ap=ap: ap(eng.nop()))
            toks[e] = self.bar[e].n
        for e in engines:
            for o in engines:
                if o != e:
                    self.wait_val(e, self.bar[o], toks[o])

    def bank_acquire(self, bank):
        if bank in self.bank_free:
            self.wait_tok("tensor", self.bank_free[bank])

    def mark_ready(self, bank, ch):
        ap, tk = self.tok(ch)
        self.bank_ready[bank] = tk
        return ap

    def wait_ready(self, eng, bank):
        self.wait_tok(eng, self.bank_ready[bank])

    def mark_free(self, bank, eng):
        chmap = {"vector": "v", "scalar": "s", "gpsimd": "g", "sync": "w"}
        ap, tk = self.tok(self.ch[chmap[eng]])
        self.bank_free[bank] = tk
        return ap

    def free_after(self, bank, eng):
        chmap = {"vector": "v", "scalar": "s", "gpsimd": "g", "sync": "w"}
        ch = self.ch[chmap[eng]]
        self.bank_free[bank] = (ch, ch.n)

    def run(self, block):
        for e in ENGINES:
            fns = self.q[e]

            def body(eng, fns=fns):
                for fn in fns:
                    fn(eng)
            getattr(block, e)(body)


def build_program(phases=("a", "f1", "b", "f2"), dumps=()):
    nc = bass.Bass("TRN2", target_bir_lowering=False)

    din = {}

    def I(name, shape, dt=F32):
        din[name] = nc.dram_tensor(name, list(shape), dt,
                                   kind="ExternalInput").ap()
        return din[name]

    x_in = I("x", (NB, D, T))
    I("wqkv", (8, 8, 128, 384))
    I("wout", (8, 8, 128, 128))
    for nm in ("wr", "wk", "wv", "wo"):
        I(nm, (8, 8, 128, 128))
    I("we1", (NE, 8, 128, 512))
    I("we3", (NE, 8, 128, 512))
    I("we2", (8, NE, 4, 128, 128))
    I("ws1", (8, 8, 128, 128))
    I("ws3", (8, 8, 128, 128))
    I("ws2", (8, 8, 128, 128))
    I("gwt", (128, 128))
    I("be2m", (16, 1024))
    I("gateb", (1, 16))
    I("cc", (128, 224))
    I("idf", (128, 128))
    I("ones128", (128, 128))
    I("onesrow", (1, 128))
    I("oneslo", (1, 128))
    I("oneshi", (1, 128))
    I("stair", (128, 896))
    I("hsel", (128, 2))
    I("exh", (16, 2048))
    I("hselt", (2, 128))

    y_out = nc.dram_tensor("y", [NB, D, T], F32, kind="ExternalOutput").ap()

    dstg = {}
    for nm, shp in (
        ("wqkv_b", (8, 8, 128, 384)), ("wout_b", (8, 8, 128, 128)),
        ("wr_b", (8, 8, 128, 128)), ("wk_b", (8, 8, 128, 128)),
        ("wv_b", (8, 8, 128, 128)), ("wo_b", (8, 8, 128, 128)),
        ("we1_b", (NE, 8, 128, 512)), ("we3_b", (NE, 8, 128, 512)),
        ("we2_b", (8, NE, 4, 128, 128)),
        ("ws1_b", (8, 8, 128, 128)), ("ws3_b", (8, 8, 128, 128)),
        ("ws2_b", (8, 8, 128, 128)),
    ):
        dstg[nm] = nc.dram_tensor(nm, list(shp), BF16, kind="Internal").ap()

    dump_outs = {}
    for nm in dumps:
        shp, dt = {"R": ([128, 16384], F32), "A1": ([128, 16384], BF16),
                   "A2": ([128, 16384], BF16), "A3": ([128, 6144], BF16),
                   "SCF": ([128, 4096], F32), "CT": ([128, 64], F32),
                   "SS": ([128, 8], F32), "SQ": ([128, 8], F32),
                   "DC": ([128, 64], F32)}[nm]
        dump_outs[nm] = nc.dram_tensor("dump_" + nm, shp, dt,
                                       kind="ExternalOutput").ap()

    es = contextlib.ExitStack()
    with es:
        sbt = lambda n, p, f, dt: es.enter_context(
            nc.sbuf_tensor(n, [p, f], dt))
        R = sbt("R", 128, 16384, F32)
        A1 = sbt("A1", 128, 16384, BF16)
        A2 = sbt("A2", 128, 16384, BF16)
        A3 = sbt("A3", 128, 6144, BF16)
        W1 = sbt("W1", 128, 8192, BF16)
        W2 = sbt("W2", 128, 8192, BF16)
        SCF = sbt("SCF", 128, 4096, F32)
        SC2 = sbt("SC2", 128, 512, F32)
        SCB = sbt("SCB", 128, 2048, BF16)
        GP = SCF[0:18, 3072:3584]
        GT = SCF[:, 2112:2184]
        CT = SCF[:, 2048:2112]
        CTROW = SCF[0:16, 3584:4096]
        CTB = sbt("CTB", 16, 512, BF16)
        CC = sbt("CC", 128, 224, F32)
        DC = sbt("DC", 128, 64, F32)
        GWT = sbt("GWT", 128, 128, F32)
        CS = sbt("CS", 128, 16, F32)
        BE2 = sbt("BE2", 16, 1024, BF16)
        GB = sbt("GB", 128, 16, F32)
        IDF = sbt("IDF", 128, 128, F32)
        IDB = sbt("IDB", 128, 128, BF16)
        ONES128B = sbt("ONES128B", 128, 128, BF16)
        ONESROWF = sbt("ONESROWF", 1, 128, F32)
        ONESROWB = sbt("ONESROWB", 1, 128, BF16)
        ONESLO = sbt("ONESLO", 1, 128, F32)
        ONESHI = sbt("ONESHI", 1, 128, F32)
        STAIRB = sbt("STAIRB", 128, 896, BF16)
        HSEL = sbt("HSEL", 128, 2, F32)
        EXHB = sbt("EXHB", 16, 2048, BF16)
        HSELT = sbt("HSELT", 2, 128, F32)
        SS = SCF[:, 2184:2192]
        SQ = SCF[:, 2192:2200]
        CAB = SCF[:, 2200:2216]
        SCL = SCF[:, 2216:2224]
        BIA = SCF[:, 2224:2232]

        PS = [es.enter_context(nc.psum_tensor("PB%d" % i, [128, 512], F32))
              for i in range(6)]
        PTB = es.enter_context(nc.psum_tensor("PTB", [128, 128], BF16))
        PTF = es.enter_context(nc.psum_tensor("PTF", [128, 128], F32))

        sems = {}
        for nm in ["w", "g", "t", "s", "v", "o"] + ["bar_" + e
                                                    for e in ENGINES]:
            sems[nm] = es.enter_context(nc.semaphore("sem_" + nm))

        block = es.enter_context(nc.Block())
        em = Emitter(sems)
        chW, chG, chT, chS, chV, chO = (em.ch[k] for k in
                                        ("w", "g", "t", "s", "v", "o"))

        def dma(eng, dst, src, ch):
            ap = em.mk_inc(ch, 16)
            em.on(eng, lambda e: ap(e.dma_start(out=dst, in_=src)))

        def s_act(out, in_, func, bias=0.0, scale=1.0, accum=None, apl=()):
            ap = em.mk_inc(chS, 1)

            def f(s):
                if accum is not None:
                    ins = s.activation(out=out, in_=in_, func=func, bias=bias,
                                       scale=scale, accum_out=accum)
                else:
                    ins = s.activation(out=out, in_=in_, func=func, bias=bias,
                                       scale=scale)
                ap(ins)
                for a in apl:
                    a(ins)
                return ins
            em.on("scalar", f)

        def v_op(fn, apl=()):
            ap = em.mk_inc(chV, 1)

            def f(v):
                ins = fn(v)
                ap(ins)
                for a in apl:
                    a(ins)
                return ins
            em.on("vector", f)

        import traceback

        def t_mm(out, lhsT, rhs, start, stop, apl=()):
            site = traceback.extract_stack(limit=3)[0]

            def f(t):
                try:
                    ins = t.matmul(out, lhsT, rhs, start=start, stop=stop)
                except Exception:
                    print("t_mm failed from %s:%d" % (site.filename,
                                                      site.lineno))
                    raise
                for a in apl:
                    a(ins)
                return ins
            em.on("tensor", f)

        def t_tr(out, in_, ident, apl=()):
            def f(t):
                ins = t.transpose(out, in_, ident)
                for a in apl:
                    a(ins)
                return ins
            em.on("tensor", f)

        def Rt(dt, c):
            return R[:, dt * 2048 + c * TC: dt * 2048 + c * TC + TC]

        dma("sync", CC[:], din["cc"][:], chW)
        dma("sync", GWT[:], din["gwt"][:], chW)
        dma("sync", IDF[:], din["idf"][:], chW)
        dma("sync", ONESROWF[:], din["onesrow"][:], chW)
        dma("sync", ONESLO[:], din["oneslo"][:], chW)
        dma("sync", ONESHI[:], din["oneshi"][:], chW)
        dma("sync", HSEL[:], din["hsel"][:], chW)
        dma("sync", HSELT[:], din["hselt"][:], chW)
        dma("sync", GB[:], din["gateb"].broadcast_to([128, 16]), chW)
        dma("gpsimd", STAIRB[:], din["stair"][:], chG)
        dma("gpsimd", ONES128B[:], din["ones128"][:], chG)
        dma("gpsimd", IDB[:], din["idf"][:], chG)
        dma("gpsimd", ONESROWB[:], din["onesrow"][:], chG)
        dma("gpsimd", BE2[:], din["be2m"][:], chG)
        dma("gpsimd", EXHB[:], din["exh"][:], chG)

        for e in ("scalar", "vector", "tensor"):
            em.wait_now(e, chW)
            em.wait_now(e, chG)

        s_act(DC[:, 0:8], CC[:, 24:32], AF.Exp)
        s_act(DC[:, 0:8], DC[:, 0:8], AF.Exp, scale=-1.0)
        s_act(DC[:, 8:16], CC[:, 32:40], AF.Exp)
        em.wait_now("vector", chS)

        def vsync0():
            ap = em.mk_inc(chV, 1)
            em.on("vector", lambda v, ap=ap: ap(v.engine_nop()))
            em.wait_now("vector", chV)

        v_op(lambda v: v.tensor_mul(DC[:, 16:24], DC[:, 0:8], DC[:, 8:16]))
        vsync0()
        v_op(lambda v: v.tensor_scalar(DC[:, 16:24], DC[:, 16:24], -1.0, 1.0,
                                       op0=ALU.mult, op1=ALU.add))
        vsync0()
        for j in range(3):
            v_op(lambda v, j=j: v.tensor_scalar(
                DC[:, 24 + 8 * j:32 + 8 * j], CC[:, 8 * j:8 * j + 8],
                -1.0, 1.0, op0=ALU.mult, op1=ALU.add))
        v_op(lambda v: v.memset(SCF[:, 0:1], 1.0))
        em.wait_now("tensor", chV)
        for kt in range(8):
            apl = [em.mark_ready(0, chT)] if kt == 7 else ()
            t_mm(PS[0][0:1, 0:16], SCF[:, 0:1], GWT[:, kt * 16:kt * 16 + 16],
                 start=(kt == 0), stop=(kt == 7), apl=apl)
        em.wait_ready("vector", 0)
        v_op(lambda v: v.tensor_copy(SCF[0:1, 4:20], PS[0][0:1, 0:16]))
        em.free_after(0, "vector")
        em.wait_now("tensor", chV)
        t_mm(PTF[:, 0:16], ONESROWF[:], SCF[0:1, 4:20], True, True,
             apl=[em.mark_ready(6, chT)])
        em.wait_ready("vector", 6)
        v_op(lambda v: v.tensor_copy(CS[:], PTF[:, 0:16]))
        em.free_after(6, "vector")

        state = {"flip": 0, "pend": [0, 0]}

        def stage(dst_ap, src_ap, cols):
            slot = state["flip"]
            state["flip"] = 1 - slot
            buf = W1[:, slot * 4096: slot * 4096 + cols]
            shp = list(src_ap.shape)
            if len(shp) == 3:
                buf = buf.rearrange("p (a c) -> p a c", a=shp[1])
            elif len(shp) == 4:
                buf = buf.rearrange("p (a b c) -> p a b c", a=shp[1],
                                    b=shp[2])
            if state["pend"][slot]:
                em.wait_val("gpsimd", chG, state["pend"][slot])
            ap1 = em.mk_inc(chG, 16)
            em.on("gpsimd", lambda g, b=buf, s=src_ap: ap1(
                g.dma_start(out=b, in_=s)))
            v1 = chG.n
            em.wait_val("gpsimd", chG, v1)
            ap2 = em.mk_inc(chG, 16)
            em.on("gpsimd", lambda g, b=buf, d=dst_ap: ap2(
                g.dma_start(out=d, in_=b)))
            state["pend"][slot] = chG.n

        for s in range(8):
            stage(dstg["wqkv_b"][s].rearrange("kt p c -> p kt c"),
                  din["wqkv"][s].rearrange("kt p c -> p kt c"), 3072)
        for nm in ("wout", "wr", "wk", "wv", "wo"):
            for eh in range(4):
                stage(dstg[nm + "_b"][2 * eh:2 * eh + 2].rearrange(
                          "e kt p c -> p e kt c"),
                      din[nm][2 * eh:2 * eh + 2].rearrange(
                          "e kt p c -> p e kt c"), 2048)
        for nm in ("we1", "we3"):
            for e in range(NE):
                stage(dstg[nm + "_b"][e].rearrange("kt p c -> p kt c"),
                      din[nm][e].rearrange("kt p c -> p kt c"), 4096)
        for dt in range(8):
            for half in range(2):
                stage(dstg["we2_b"][dt, 8 * half:8 * half + 8].rearrange(
                          "e it p c -> p e it c"),
                      din["we2"][dt, 8 * half:8 * half + 8].rearrange(
                          "e it p c -> p e it c"), 4096)
        for nm in ("ws1", "ws3", "ws2"):
            for ih in range(4):
                stage(dstg[nm + "_b"][2 * ih:2 * ih + 2].rearrange(
                          "i kt p c -> p i kt c"),
                      din[nm][2 * ih:2 * ih + 2].rearrange(
                          "i kt p c -> p i kt c"), 2048)
        em.barrier()

        scb_guard = [0, 0]
        scb_ffn_guard = [0, 0]

        def ln_chunk(c, dest_tile_fn):
            for bank in (0, 1):
                em.bank_acquire(bank)
            for dt in range(NDT):
                par = dt % 2
                xb = SCB[:, par * 1024: par * 1024 + 512]
                x2b = SCB[:, par * 1024 + 512: par * 1024 + 1024]
                if scb_guard[par]:
                    em.wait_val("scalar", chT, scb_guard[par])
                if scb_ffn_guard[par]:
                    em.wait_val("scalar", chV, scb_ffn_guard[par])
                s_act(xb, Rt(dt, c), AF.Copy)
                s_act(x2b, Rt(dt, c), AF.Square)
                em.wait_now("tensor", chS)
                apl0 = [em.mark_ready(0, chT)] if dt == NDT - 1 else ()
                t_mm(PS[0][0:1, 0:512], ONES128B[:, 0:1], xb,
                     start=(dt == 0), stop=(dt == NDT - 1), apl=apl0)
                if dt == NDT - 1:
                    apl1 = [em.mark_ready(1, chT)]
                else:
                    apl1 = [em.mk_inc(chT, 1)]
                t_mm(PS[1][0:1, 0:512], ONES128B[:, 0:1], x2b,
                     start=(dt == 0), stop=(dt == NDT - 1), apl=apl1)
                scb_guard[par] = chT.n
            em.wait_ready("vector", 0)
            em.wait_ready("vector", 1)
            arow = SCF[0:1, 0:512]
            brow = SCF[0:1, 512:1024]
            vrow = SCF[0:1, 1024:1536]
            mrow = SCF[0:1, 1536:2048]
            v_op(lambda v: v.tensor_scalar_mul(mrow, PS[0][0:1, 0:512],
                                               1.0 / D))
            em.free_after(0, "vector")
            v_op(lambda v: v.tensor_scalar_mul(vrow, PS[1][0:1, 0:512],
                                               1.0 / D))
            em.free_after(1, "vector")
            v_op(lambda v: v.tensor_mul(arow, mrow, mrow))
            v_op(lambda v: v.tensor_sub(vrow, vrow, arow))
            em.wait_now("scalar", chV)
            s_act(arow, vrow, AF.Ln, bias=CC[0:1, 208:209])
            s_act(arow, arow, AF.Exp, scale=-0.5)
            em.wait_now("vector", chS)
            v_op(lambda v: v.tensor_mul(brow, mrow, arow))
            v_op(lambda v: v.tensor_scalar_mul(brow, brow, -1.0))
            em.bank_acquire(2)
            em.bank_acquire(3)
            em.wait_now("tensor", chV)
            t_mm(PS[2][:, 0:512], ONESROWF[:], arow, True, True,
                 apl=[em.mark_ready(2, chT)])
            t_mm(PS[3][:, 0:512], ONESROWF[:], brow, True, True,
                 apl=[em.mark_ready(3, chT)])
            em.wait_ready("vector", 2)
            em.wait_ready("vector", 3)
            t1 = SC2[:, 0:512]
            for dt in range(NDT):
                v_op(lambda v, dt=dt: v.tensor_mul(t1, Rt(dt, c),
                                                   PS[2][:, 0:512]))
                v_op(lambda v, dt=dt: v.tensor_add(dest_tile_fn(dt), t1,
                                                   PS[3][:, 0:512]))
                if dt == NDT - 1:
                    em.free_after(2, "vector")
                    em.free_after(3, "vector")

        def phase_attn_a(b):
            for c in range(NCH):
                ln_chunk(c, lambda dt, c=c: A1[:, dt * 2048 + c * TC:
                                               dt * 2048 + c * TC + TC])
            em.barrier(("tensor", "vector", "scalar", "sync"))

            VT = [W2[:, 0:1040], W2[:, 1040:2080]]
            PT = [W2[:, 2080 + i * 512: 2080 + (i + 1) * 512]
                  for i in range(4)]
            for vb in range(2):
                for kt in range(16):
                    v_op(lambda v, vb=vb, kt=kt: v.memset(
                        VT[vb][:, kt * 65 + 64: kt * 65 + 65], 1.0))
            wslab = [W1[:, 0:3072], W1[:, 4096:4096 + 3072]]
            slab_owner = [0, 0]
            wq_tok = {}

            def wq_load(s):
                slot = s % 2
                if slab_owner[slot]:
                    em.wait_val("sync", chT, slab_owner[slot])
                dma("sync",
                    wslab[slot].rearrange("p (kt c) -> p kt c", kt=8),
                    dstg["wqkv_b"][s].rearrange("kt p c -> p kt c"), chW)
                wq_tok[s] = chW.n

            vt_done = {}

            def tr_gen(h):
                s_, par = h // 2, h % 2
                rows = slice(par * 64, par * 64 + 64)
                vtb = VT[h % 2]
                for kt in range(16):
                    em.bank_acquire("ptb")
                    t_tr(PTB[:, 0:64],
                         A3[rows, 2 * 2048 + kt * 128:
                            2 * 2048 + kt * 128 + 128],
                         IDB[rows, par * 64: par * 64 + 64],
                         apl=[em.mark_ready("ptb", chT)])
                    em.wait_ready("vector", "ptb")
                    v_op(lambda v, kt=kt, vtb=vtb: v.tensor_copy(
                        vtb[:, kt * 65: kt * 65 + 64], PTB[:, 0:64]))
                    em.free_after("ptb", "vector")
                    yield
                vt_done[h] = chV.n

            def drain(gen):
                if gen is not None:
                    for _ in gen:
                        pass

            wq_load(0)
            pend = None
            pt_guard = [0, 0, 0, 0]
            for s in range(8):
                slot = s % 2
                if s + 1 < 8:
                    wq_load(s + 1)
                em.wait_val("tensor", chW, wq_tok[s])
                qkv_banks = (2, 3, 4, 5)
                bi = 0
                for mt in (2, 0, 1):
                    for c in range(NCH):
                        if mt == 0 and c == 0:
                            em.wait_now("tensor", chS)
                            pend = tr_gen(2 * s)
                        bank = qkv_banks[bi % 4]
                        bi += 1
                        em.bank_acquire(bank)
                        for kt in range(8):
                            last = kt == 7
                            apl = ([em.mark_ready(bank, chT)]
                                   if last else ())
                            t_mm(PS[bank][:, 0:512],
                                 wslab[slot][:, kt * 384 + mt * 128:
                                             kt * 384 + mt * 128 + 128],
                                 A1[:, kt * 2048 + c * TC:
                                    kt * 2048 + c * TC + TC],
                                 start=(kt == 0), stop=last, apl=apl)
                        if pend is not None and mt != 2:
                            next(pend, None)
                            next(pend, None)
                        em.wait_ready("scalar", bank)
                        dst = A3[:, mt * 2048 + c * TC:
                                 mt * 2048 + c * TC + TC]
                        s_act(dst, PS[bank][:, 0:512], AF.Copy)
                        em.free_after(bank, "scalar")
                slab_owner[slot] = chT.n
                drain(pend)
                pend = None
                for par in range(2):
                    h = 2 * s + par
                    if par == 0:
                        em.wait_now("tensor", chS)
                        pend = tr_gen(2 * s + 1)
                    else:
                        drain(pend)
                        pend = None
                    em.wait_val("tensor", chV, vt_done[h])
                    rows = slice(par * 64, par * 64 + 64)
                    vtb = VT[h % 2]
                    for qc in range(NCH):
                        nkt = 4 * (qc + 1)
                        em.bank_acquire(4)
                        exp_tok = {}

                        def emit_pv(kt, qc=qc, nkt=nkt, vtb=vtb):
                            cn, vv = exp_tok[kt]
                            em.wait_val("tensor", em.ch[cn], vv)
                            if kt == nkt - 1:
                                apl = [em.mark_ready(4, chT)]
                            else:
                                apl = [em.mk_inc(chT, 1)]
                            t_mm(PS[4][0:65, 0:512],
                                 vtb[:, kt * 65: kt * 65 + 65], PT[kt % 4],
                                 start=(kt == 0), stop=(kt == nkt - 1),
                                 apl=apl)
                            pt_guard[kt % 4] = chT.n

                        for kt in range(nkt):
                            sbank = 2 + kt % 2
                            em.bank_acquire(sbank)
                            t_mm(PS[sbank][:, 0:512],
                                 A3[rows, 1 * 2048 + kt * 128:
                                    1 * 2048 + kt * 128 + 128],
                                 A3[rows, 0 * 2048 + qc * TC:
                                    0 * 2048 + qc * TC + TC],
                                 start=True, stop=True,
                                 apl=[em.mark_ready(sbank, chT)])
                            if kt >= 2:
                                emit_pv(kt - 2)
                            em.wait_ready("scalar", sbank)
                            pt = PT[kt % 4]
                            if pt_guard[kt % 4]:
                                em.wait_val("scalar", chT, pt_guard[kt % 4])
                            s_act(pt, PS[sbank][:, 0:512], AF.Exp,
                                  scale=0.125)
                            em.free_after(sbank, "scalar")
                            dd = kt - 4 * qc
                            if dd >= 0:
                                off = 384 - 128 * dd
                                em.wait_now("vector", chS)
                                v_op(lambda v, pt=pt, off=off: v.tensor_mul(
                                    pt, pt, STAIRB[:, off:off + 512]))
                                exp_tok[kt] = ("v", chV.n)
                            else:
                                exp_tok[kt] = ("s", chS.n)
                            if pend is not None:
                                next(pend, None)
                        emit_pv(nkt - 2)
                        emit_pv(nkt - 1)
                        em.wait_ready("scalar", 4)
                        odst = A2[rows, s * 2048 + qc * TC:
                                  s * 2048 + qc * TC + TC]
                        s_act(odst, PS[4][0:64, 0:512], AF.Copy)
                        drow = SCF[0:1, par * 2048 + qc * TC:
                                   par * 2048 + qc * TC + TC]
                        s_act(drow, PS[4][64:65, 0:512], AF.Copy)
                        em.free_after(4, "scalar")
                em.wait_now("tensor", chS)
                for qc in range(NCH):
                    em.bank_acquire(5)
                    t_mm(PS[5][:, 0:512], ONESLO[:],
                         SCF[0:1, qc * TC: qc * TC + TC],
                         start=True, stop=False)
                    t_mm(PS[5][:, 0:512], ONESHI[:],
                         SCF[0:1, 2048 + qc * TC: 2048 + qc * TC + TC],
                         start=False, stop=True,
                         apl=[em.mark_ready(5, chT)])
                    em.wait_ready("scalar", 5)
                    rec = SC2[:, 0:512]
                    em.wait_now("scalar", chV)
                    s_act(rec, PS[5][:, 0:512], AF.Ln)
                    em.free_after(5, "scalar")
                    s_act(rec, rec, AF.Exp, scale=-1.0)
                    em.wait_now("vector", chS)
                    ot = A2[:, s * 2048 + qc * TC: s * 2048 + qc * TC + TC]
                    v_op(lambda v, ot=ot: v.tensor_mul(ot, ot, rec))

            em.barrier(("tensor", "vector", "scalar", "sync"))
            wslot = [W1[:, 0:1024], W1[:, 4096:4096 + 1024]]
            owner = [0, 0]
            wo_tok = {}

            def wo_load(et):
                sl = et % 2
                if owner[sl]:
                    em.wait_val("sync", chT, owner[sl])
                dma("sync",
                    wslot[sl].rearrange("p (kt c) -> p kt c", kt=8),
                    dstg["wout_b"][et].rearrange("kt p c -> p kt c"), chW)
                wo_tok[et] = chW.n

            wo_load(0)
            for et in range(8):
                sl = et % 2
                if et + 1 < 8:
                    wo_load(et + 1)
                em.wait_val("tensor", chW, wo_tok[et])
                for c in range(NCH):
                    bank = 4 + c % 2
                    em.bank_acquire(bank)
                    for kt in range(8):
                        last = kt == 7
                        apl = [em.mark_ready(bank, chT)] if last else ()
                        t_mm(PS[bank][:, 0:512],
                             wslot[sl][:, kt * 128: kt * 128 + 128],
                             A2[:, kt * 2048 + c * TC:
                                kt * 2048 + c * TC + TC],
                             start=(kt == 0), stop=last, apl=apl)
                    em.wait_ready("vector", bank)
                    v_op(lambda v, et=et, c=c, bank=bank: v.tensor_add(
                        Rt(et, c), Rt(et, c), PS[bank][:, 0:512]))
                    em.free_after(bank, "vector")
                owner[sl] = chT.n
            em.barrier()


        def phase_ffn(b):
            eslot_guard = [0, 0, 0]
            w2hi_guard = [0]
            for c in range(NCH):
                ln_chunk(c, lambda dt: A3[:, dt * 512: dt * 512 + 512])
                em.bank_acquire(0)
                for kt in range(8):
                    apl = [em.mark_ready(0, chT)] if kt == 7 else ()
                    t_mm(PS[0][0:16, 0:512], GWT[:, kt * 16: kt * 16 + 16],
                         Rt(kt, c), start=(kt == 0), stop=(kt == 7), apl=apl)
                em.wait_ready("scalar", 0)
                GPS = SCF[0:18, 3072:3584]
                s_act(GPS[0:16, :], PS[0][0:16, 0:512], AF.Copy)
                em.free_after(0, "scalar")
                arow = SCF[0:1, 0:512]
                brow = SCF[0:1, 512:1024]
                em.wait_now("gpsimd", chV)
                em.wait_now("gpsimd", chS)
                dma("gpsimd", SCF[16:17, 3072:3584], arow, chG)
                dma("gpsimd", SCF[17:18, 3072:3584], brow, chG)
                em.wait_now("tensor", chG)

                def vsync():
                    ap = em.mk_inc(chV, 1)
                    em.on("vector", lambda v, ap=ap: ap(v.engine_nop()))
                    em.wait_now("vector", chV)

                WGS = []
                for j in range(4):
                    base = 2240 + j * 192
                    W = {}
                    o = SCF[:, base:base + 192]
                    W["gt"] = o[:, 0:18]
                    W["t0"] = o[:, 18:34]
                    W["sc"] = o[:, 34:50]
                    W["p0"] = o[:, 50:66]
                    W["s2"] = o[:, 66:82]
                    W["gm"] = o[:, 82:90]
                    W["gs"] = o[:, 90:94]
                    W["keep"] = o[:, 94:98]
                    W["ke"] = o[:, 98:114]
                    W["s2k"] = o[:, 114:130]
                    W["e1"] = o[:, 130:146]
                    W["e2"] = o[:, 146:162]
                    W["eqg"] = o[:, 162:166]
                    W["tmp"] = o[:, 166:182]
                    W["nmx"] = o[:, 182:183]
                    W["sume"] = o[:, 183:184]
                    W["rcp"] = o[:, 184:185]
                    W["r1"] = o[:, 185:186]
                    W["r2"] = o[:, 186:187]
                    W["m1"] = o[:, 187:188]
                    W["m2"] = o[:, 188:189]
                    WGS.append(W)
                for j in range(4):
                    em.bank_acquire("ptf")
                    em.wait_now("tensor", chS)
                    t_tr(PTF[:, 0:18], GPS[0:18, j * 128: j * 128 + 128],
                         IDF[0:18, 0:18], apl=[em.mark_ready("ptf", chT)])
                    em.wait_ready("vector", "ptf")
                    v_op(lambda v, j=j: v.tensor_copy(WGS[j]["gt"],
                                                      PTF[:, 0:18]))
                    em.free_after("ptf", "vector")
                vsync()
                for j in range(4):
                    W = WGS[j]
                    v_op(lambda v, W=W: v.tensor_scalar(
                        W["t0"], W["gt"][:, 0:16], W["gt"][:, 16:17], None,
                        op0=ALU.mult))
                vsync()
                for j in range(4):
                    W = WGS[j]
                    v_op(lambda v, W=W: v.scalar_tensor_tensor(
                        W["sc"], CS[:], W["gt"][:, 17:18], W["t0"],
                        op0=ALU.mult, op1=ALU.add))
                vsync()
                for j in range(4):
                    W = WGS[j]
                    v_op(lambda v, W=W: v.reduce_max(W["nmx"], W["sc"],
                                                     axis=AX.X, negate=True))
                vsync()
                em.wait_now("scalar", chV)
                for j in range(4):
                    W = WGS[j]
                    s_act(W["p0"], W["sc"], AF.Exp, bias=W["nmx"],
                          accum=W["sume"])
                for j in range(4):
                    W = WGS[j]
                    s_act(W["rcp"], W["sume"], AF.Ln)
                for j in range(4):
                    W = WGS[j]
                    s_act(W["rcp"], W["rcp"], AF.Exp, scale=-1.0)
                em.wait_now("vector", chS)
                for j in range(4):
                    W = WGS[j]
                    v_op(lambda v, W=W: v.tensor_scalar(
                        W["p0"], W["p0"], W["rcp"], None, op0=ALU.mult))
                vsync()
                for j in range(4):
                    W = WGS[j]
                    v_op(lambda v, W=W: v.tensor_add(W["s2"], W["p0"], GB[:]))
                vsync()
                for gi in range(4):
                    for j in range(4):
                        W = WGS[j]
                        v_op(lambda v, W=W, gi=gi: v.reduce_max(
                            W["gm"][:, gi:gi + 1],
                            W["s2"][:, 4 * gi:4 * gi + 4], axis=AX.X))
                vsync()
                for gi in range(4):
                    for j in range(4):
                        W = WGS[j]
                        v_op(lambda v, W=W, gi=gi: v.tensor_scalar(
                            W["eqg"], W["s2"][:, 4 * gi:4 * gi + 4],
                            W["gm"][:, gi:gi + 1], None, op0=ALU.is_equal))
                    vsync()
                    for j in range(4):
                        W = WGS[j]
                        v_op(lambda v, W=W, gi=gi: v.scalar_tensor_tensor(
                            W["tmp"][:, 0:4], W["eqg"], -1e30,
                            W["s2"][:, 4 * gi:4 * gi + 4],
                            op0=ALU.mult, op1=ALU.add))
                    vsync()
                    for j in range(4):
                        W = WGS[j]
                        v_op(lambda v, W=W, gi=gi: v.reduce_max(
                            W["gm"][:, 4 + gi:5 + gi], W["tmp"][:, 0:4],
                            axis=AX.X))
                    vsync()
                for j in range(4):
                    W = WGS[j]
                    v_op(lambda v, W=W: v.tensor_add(
                        W["gs"], W["gm"][:, 0:4], W["gm"][:, 4:8]))
                vsync()
                for j in range(4):
                    W = WGS[j]
                    v_op(lambda v, W=W: v.reduce_max(W["r1"], W["gs"],
                                                     axis=AX.X))
                vsync()
                for j in range(4):
                    W = WGS[j]
                    v_op(lambda v, W=W: v.tensor_scalar(
                        W["eqg"], W["gs"], W["r1"], None, op0=ALU.is_equal))
                vsync()
                for j in range(4):
                    W = WGS[j]
                    v_op(lambda v, W=W: v.scalar_tensor_tensor(
                        W["tmp"][:, 0:4], W["eqg"], -1e30, W["gs"],
                        op0=ALU.mult, op1=ALU.add))
                vsync()
                for j in range(4):
                    W = WGS[j]
                    v_op(lambda v, W=W: v.reduce_max(W["r2"],
                                                     W["tmp"][:, 0:4],
                                                     axis=AX.X))
                vsync()
                for j in range(4):
                    W = WGS[j]
                    v_op(lambda v, W=W: v.tensor_scalar(
                        W["keep"], W["gs"], W["r2"], None, op0=ALU.is_ge))
                vsync()
                for j in range(4):
                    W = WGS[j]
                    v_op(lambda v, W=W: v.memset(W["ke"], 1.0))
                vsync()
                for gi in range(4):
                    for j in range(4):
                        W = WGS[j]
                        v_op(lambda v, W=W, gi=gi: v.tensor_scalar(
                            W["ke"][:, 4 * gi:4 * gi + 4],
                            W["ke"][:, 4 * gi:4 * gi + 4],
                            W["keep"][:, gi:gi + 1], None, op0=ALU.mult))
                    vsync()
                for j in range(4):
                    W = WGS[j]
                    v_op(lambda v, W=W: v.tensor_scalar(
                        W["ke"], W["ke"], 1e30, -1e30,
                        op0=ALU.mult, op1=ALU.add))
                vsync()
                for j in range(4):
                    W = WGS[j]
                    v_op(lambda v, W=W: v.tensor_add(W["s2k"], W["s2"],
                                                     W["ke"]))
                vsync()
                for j in range(4):
                    W = WGS[j]
                    v_op(lambda v, W=W: v.reduce_max(W["m1"], W["s2k"],
                                                     axis=AX.X))
                vsync()
                for j in range(4):
                    W = WGS[j]
                    v_op(lambda v, W=W: v.tensor_scalar(
                        W["e1"], W["s2k"], W["m1"], None, op0=ALU.is_equal))
                vsync()
                for j in range(4):
                    W = WGS[j]
                    v_op(lambda v, W=W: v.scalar_tensor_tensor(
                        W["tmp"], W["e1"], -1e30, W["s2k"],
                        op0=ALU.mult, op1=ALU.add))
                vsync()
                for j in range(4):
                    W = WGS[j]
                    v_op(lambda v, W=W: v.reduce_max(W["m2"], W["tmp"],
                                                     axis=AX.X))
                vsync()
                for j in range(4):
                    W = WGS[j]
                    v_op(lambda v, W=W: v.tensor_scalar(
                        W["e2"], W["tmp"], W["m2"], None, op0=ALU.is_equal))
                vsync()
                for j in range(4):
                    W = WGS[j]
                    v_op(lambda v, W=W: v.tensor_add(W["e1"], W["e1"],
                                                     W["e2"]))
                vsync()
                for j in range(4):
                    W = WGS[j]
                    v_op(lambda v, W=W, j=j: v.tensor_mul(
                        CT[:, j * 16: j * 16 + 16], W["p0"], W["e1"]))
                vsync()
                em.wait_now("tensor", chV)
                for j in range(4):
                    em.bank_acquire("ptf")
                    t_tr(PTF[0:16, 0:128], CT[:, j * 16: j * 16 + 16],
                         IDF[:, 0:128], apl=[em.mark_ready("ptf", chT)])
                    em.wait_ready("scalar", "ptf")
                    s_act(CTROW[0:16, j * 128: j * 128 + 128],
                          PTF[0:16, 0:128], AF.Copy)
                    em.free_after("ptf", "scalar")
                em.wait_now("vector", chS)
                v_op(lambda v: v.tensor_copy(CTB[:], CTROW[:]))

                shtok = {}

                def sh_load(it):
                    sl = it % 2
                    base = 4096 + sl * 1024
                    if it < 2:
                        if w2hi_guard[0]:
                            em.wait_val("sync", chT, w2hi_guard[0])
                    elif shguard[sl]:
                        em.wait_val("sync", chT, shguard[sl])
                    dma("sync", W2[:, base:base + 1024].rearrange(
                        "p (kt c) -> p kt c", kt=8),
                        dstg["ws1_b"][it].rearrange("kt p c -> p kt c"), chW)
                    dma("sync", W2[:, 6144 + sl * 1024: 7168 + sl * 1024]
                        .rearrange("p (kt c) -> p kt c", kt=8),
                        dstg["ws3_b"][it].rearrange("kt p c -> p kt c"), chW)
                    shtok[it] = chW.n

                shguard = [0, 0]
                sh_load(0)
                for it in range(8):
                    sl = it % 2
                    base = 4096 + sl * 1024
                    if it + 1 < 8:
                        sh_load(it + 1)
                    em.wait_val("tensor", chW, shtok[it])
                    b1, b3 = (0, 1) if it % 2 == 0 else (4, 5)
                    em.bank_acquire(b1)
                    em.bank_acquire(b3)
                    for kt in range(8):
                        apl = [em.mark_ready(b1, chT)] if kt == 7 else ()
                        t_mm(PS[b1][:, 0:512],
                             W2[:, base + kt * 128: base + kt * 128 + 128],
                             A3[:, kt * 512: kt * 512 + 512],
                             start=(kt == 0), stop=(kt == 7), apl=apl)
                    for kt in range(8):
                        apl = [em.mark_ready(b3, chT)] if kt == 7 else ()
                        t_mm(PS[b3][:, 0:512],
                             W2[:, 6144 + sl * 1024 + kt * 128:
                                6144 + sl * 1024 + kt * 128 + 128],
                             A3[:, kt * 512: kt * 512 + 512],
                             start=(kt == 0), stop=(kt == 7), apl=apl)
                    shguard[sl] = chT.n
                    em.wait_ready("scalar", b1)
                    sa = SCB[:, (it % 2) * 1024: (it % 2) * 1024 + 512]
                    if scb_ffn_guard[it % 2]:
                        em.wait_val("scalar", chV, scb_ffn_guard[it % 2])
                    if scb_guard[it % 2]:
                        em.wait_val("scalar", chT, scb_guard[it % 2])
                    s_act(sa, PS[b1][:, 0:512], AF.Silu,
                          bias=CC[:, 56 + it:57 + it])
                    em.free_after(b1, "scalar")
                    em.wait_ready("vector", b3)
                    em.wait_now("vector", chS)
                    v_op(lambda v, it=it, sa=sa, b3=b3: v.scalar_tensor_tensor(
                        W2[:, it * 512: it * 512 + 512], PS[b3][:, 0:512],
                        CC[:, 72 + it:73 + it], sa,
                        op0=ALU.add, op1=ALU.mult))
                    em.free_after(b3, "vector")
                    scb_ffn_guard[it % 2] = chV.n
                eslot_guard[2] = max(eslot_guard[2], chT.n)

                slots = [W1[:, 0:4096], W1[:, 4096:8192], W2[:, 4096:8192]]
                etok = {}

                def e_load(li, nm, e2):
                    if eslot_guard[li % 3]:
                        em.wait_val("sync", chT, eslot_guard[li % 3])
                    dma("sync", slots[li % 3].rearrange(
                        "p (kt c) -> p kt c", kt=8),
                        dstg[nm][e2].rearrange("kt p c -> p kt c"), chW)
                    etok[li] = chW.n

                em.wait_now("tensor", chV)
                e_load(0, "we1_b", 0)
                e_load(1, "we3_b", 0)
                h1banks = (0, 1, 4, 5)
                h3banks = (2, 3)
                for e in range(NE):
                    if e + 1 < NE:
                        e_load(2 * (e + 1), "we1_b", e + 1)
                    em.bank_acquire(2)
                    t_mm(PS[2][:, 0:512],
                         EXHB[:, e * 128: e * 128 + 128],
                         CTB[:, 0:512], start=True, stop=True,
                         apl=[em.mark_ready(2, chT)])
                    em.wait_ready("scalar", 2)
                    bcb = A3[:, 4096:4608]
                    em.wait_now("scalar", chV)
                    s_act(bcb, PS[2][:, 0:512], AF.Copy)
                    em.free_after(2, "scalar")
                    s1 = slots[(2 * e) % 3]
                    s3 = slots[(2 * e + 1) % 3]
                    em.wait_val("tensor", chW, etok[2 * e])
                    for it in range(4):
                        b1 = h1banks[it]
                        em.bank_acquire(b1)
                        for kt in range(8):
                            apl = [em.mark_ready(b1, chT)] if kt == 7 else ()
                            t_mm(PS[b1][:, 0:512],
                                 s1[:, kt * 512 + it * 128:
                                    kt * 512 + it * 128 + 128],
                                 A3[:, kt * 512: kt * 512 + 512],
                                 start=(kt == 0), stop=(kt == 7), apl=apl)
                    eslot_guard[(2 * e) % 3] = chT.n
                    if e + 1 < NE:
                        e_load(2 * (e + 1) + 1, "we3_b", e + 1)
                    em.wait_val("tensor", chW, etok[2 * e + 1])
                    for it in range(4):
                        b1 = h1banks[it]
                        b3 = h3banks[it % 2]
                        em.bank_acquire(b3)
                        for kt in range(8):
                            apl = [em.mark_ready(b3, chT)] if kt == 7 else ()
                            t_mm(PS[b3][:, 0:512],
                                 s3[:, kt * 512 + it * 128:
                                    kt * 512 + it * 128 + 128],
                                 A3[:, kt * 512: kt * 512 + 512],
                                 start=(kt == 0), stop=(kt == 7), apl=apl)
                        par2 = it % 2
                        sa = SCB[:, par2 * 1024: par2 * 1024 + 512]
                        sb = SCB[:, par2 * 1024 + 512: par2 * 1024 + 1024]
                        em.wait_ready("scalar", b1)
                        if scb_ffn_guard[par2]:
                            em.wait_val("scalar", chV, scb_ffn_guard[par2])
                        if scb_guard[par2]:
                            em.wait_val("scalar", chT, scb_guard[par2])
                        ei = e * 4 + it
                        s_act(sa, PS[b1][:, 0:512], AF.Silu,
                              bias=CC[:, 80 + ei:81 + ei])
                        em.free_after(b1, "scalar")
                        em.wait_ready("scalar", b3)
                        s_act(sb, PS[b3][:, 0:512], AF.Identity,
                              bias=CC[:, 144 + ei:145 + ei])
                        em.free_after(b3, "scalar")
                        em.wait_now("vector", chS)
                        n = e * 4 + it
                        if n < 32:
                            chdst = A1[:, n * 512: n * 512 + 512]
                        else:
                            chdst = A2[:, (n - 32) * 512: (n - 32) * 512 + 512]
                        v_op(lambda v, sa=sa, bcb=bcb: v.tensor_mul(
                            sa, sa, bcb))
                        v_op(lambda v, chdst=chdst, sa=sa, sb=sb:
                             v.tensor_mul(chdst, sa, sb))
                        scb_ffn_guard[par2] = chV.n
                    eslot_guard[(2 * e + 1) % 3] = chT.n

                em.wait_now("tensor", chV)
                ybuf = [W2[:, 4096:8192], W1[:, 2048:6144]]
                ybuf_guard = [eslot_guard[2],
                              max(eslot_guard[0], eslot_guard[1])]
                ws2_guard = [max(eslot_guard[0], eslot_guard[1]), 0]
                ytok = {}
                wtok = {}

                def y_load(s):
                    dt2, half = s // 2, s % 2
                    if ybuf_guard[s % 2]:
                        em.wait_val("sync", chT, ybuf_guard[s % 2])
                    dma("sync",
                        ybuf[s % 2].rearrange(
                            "p (e it c) -> p e it c", e=8, it=4),
                        dstg["we2_b"][dt2, 8 * half:8 * half + 8]
                        .rearrange("e it p c -> p e it c"), chW)
                    ytok[s] = chW.n

                def ws2_load(dt2):
                    sl = dt2 % 2
                    if ws2_guard[sl]:
                        em.wait_val("sync", chT, ws2_guard[sl])
                    dma("sync", W1[:, sl * 1024: sl * 1024 + 1024]
                        .rearrange("p (it c) -> p it c", it=8),
                        dstg["ws2_b"][dt2].rearrange("it p c -> p it c"), chW)
                    wtok[dt2] = chW.n

                y_load(0)
                ws2_load(0)
                y_load(1)
                for dt in range(8):
                    em.bank_acquire(2)
                    for s in (2 * dt, 2 * dt + 1):
                        if s % 2 == 1 and dt + 1 < 8:
                            ws2_load(dt + 1)
                        em.wait_val("tensor", chW, ytok[s])
                        for n0 in range(32):
                            n = (s % 2) * 32 + n0
                            if n < 32:
                                chsrc = A1[:, n * 512: n * 512 + 512]
                            else:
                                chsrc = A2[:, (n - 32) * 512:
                                           (n - 32) * 512 + 512]
                            apl = [em.mk_inc(chT, 1)] if n0 == 31 else ()
                            t_mm(PS[2][:, 0:512],
                                 ybuf[s % 2][:, n0 * 128: n0 * 128 + 128],
                                 chsrc, start=(s % 2 == 0 and n0 == 0),
                                 stop=False, apl=apl)
                        ybuf_guard[s % 2] = chT.n
                        if s + 2 < 16:
                            y_load(s + 2)
                    t_mm(PS[2][:, 0:512], BE2[:, dt * 128: dt * 128 + 128],
                         CTB[:, 0:512], start=False, stop=False)
                    em.wait_val("tensor", chW, wtok[dt])
                    for it in range(8):
                        last = it == 7
                        apl = [em.mark_ready(2, chT)] if last else ()
                        t_mm(PS[2][:, 0:512],
                             W1[:, (dt % 2) * 1024 + it * 128:
                                (dt % 2) * 1024 + it * 128 + 128],
                             W2[:, it * 512: it * 512 + 512],
                             start=False, stop=last, apl=apl)
                    ws2_guard[dt % 2] = chT.n
                    em.wait_ready("vector", 2)
                    v_op(lambda v, dt=dt, c=c: v.scalar_tensor_tensor(
                        Rt(dt, c), PS[2][:, 0:512], CC[:, 64 + dt:65 + dt],
                        Rt(dt, c), op0=ALU.add, op1=ALU.add))
                    em.free_after(2, "vector")
                eslot_guard[0] = chT.n
                eslot_guard[1] = chT.n
                eslot_guard[2] = chT.n
                w2hi_guard[0] = chT.n
            em.barrier(("tensor", "vector", "scalar", "sync"))

        def phase_attn_b(b):
            TB = 256
            NCB = T // TB

            def vsync():
                ap = em.mk_inc(chV, 1)
                em.on("vector", lambda v, ap=ap: ap(v.engine_nop()))
                em.wait_now("vector", chV)

            sdummy = SCF[0:1, 2238:2239]

            def ssync():
                ap = em.mk_inc(chS, 1)
                em.on("scalar", lambda s, ap=ap: ap(
                    s.activation(out=sdummy, in_=sdummy, func=AF.Copy)))
                em.wait_now("scalar", chS)

            v_op(lambda v: v.memset(CAB[:], 0.0))
            v_op(lambda v: v.memset(SS[:], 0.0))
            v_op(lambda v: v.memset(SQ[:], 0.0))
            vsync()
            for c in range(NCH):
                ln_chunk(c, lambda dt, c=c: A1[:, dt * 2048 + c * TC:
                                               dt * 2048 + c * TC + TC])
            em.barrier(("tensor", "vector", "scalar", "sync", "gpsimd"))

            EK = SCF[:, 0:256]
            A_ = SCF[:, 256:512]
            SA = SCF[:, 512:800]
            SB = SCF[:, 800:1088]
            WREP = SCF[:, 1088:1344]
            NN = SCF[:, 1344:1600]
            DD = SCF[:, 1600:1856]
            RR = SCF[:, 1856:2112]
            ss_tmp = SCF[:, 2232:2233]
            sq_tmp = SCF[:, 2233:2234]

            wslot_guard = [0, 0]
            for cb in range(NCB):
                em.wait_now("vector", chT)
                for mi, moff in ((0, 0), (1, 2048), (2, 4096)):
                    for dt in range(NDT):
                        xt = A1[:, dt * 2048 + cb * TB:
                                dt * 2048 + cb * TB + TB]
                        dst = A3[:, moff + dt * TB: moff + dt * TB + TB]
                        v_op(lambda v, xt=xt, dst=dst, mi=mi, dt=dt:
                             v.tensor_scalar(dst, xt,
                                             CC[:, 8 * mi + dt:8 * mi + dt
                                                + 1], None, op0=ALU.mult))
                for mi, moff in ((0, 0), (1, 2048), (2, 4096)):
                    for dt in range(NDT):
                        dst = A3[:, moff + dt * TB: moff + dt * TB + TB]
                        if cb == 0:
                            xp = A1[:, dt * 2048: dt * 2048 + TB - 1]
                            v_op(lambda v, xp=xp, dst=dst, mi=mi, dt=dt:
                                 v.scalar_tensor_tensor(
                                     dst[:, 1:TB], xp,
                                     DC[:, 24 + 8 * mi + dt:
                                        25 + 8 * mi + dt],
                                     dst[:, 1:TB],
                                     op0=ALU.mult, op1=ALU.add))
                        else:
                            xp = A1[:, dt * 2048 + cb * TB - 1:
                                    dt * 2048 + cb * TB + TB - 1]
                            v_op(lambda v, xp=xp, dst=dst, mi=mi, dt=dt:
                                 v.scalar_tensor_tensor(
                                     dst, xp,
                                     DC[:, 24 + 8 * mi + dt:
                                        25 + 8 * mi + dt],
                                     dst, op0=ALU.mult, op1=ALU.add))
                vsync()
                for g in range(8):
                    sl = g % 2
                    base = sl * 3072
                    if wslot_guard[sl]:
                        em.wait_val("sync", chT, wslot_guard[sl])
                    for wi, nm in ((0, "wr_b"), (1, "wk_b"), (2, "wv_b")):
                        dma("sync", W1[:, base + wi * 1024:
                                       base + wi * 1024 + 1024].rearrange(
                                           "p (kt c) -> p kt c", kt=8),
                            dstg[nm][g].rearrange("kt p c -> p kt c"), chW)
                    em.wait_now("tensor", chW)
                    em.wait_now("tensor", chV)
                    for bank, wi, src_off in ((3, 0, 0), (4, 1, 2048),
                                              (5, 2, 4096)):
                        em.bank_acquire(bank)
                        for kt in range(8):
                            apl = ([em.mark_ready(bank, chT)]
                                   if kt == 7 else ())
                            t_mm(PS[bank][:, 0:TB],
                                 W1[:, base + wi * 1024 + kt * 128:
                                    base + wi * 1024 + kt * 128 + 128],
                                 A3[:, src_off + kt * TB:
                                    src_off + kt * TB + TB],
                                 start=(kt == 0), stop=(kt == 7), apl=apl)
                    wslot_guard[sl] = chT.n
                    em.wait_ready("scalar", 4)
                    em.wait_now("scalar", chV)
                    s_act(EK, PS[4][:, 0:TB], AF.Exp)
                    em.free_after(4, "scalar")
                    em.wait_ready("scalar", 3)
                    s_act(RR, PS[3][:, 0:TB], AF.Sigmoid)
                    em.free_after(3, "scalar")
                    em.wait_ready("vector", 5)
                    em.wait_now("vector", chS)
                    v_op(lambda v: v.memset(WREP, 1.0))
                    v_op(lambda v, g=g: v.tensor_mul(A_, EK, PS[5][:, 0:TB]))
                    em.free_after(5, "vector")
                    v_op(lambda v, g=g: v.tensor_scalar(
                        WREP, WREP, DC[:, g:g + 1], None, op0=ALU.mult))
                    vsync()
                    v_op(lambda v, g=g: v.tensor_tensor_scan(
                        SA[:, 1:257], WREP, A_, CAB[:, g:g + 1],
                        op0=ALU.mult, op1=ALU.add))
                    v_op(lambda v, g=g: v.tensor_tensor_scan(
                        SB[:, 1:257], WREP, EK, CAB[:, 8 + g:9 + g],
                        op0=ALU.mult, op1=ALU.add))
                    v_op(lambda v, g=g: v.tensor_copy(SA[:, 0:1],
                                                      CAB[:, g:g + 1]))
                    v_op(lambda v, g=g: v.tensor_copy(SB[:, 0:1],
                                                      CAB[:, 8 + g:9 + g]))
                    vsync()
                    v_op(lambda v, g=g: v.tensor_copy(CAB[:, g:g + 1],
                                                      SA[:, 256:257]))
                    v_op(lambda v, g=g: v.tensor_copy(CAB[:, 8 + g:9 + g],
                                                      SB[:, 256:257]))
                    v_op(lambda v, g=g: v.tensor_scalar(
                        EK, SA[:, 1:257], DC[:, 8 + g:9 + g], None,
                        op0=ALU.mult))
                    v_op(lambda v, g=g: v.tensor_scalar(
                        A_, SB[:, 1:257], DC[:, 8 + g:9 + g], 1e-9,
                        op0=ALU.mult, op1=ALU.add))
                    vsync()
                    v_op(lambda v, g=g: v.scalar_tensor_tensor(
                        NN, SA[:, 0:256], DC[:, 16 + g:17 + g], EK,
                        op0=ALU.mult, op1=ALU.add))
                    v_op(lambda v, g=g: v.scalar_tensor_tensor(
                        DD, SB[:, 0:256], DC[:, 16 + g:17 + g], A_,
                        op0=ALU.mult, op1=ALU.add))
                    vsync()
                    em.wait_now("scalar", chV)
                    s_act(DD, DD, AF.Ln)
                    ssync()
                    s_act(DD, DD, AF.Exp, scale=-1.0)
                    em.wait_now("vector", chS)
                    v_op(lambda v: v.tensor_mul(NN, NN, DD))
                    vsync()
                    v_op(lambda v: v.tensor_mul(A_, NN, RR))
                    vsync()
                    odst = A2[:, g * 2048 + cb * TB: g * 2048 + cb * TB + TB]
                    v_op(lambda v, odst=odst: v.tensor_scalar(
                        odst, A_, 1.0, None, op0=ALU.mult, op1=ALU.add,
                        accum_out=ss_tmp))
                    v_op(lambda v: v.scalar_tensor_tensor(
                        DD, A_, 1.0, A_, op0=ALU.mult, op1=ALU.mult,
                        accum_out=sq_tmp))
                    vsync()
                    v_op(lambda v, g=g: v.tensor_add(
                        SS[:, g:g + 1], SS[:, g:g + 1], ss_tmp))
                    v_op(lambda v, g=g: v.tensor_add(
                        SQ[:, g:g + 1], SQ[:, g:g + 1], sq_tmp))
                    vsync()

            em.barrier(("tensor", "vector", "scalar", "sync"))
            em.bank_acquire("ptf")
            t_mm(PTF[0:2, 0:8], HSEL[:], SS[:], start=True, stop=True)
            t_mm(PTF[0:2, 8:16], HSEL[:], SQ[:], start=True, stop=True,
                 apl=[em.mark_ready("ptf", chT)])
            em.wait_ready("scalar", "ptf")
            HSROW = SCF[0:2, 2240:2256]
            s_act(HSROW, PTF[0:2, 0:16], AF.Copy)
            em.free_after("ptf", "scalar")
            hm = SCF[0:2, 2256:2264]
            hv = SCF[0:2, 2264:2272]
            em.wait_now("vector", chS)
            NINV = 1.0 / (64.0 * 2048.0)
            v_op(lambda v: v.tensor_scalar(hm, HSROW[0:2, 0:8], NINV, None,
                                           op0=ALU.mult))
            v_op(lambda v: v.tensor_scalar(hv, HSROW[0:2, 8:16], NINV, None,
                                           op0=ALU.mult))
            vsync()
            hm2 = SCF[0:2, 2272:2280]
            v_op(lambda v: v.tensor_mul(hm2, hm, hm))
            vsync()
            v_op(lambda v: v.tensor_sub(hv, hv, hm2))
            vsync()
            em.wait_now("scalar", chV)
            s_act(hv, hv, AF.Ln, bias=CC[0:2, 208:209])
            ssync()
            s_act(hv, hv, AF.Exp, scale=-0.5)
            em.wait_now("tensor", chS)
            em.bank_acquire("ptf")
            t_mm(PTF[:, 0:8], HSELT[:], hm, start=True, stop=True)
            t_mm(PTF[:, 8:16], HSELT[:], hv, start=True, stop=True,
                 apl=[em.mark_ready("ptf", chT)])
            em.wait_ready("vector", "ptf")
            v_op(lambda v: v.tensor_mul(SCL[:], PTF[:, 8:16],
                                        CC[:, 40:48]))
            em.free_after("ptf", "vector")
            vsync()
            tb8 = SCF[:, 2280:2288]
            v_op(lambda v: v.tensor_mul(tb8, PTF[:, 0:8], SCL[:]))
            vsync()
            v_op(lambda v: v.tensor_sub(BIA[:], CC[:, 48:56], tb8))
            vsync()
            for g in range(8):
                for c in range(NCH):
                    src_t = A2[:, g * 2048 + c * TC: g * 2048 + c * TC + TC]
                    dst_t = A1[:, g * 2048 + c * TC: g * 2048 + c * TC + TC]
                    v_op(lambda v, src_t=src_t, dst_t=dst_t, g=g:
                         v.tensor_scalar(dst_t, src_t, SCL[:, g:g + 1],
                                         BIA[:, g:g + 1], op0=ALU.mult,
                                         op1=ALU.add))
            vsync()
            woguard = [0, 0]
            for et in range(8):
                sl = et % 2
                if woguard[sl]:
                    em.wait_val("sync", chT, woguard[sl])
                dma("sync", W2[:, sl * 1024: sl * 1024 + 1024].rearrange(
                    "p (kt c) -> p kt c", kt=8),
                    dstg["wo_b"][et].rearrange("kt p c -> p kt c"), chW)
                em.wait_now("tensor", chW)
                em.wait_now("tensor", chV)
                for c in range(NCH):
                    bank = 4 + c % 2
                    em.bank_acquire(bank)
                    for kt in range(8):
                        last = kt == 7
                        apl = [em.mark_ready(bank, chT)] if last else ()
                        t_mm(PS[bank][:, 0:512],
                             W2[:, sl * 1024 + kt * 128:
                                sl * 1024 + kt * 128 + 128],
                             A1[:, kt * 2048 + c * TC:
                                kt * 2048 + c * TC + TC],
                             start=(kt == 0), stop=last, apl=apl)
                    em.wait_ready("vector", bank)
                    v_op(lambda v, et=et, c=c, bank=bank: v.tensor_add(
                        Rt(et, c), Rt(et, c), PS[bank][:, 0:512]))
                    em.free_after(bank, "vector")
                woguard[sl] = chT.n
            em.barrier()

        for b in range(NB):
            dma("gpsimd", R[:].rearrange("p (dt t) -> p dt t", dt=8),
                x_in[b].rearrange("(dt p) t -> p dt t", p=128), chG)
            em.barrier()
            if "a" in phases:
                phase_attn_a(b)
            if "f1" in phases:
                phase_ffn(b)
            if "b" in phases:
                phase_attn_b(b)
            if "f2" in phases:
                phase_ffn(b)
            ap = em.mk_inc(chO, 16)
            dstp = y_out[b].rearrange("(dt p) t -> p dt t", p=128)
            em.on("sync", lambda e, d=dstp, ap=ap: ap(
                e.dma_start(out=d,
                            in_=R[:].rearrange("p (dt t) -> p dt t", dt=8))))
            em.barrier()

        for nm, out_ap in dump_outs.items():
            buf = {"R": R, "A1": A1, "A2": A2, "A3": A3, "SCF": SCF,
                   "CT": CT, "SS": SS, "SQ": SQ, "DC": DC}[nm]
            ap = em.mk_inc(chO, 16)
            em.on("sync", lambda e, bu=buf, o=out_ap, ap=ap: ap(
                e.dma_start(out=o, in_=bu[:])))
        em.wait_now("sync", chO)

        em.run(block)
    return nc



def _prep_shared_inputs(kw):
    f = np.float32
    out = {}
    wqkv = np.asarray(kw["L0_attn_a_w_qkv"], f)
    cols = []
    for s in range(8):
        for part in range(3):
            for hh in (2 * s, 2 * s + 1):
                base = part * 1024 + hh * 64
                cols.append(np.arange(base, base + 64))
    perm = np.concatenate(cols)
    wq = wqkv[:, perm].reshape(8, 128, 8, 384)
    out["wqkv"] = np.ascontiguousarray(wq.transpose(2, 0, 1, 3))

    def tile_oi(w):
        return np.ascontiguousarray(
            w.reshape(8, 128, 8, 128).transpose(2, 0, 1, 3))

    out["wout"] = tile_oi(np.asarray(kw["L0_attn_a_w_out"], f))
    out["wr"] = tile_oi(np.asarray(kw["L0_attn_b_W_r"], f).T)
    out["wk"] = tile_oi(np.asarray(kw["L0_attn_b_W_k"], f).T)
    out["wv"] = tile_oi(np.asarray(kw["L0_attn_b_W_v"], f).T)
    out["wo"] = tile_oi(np.asarray(kw["L0_attn_b_W_o"], f).T)
    out["we1"] = np.ascontiguousarray(
        np.asarray(kw["L0_ffn_We1"], f).reshape(16, 8, 128, 512))
    out["we3"] = np.ascontiguousarray(
        np.asarray(kw["L0_ffn_We3"], f).reshape(16, 8, 128, 512))
    out["we2"] = np.ascontiguousarray(
        np.asarray(kw["L0_ffn_We2"], f).reshape(16, 4, 128, 8, 128)
        .transpose(3, 0, 1, 2, 4))
    out["ws1"] = tile_oi(np.asarray(kw["L0_ffn_ws1"], f))
    out["ws3"] = tile_oi(np.asarray(kw["L0_ffn_ws3"], f))
    out["ws2"] = tile_oi(np.asarray(kw["L0_ffn_ws2"], f))
    gwt = np.asarray(kw["L0_ffn_gate_w"], f).T
    out["gwt"] = np.ascontiguousarray(
        gwt.reshape(8, 128, 16).transpose(1, 0, 2).reshape(128, 128))
    out["be2m"] = np.asarray(kw["L0_ffn_be2"], f)
    out["gateb"] = np.asarray(kw["L0_ffn_gate_b"], f).reshape(1, 16)

    cc = np.zeros((128, 224), f)

    def pack8(vec):
        return np.asarray(vec, f).reshape(8, 128).T

    cc[:, 0:8] = pack8(kw["L0_attn_b_time_mix_r"])
    cc[:, 8:16] = pack8(kw["L0_attn_b_time_mix_k"])
    cc[:, 16:24] = pack8(kw["L0_attn_b_time_mix_v"])
    cc[:, 24:32] = pack8(np.asarray(kw["L0_attn_b_time_decay"], f).reshape(-1))
    cc[:, 32:40] = pack8(np.asarray(kw["L0_attn_b_time_first"], f).reshape(-1))
    cc[:, 40:48] = pack8(kw["L0_attn_b_gn_weight"])
    cc[:, 48:56] = pack8(kw["L0_attn_b_gn_bias"])
    cc[:, 56:64] = pack8(kw["L0_ffn_bs1"])
    cc[:, 64:72] = pack8(kw["L0_ffn_bs2"])
    cc[:, 72:80] = pack8(kw["L0_ffn_bs3"])
    be1 = np.asarray(kw["L0_ffn_be1"], f)
    be3 = np.asarray(kw["L0_ffn_be3"], f)
    cc[:, 80:144] = be1.reshape(16, 4, 128).transpose(2, 0, 1).reshape(128, 64)
    cc[:, 144:208] = be3.reshape(16, 4, 128).transpose(2, 0, 1).reshape(128, 64)
    cc[:, 208] = 1e-5
    cc[:, 209] = 1e-9
    out["cc"] = cc

    out["idf"] = np.eye(128, dtype=f)
    out["ones128"] = np.ones((128, 128), f)
    out["onesrow"] = np.ones((1, 128), f)
    lo = np.zeros((1, 128), f)
    lo[0, 0:64] = 1.0
    out["oneslo"] = lo
    hi = np.zeros((1, 128), f)
    hi[0, 64:128] = 1.0
    out["oneshi"] = hi
    out["stair"] = (np.arange(896)[None, :] >=
                    (np.arange(128)[:, None] + 384)).astype(f)
    out["exh"] = np.kron(np.eye(16, dtype=f), np.ones((1, 128), f))
    hsel = np.zeros((128, 2), f)
    hsel[0:64, 0] = 1.0
    hsel[64:128, 1] = 1.0
    out["hsel"] = hsel
    out["hselt"] = np.ascontiguousarray(hsel.T)
    return out


def _run(inputs, phases=("a", "f1", "b", "f2"), dumps=(), cores=8):
    x = np.asarray(inputs["L0_attn_a_x"], np.float32)
    shared = _prep_shared_inputs(inputs)
    nc = build_program(phases=phases, dumps=dumps)
    in_maps = []
    for core in range(cores):
        m = dict(shared)
        m["x"] = np.ascontiguousarray(
            x[core * NB:(core + 1) * NB].transpose(0, 2, 1))
        in_maps.append(m)
    res = run_bass_kernel_spmd(nc, in_maps, core_ids=list(range(cores)))
    return res


def kernel(**inputs):
    res = _run(inputs)
    outs = []
    for core in range(8):
        y = np.asarray(res.results[core]["y"])
        outs.append(np.ascontiguousarray(y.transpose(0, 2, 1)))
    return np.concatenate(outs, axis=0)


# version 07
# speedup vs baseline: 1.5665x; 1.1508x over previous
import contextlib
import numpy as np

import concourse.bass as bass
import concourse.mybir as mybir
from concourse.bass_utils import run_bass_kernel_spmd

F32 = mybir.dt.float32
BF16 = mybir.dt.bfloat16
AF = mybir.ActivationFunctionType
ALU = mybir.AluOpType
AX = mybir.AxisListType

D = 1024
T = 2048
H = 16
HD = 64
NE = 16
INTER = 512
NB = 2
NDT = 8
TC = 512
NCH = T // TC
EPS = 1e-5

ENGINES = ("sync", "gpsimd", "vector", "scalar", "tensor")


class Ch:
    def __init__(self, handle):
        self.h = handle
        self.n = 0


class Emitter:

    def __init__(self, sems):
        self.q = {e: [] for e in ENGINES}
        self.ch = {name: Ch(h) for name, h in sems.items()}
        self.bar = {e: self.ch["bar_" + e] for e in ENGINES}
        self.bank_free = {}
        self.bank_ready = {}

    def on(self, eng, fn):
        self.q[eng].append(fn)

    def mk_inc(self, ch, k=1):
        ch.n += k
        h = ch.h

        def apply(instr):
            instr.then_inc(h, k)
            return instr
        return apply

    def tok(self, ch):
        ap = self.mk_inc(ch, 1)
        return ap, (ch, ch.n)

    def wait_tok(self, eng, tok):
        ch, v = tok
        self.wait_val(eng, ch, v)

    def wait_now(self, eng, ch):
        self.wait_val(eng, ch, ch.n)

    def wait_val(self, eng, ch, v):
        h = ch.h
        if v > 0:
            self.on(eng, lambda e: e.wait_ge(h, v))

    def barrier(self, engines=ENGINES):
        dma_chs = {"sync": tuple(["w", "o"] + ["dma%d" % i
                                                for i in range(8)]),
                   "gpsimd": ("g", "sx", "sta", "stb", "stall")}
        for e in engines:
            for cn in dma_chs.get(e, ()):
                self.wait_now(e, self.ch[cn])
        toks = {}
        for e in engines:
            ap = self.mk_inc(self.bar[e], 1)
            self.on(e, lambda eng, ap=ap: ap(eng.nop()))
            toks[e] = self.bar[e].n
        for e in engines:
            for o in engines:
                if o != e:
                    self.wait_val(e, self.bar[o], toks[o])

    def bank_acquire(self, bank):
        if bank in self.bank_free:
            self.wait_tok("tensor", self.bank_free[bank])

    def mark_ready(self, bank, ch):
        ap, tk = self.tok(ch)
        self.bank_ready[bank] = tk
        return ap

    def wait_ready(self, eng, bank):
        self.wait_tok(eng, self.bank_ready[bank])

    def mark_free(self, bank, eng):
        chmap = {"vector": "v", "scalar": "s", "gpsimd": "g", "sync": "w"}
        ap, tk = self.tok(self.ch[chmap[eng]])
        self.bank_free[bank] = tk
        return ap

    def free_after(self, bank, eng):
        chmap = {"vector": "v", "scalar": "s", "gpsimd": "g", "sync": "w"}
        ch = self.ch[chmap[eng]]
        self.bank_free[bank] = (ch, ch.n)

    def run(self, block):
        for e in ENGINES:
            fns = self.q[e]

            def body(eng, fns=fns):
                for fn in fns:
                    fn(eng)
            getattr(block, e)(body)


def build_program(phases=("a", "f1", "b", "f2"), dumps=()):
    nc = bass.Bass("TRN2", target_bir_lowering=False)

    din = {}

    def I(name, shape, dt=F32):
        din[name] = nc.dram_tensor(name, list(shape), dt,
                                   kind="ExternalInput").ap()
        return din[name]

    x_in = I("x", (NB, D, T))
    I("wqkv", (8, 8, 128, 384))
    I("wout", (8, 8, 128, 128))
    for nm in ("wr", "wk", "wv", "wo"):
        I(nm, (8, 8, 128, 128))
    I("we1", (NE, 8, 128, 512))
    I("we3", (NE, 8, 128, 512))
    I("we2", (8, NE, 4, 128, 128))
    I("ws1", (8, 8, 128, 128))
    I("ws3", (8, 8, 128, 128))
    I("ws2", (8, 8, 128, 128))
    I("gwt", (128, 128))
    I("be2m", (16, 1024))
    I("gateb", (1, 16))
    I("cc", (128, 224))
    I("idf", (128, 128))
    I("ones128", (128, 128))
    I("onesrow", (1, 128))
    I("oneslo", (1, 128))
    I("oneshi", (1, 128))
    I("stair", (128, 896))
    I("hsel", (128, 2))
    I("exh", (16, 2048))
    I("hselt", (2, 128))

    y_out = nc.dram_tensor("y", [NB, D, T], F32, kind="ExternalOutput").ap()

    dstg = {}
    for nm, shp in (
        ("wqkv_b", (8, 8, 128, 384)), ("wout_b", (8, 8, 128, 128)),
        ("wr_b", (8, 8, 128, 128)), ("wk_b", (8, 8, 128, 128)),
        ("wv_b", (8, 8, 128, 128)), ("wo_b", (8, 8, 128, 128)),
        ("we1_b", (NE, 8, 128, 512)), ("we3_b", (NE, 8, 128, 512)),
        ("we2_b", (8, NE, 4, 128, 128)),
        ("ws1_b", (8, 8, 128, 128)), ("ws3_b", (8, 8, 128, 128)),
        ("ws2_b", (8, 8, 128, 128)),
    ):
        dstg[nm] = nc.dram_tensor(nm, list(shp), BF16, kind="Internal").ap()

    dump_outs = {}
    for nm in dumps:
        shp, dt = {"R": ([128, 16384], F32), "A1": ([128, 16384], BF16),
                   "A2": ([128, 16384], BF16), "A3": ([128, 6144], BF16),
                   "SCF": ([128, 4096], F32), "CT": ([128, 64], F32),
                   "SS": ([128, 8], F32), "SQ": ([128, 8], F32),
                   "DC": ([128, 64], F32)}[nm]
        dump_outs[nm] = nc.dram_tensor("dump_" + nm, shp, dt,
                                       kind="ExternalOutput").ap()

    es = contextlib.ExitStack()
    with es:
        sbt = lambda n, p, f, dt: es.enter_context(
            nc.sbuf_tensor(n, [p, f], dt))
        R = sbt("R", 128, 16384, F32)
        A1 = sbt("A1", 128, 16384, BF16)
        A2 = sbt("A2", 128, 16384, BF16)
        A3 = sbt("A3", 128, 6144, BF16)
        W1 = sbt("W1", 128, 8192, BF16)
        W2 = sbt("W2", 128, 8192, BF16)
        SCF = sbt("SCF", 128, 4096, F32)
        SC2 = sbt("SC2", 128, 512, F32)
        SCB = sbt("SCB", 128, 2048, BF16)
        GP = SCF[0:18, 3072:3584]
        GT = SCF[:, 2112:2184]
        CT = SCF[:, 2048:2112]
        CTROW = SCF[0:16, 3584:4096]
        CTB = sbt("CTB", 16, 512, BF16)
        CC = sbt("CC", 128, 224, F32)
        DC = sbt("DC", 128, 64, F32)
        GWT = sbt("GWT", 128, 128, F32)
        CS = sbt("CS", 128, 16, F32)
        BE2 = sbt("BE2", 16, 1024, BF16)
        GB = sbt("GB", 128, 16, F32)
        IDF = sbt("IDF", 128, 128, F32)
        IDB = sbt("IDB", 128, 128, BF16)
        ONES128B = sbt("ONES128B", 128, 128, BF16)
        ONESROWF = sbt("ONESROWF", 1, 128, F32)
        ONESROWB = sbt("ONESROWB", 1, 128, BF16)
        ONESLO = sbt("ONESLO", 1, 128, F32)
        ONESHI = sbt("ONESHI", 1, 128, F32)
        STAIRB = sbt("STAIRB", 128, 896, BF16)
        HSEL = sbt("HSEL", 128, 2, F32)
        EXHB = sbt("EXHB", 16, 2048, BF16)
        HSELT = sbt("HSELT", 2, 128, F32)
        SS = SCF[:, 2184:2192]
        SQ = SCF[:, 2192:2200]
        CAB = SCF[:, 2200:2216]
        SCL = SCF[:, 2216:2224]
        BIA = SCF[:, 2224:2232]

        PS = [es.enter_context(nc.psum_tensor("PB%d" % i, [128, 512], F32))
              for i in range(6)]
        PTB = es.enter_context(nc.psum_tensor("PTB", [128, 128], BF16))
        PTF = es.enter_context(nc.psum_tensor("PTF", [128, 128], F32))

        sems = {}
        for nm in (["w", "g", "t", "s", "v", "o", "sx", "sta", "stb",
                    "stall"] + ["bar_" + e for e in ENGINES]
                   + ["dma%d" % i for i in range(8)]):
            sems[nm] = es.enter_context(nc.semaphore("sem_" + nm))

        block = es.enter_context(nc.Block())
        em = Emitter(sems)
        chW, chG, chT, chS, chV, chO = (em.ch[k] for k in
                                        ("w", "g", "t", "s", "v", "o"))
        chD = [em.ch["dma%d" % i] for i in range(8)]

        def dma(eng, dst, src, ch):
            ap = em.mk_inc(ch, 16)
            em.on(eng, lambda e: ap(e.dma_start(out=dst, in_=src)))

        def s_act(out, in_, func, bias=0.0, scale=1.0, accum=None, apl=()):
            ap = em.mk_inc(chS, 1)

            def f(s):
                if accum is not None:
                    ins = s.activation(out=out, in_=in_, func=func, bias=bias,
                                       scale=scale, accum_out=accum)
                else:
                    ins = s.activation(out=out, in_=in_, func=func, bias=bias,
                                       scale=scale)
                ap(ins)
                for a in apl:
                    a(ins)
                return ins
            em.on("scalar", f)

        def v_op(fn, apl=()):
            ap = em.mk_inc(chV, 1)

            def f(v):
                ins = fn(v)
                ap(ins)
                for a in apl:
                    a(ins)
                return ins
            em.on("vector", f)

        import traceback

        def t_mm(out, lhsT, rhs, start, stop, apl=()):
            site = traceback.extract_stack(limit=3)[0]

            def f(t):
                try:
                    ins = t.matmul(out, lhsT, rhs, start=start, stop=stop)
                except Exception:
                    print("t_mm failed from %s:%d" % (site.filename,
                                                      site.lineno))
                    raise
                for a in apl:
                    a(ins)
                return ins
            em.on("tensor", f)

        def t_tr(out, in_, ident, apl=()):
            def f(t):
                ins = t.transpose(out, in_, ident)
                for a in apl:
                    a(ins)
                return ins
            em.on("tensor", f)

        def Rt(dt, c):
            return R[:, dt * 2048 + c * TC: dt * 2048 + c * TC + TC]

        dma("sync", CC[:], din["cc"][:], chW)
        dma("sync", GWT[:], din["gwt"][:], chW)
        dma("sync", IDF[:], din["idf"][:], chW)
        dma("sync", ONESROWF[:], din["onesrow"][:], chW)
        dma("sync", ONESLO[:], din["oneslo"][:], chW)
        dma("sync", ONESHI[:], din["oneshi"][:], chW)
        dma("sync", HSEL[:], din["hsel"][:], chW)
        dma("sync", HSELT[:], din["hselt"][:], chW)
        dma("sync", GB[:], din["gateb"].broadcast_to([128, 16]), chW)
        dma("gpsimd", STAIRB[:], din["stair"][:], chG)
        dma("gpsimd", ONES128B[:], din["ones128"][:], chG)
        dma("gpsimd", IDB[:], din["idf"][:], chG)
        dma("gpsimd", ONESROWB[:], din["onesrow"][:], chG)
        dma("gpsimd", BE2[:], din["be2m"][:], chG)
        dma("gpsimd", EXHB[:], din["exh"][:], chG)

        for e in ("scalar", "vector", "tensor"):
            em.wait_now(e, chW)
            em.wait_now(e, chG)

        s_act(DC[:, 0:8], CC[:, 24:32], AF.Exp)
        s_act(DC[:, 0:8], DC[:, 0:8], AF.Exp, scale=-1.0)
        s_act(DC[:, 8:16], CC[:, 32:40], AF.Exp)
        em.wait_now("vector", chS)

        def vsync0():
            ap = em.mk_inc(chV, 1)
            em.on("vector", lambda v, ap=ap: ap(v.engine_nop()))
            em.wait_now("vector", chV)

        v_op(lambda v: v.tensor_mul(DC[:, 16:24], DC[:, 0:8], DC[:, 8:16]))
        vsync0()
        v_op(lambda v: v.tensor_scalar(DC[:, 16:24], DC[:, 16:24], -1.0, 1.0,
                                       op0=ALU.mult, op1=ALU.add))
        vsync0()
        for j in range(3):
            v_op(lambda v, j=j: v.tensor_scalar(
                DC[:, 24 + 8 * j:32 + 8 * j], CC[:, 8 * j:8 * j + 8],
                -1.0, 1.0, op0=ALU.mult, op1=ALU.add))
        v_op(lambda v: v.memset(SCF[:, 0:1], 1.0))
        em.wait_now("tensor", chV)
        for kt in range(8):
            apl = [em.mark_ready(0, chT)] if kt == 7 else ()
            t_mm(PS[0][0:1, 0:16], SCF[:, 0:1], GWT[:, kt * 16:kt * 16 + 16],
                 start=(kt == 0), stop=(kt == 7), apl=apl)
        em.wait_ready("vector", 0)
        v_op(lambda v: v.tensor_copy(SCF[0:1, 4:20], PS[0][0:1, 0:16]))
        em.free_after(0, "vector")
        em.wait_now("tensor", chV)
        t_mm(PTF[:, 0:16], ONESROWF[:], SCF[0:1, 4:20], True, True,
             apl=[em.mark_ready(6, chT)])
        em.wait_ready("vector", 6)
        v_op(lambda v: v.tensor_copy(CS[:], PTF[:, 0:16]))
        em.free_after(6, "vector")

        chSX, chSTA, chSTB, chSTALL = (em.ch[k] for k in
                                       ("sx", "sta", "stb", "stall"))
        dma("gpsimd", R[:].rearrange("p (dt t) -> p dt t", dt=8),
            x_in[0].rearrange("(dt p) t -> p dt t", p=128), chSX)
        for nm in ("wqkv", "wout"):
            dma("gpsimd", dstg[nm + "_b"][:], din[nm][:], chSTA)
        for nm in ("wr", "wk", "wv", "wo"):
            dma("gpsimd", dstg[nm + "_b"][:], din[nm][:], chSTB)
        for nm in ("ws1", "ws3", "ws2", "we1", "we3", "we2"):
            dma("gpsimd", dstg[nm + "_b"][:], din[nm][:], chSTALL)


        scb_guard = [0, 0]
        scb_ffn_guard = [0, 0]

        def ln_chunk(c, dest_tile_fn):
            for bank in (0, 1):
                em.bank_acquire(bank)
            for dt in range(NDT):
                par = dt % 2
                xb = SCB[:, par * 1024: par * 1024 + 512]
                x2b = SCB[:, par * 1024 + 512: par * 1024 + 1024]
                if scb_guard[par]:
                    em.wait_val("scalar", chT, scb_guard[par])
                if scb_ffn_guard[par]:
                    em.wait_val("scalar", chV, scb_ffn_guard[par])
                s_act(xb, Rt(dt, c), AF.Copy)
                s_act(x2b, Rt(dt, c), AF.Square)
                em.wait_now("tensor", chS)
                apl0 = [em.mark_ready(0, chT)] if dt == NDT - 1 else ()
                t_mm(PS[0][0:1, 0:512], ONES128B[:, 0:1], xb,
                     start=(dt == 0), stop=(dt == NDT - 1), apl=apl0)
                if dt == NDT - 1:
                    apl1 = [em.mark_ready(1, chT)]
                else:
                    apl1 = [em.mk_inc(chT, 1)]
                t_mm(PS[1][0:1, 0:512], ONES128B[:, 0:1], x2b,
                     start=(dt == 0), stop=(dt == NDT - 1), apl=apl1)
                scb_guard[par] = chT.n
            em.wait_ready("vector", 0)
            em.wait_ready("vector", 1)
            arow = SCF[0:1, 0:512]
            brow = SCF[0:1, 512:1024]
            vrow = SCF[0:1, 1024:1536]
            mrow = SCF[0:1, 1536:2048]
            v_op(lambda v: v.tensor_scalar_mul(mrow, PS[0][0:1, 0:512],
                                               1.0 / D))
            em.free_after(0, "vector")
            v_op(lambda v: v.tensor_scalar_mul(vrow, PS[1][0:1, 0:512],
                                               1.0 / D))
            em.free_after(1, "vector")
            v_op(lambda v: v.tensor_mul(arow, mrow, mrow))
            v_op(lambda v: v.tensor_sub(vrow, vrow, arow))
            em.wait_now("scalar", chV)
            s_act(arow, vrow, AF.Ln, bias=CC[0:1, 208:209])
            s_act(arow, arow, AF.Exp, scale=-0.5)
            em.wait_now("vector", chS)
            v_op(lambda v: v.tensor_mul(brow, mrow, arow))
            v_op(lambda v: v.tensor_scalar_mul(brow, brow, -1.0))
            em.bank_acquire(4)
            em.bank_acquire(5)
            em.wait_now("tensor", chV)
            t_mm(PS[4][:, 0:512], ONESROWF[:], arow, True, True,
                 apl=[em.mark_ready(4, chT)])
            t_mm(PS[5][:, 0:512], ONESROWF[:], brow, True, True,
                 apl=[em.mark_ready(5, chT)])
            em.wait_ready("vector", 4)
            em.wait_ready("vector", 5)
            t1 = SC2[:, 0:512]
            for dt in range(NDT):
                v_op(lambda v, dt=dt: v.tensor_mul(t1, Rt(dt, c),
                                                   PS[4][:, 0:512]))
                v_op(lambda v, dt=dt: v.tensor_add(dest_tile_fn(dt), t1,
                                                   PS[5][:, 0:512]))
                if dt == NDT - 1:
                    em.free_after(4, "vector")
                    em.free_after(5, "vector")

        def phase_attn_a(b):
            for c in range(NCH):
                ln_chunk(c, lambda dt, c=c: A1[:, dt * 2048 + c * TC:
                                               dt * 2048 + c * TC + TC])
            em.barrier(("tensor", "vector", "scalar", "sync"))

            VT = [W2[:, 0:1040], W2[:, 1040:2080]]
            PT = [W2[:, 2080 + i * 512: 2080 + (i + 1) * 512]
                  for i in range(4)]
            for vb in range(2):
                for kt in range(16):
                    v_op(lambda v, vb=vb, kt=kt: v.memset(
                        VT[vb][:, kt * 65 + 64: kt * 65 + 65], 1.0))
            wslab = [W1[:, 0:3072], W1[:, 4096:4096 + 3072]]
            slab_owner = [0, 0]
            wq_tok = {}

            def wq_load(s):
                slot = s % 2
                if slab_owner[slot]:
                    em.wait_val("sync", chT, slab_owner[slot])
                dma("sync",
                    wslab[slot].rearrange("p (kt c) -> p kt c", kt=8),
                    dstg["wqkv_b"][s].rearrange("kt p c -> p kt c"),
                    chD[slot])
                wq_tok[s] = (chD[slot], chD[slot].n)

            vt_done = {}

            def tr_gen(h):
                s_, par = h // 2, h % 2
                rows = slice(par * 64, par * 64 + 64)
                vtb = VT[h % 2]
                for kt in range(16):
                    em.bank_acquire("ptb")
                    t_tr(PTB[:, 0:64],
                         A3[rows, 2 * 2048 + kt * 128:
                            2 * 2048 + kt * 128 + 128],
                         IDB[rows, par * 64: par * 64 + 64],
                         apl=[em.mark_ready("ptb", chT)])
                    em.wait_ready("vector", "ptb")
                    v_op(lambda v, kt=kt, vtb=vtb: v.tensor_copy(
                        vtb[:, kt * 65: kt * 65 + 64], PTB[:, 0:64]))
                    em.free_after("ptb", "vector")
                    yield
                vt_done[h] = chV.n

            def drain(gen):
                if gen is not None:
                    for _ in gen:
                        pass

            em.wait_now("sync", chSTA)
            wq_load(0)
            pend = None
            pt_guard = [0, 0, 0, 0]
            for s in range(8):
                slot = s % 2
                if s + 1 < 8:
                    wq_load(s + 1)
                em.wait_tok("tensor", wq_tok[s])
                qkv_banks = (2, 3, 4, 5)
                bi = 0
                for mt in (2, 0, 1):
                    for c in range(NCH):
                        if mt == 0 and c == 0:
                            em.wait_now("tensor", chS)
                            pend = tr_gen(2 * s)
                        bank = qkv_banks[bi % 4]
                        bi += 1
                        em.bank_acquire(bank)
                        for kt in range(8):
                            last = kt == 7
                            apl = ([em.mark_ready(bank, chT)]
                                   if last else ())
                            t_mm(PS[bank][:, 0:512],
                                 wslab[slot][:, kt * 384 + mt * 128:
                                             kt * 384 + mt * 128 + 128],
                                 A1[:, kt * 2048 + c * TC:
                                    kt * 2048 + c * TC + TC],
                                 start=(kt == 0), stop=last, apl=apl)
                        if pend is not None and mt != 2:
                            next(pend, None)
                            next(pend, None)
                        em.wait_ready("scalar", bank)
                        dst = A3[:, mt * 2048 + c * TC:
                                 mt * 2048 + c * TC + TC]
                        s_act(dst, PS[bank][:, 0:512], AF.Copy)
                        em.free_after(bank, "scalar")
                slab_owner[slot] = chT.n
                drain(pend)
                pend = None
                for par in range(2):
                    h = 2 * s + par
                    if par == 0:
                        em.wait_now("tensor", chS)
                        pend = tr_gen(2 * s + 1)
                    else:
                        drain(pend)
                        pend = None
                    em.wait_val("tensor", chV, vt_done[h])
                    rows = slice(par * 64, par * 64 + 64)
                    vtb = VT[h % 2]
                    for qc in range(NCH):
                        nkt = 4 * (qc + 1)
                        em.bank_acquire(4)
                        exp_tok = {}

                        def emit_pv(kt, qc=qc, nkt=nkt, vtb=vtb):
                            cn, vv = exp_tok[kt]
                            em.wait_val("tensor", em.ch[cn], vv)
                            if kt == nkt - 1:
                                apl = [em.mark_ready(4, chT)]
                            else:
                                apl = [em.mk_inc(chT, 1)]
                            t_mm(PS[4][0:65, 0:512],
                                 vtb[:, kt * 65: kt * 65 + 65], PT[kt % 4],
                                 start=(kt == 0), stop=(kt == nkt - 1),
                                 apl=apl)
                            pt_guard[kt % 4] = chT.n

                        for kt in range(nkt):
                            sbank = 2 + kt % 2
                            em.bank_acquire(sbank)
                            t_mm(PS[sbank][:, 0:512],
                                 A3[rows, 1 * 2048 + kt * 128:
                                    1 * 2048 + kt * 128 + 128],
                                 A3[rows, 0 * 2048 + qc * TC:
                                    0 * 2048 + qc * TC + TC],
                                 start=True, stop=True,
                                 apl=[em.mark_ready(sbank, chT)])
                            if kt >= 2:
                                emit_pv(kt - 2)
                            em.wait_ready("scalar", sbank)
                            pt = PT[kt % 4]
                            if pt_guard[kt % 4]:
                                em.wait_val("scalar", chT, pt_guard[kt % 4])
                            s_act(pt, PS[sbank][:, 0:512], AF.Exp,
                                  scale=0.125)
                            em.free_after(sbank, "scalar")
                            dd = kt - 4 * qc
                            if dd >= 0:
                                off = 384 - 128 * dd
                                em.wait_now("vector", chS)
                                v_op(lambda v, pt=pt, off=off: v.tensor_mul(
                                    pt, pt, STAIRB[:, off:off + 512]))
                                exp_tok[kt] = ("v", chV.n)
                            else:
                                exp_tok[kt] = ("s", chS.n)
                            if pend is not None:
                                next(pend, None)
                        emit_pv(nkt - 2)
                        emit_pv(nkt - 1)
                        em.wait_ready("scalar", 4)
                        odst = A2[rows, s * 2048 + qc * TC:
                                  s * 2048 + qc * TC + TC]
                        s_act(odst, PS[4][0:64, 0:512], AF.Copy)
                        drow = SCF[0:1, par * 2048 + qc * TC:
                                   par * 2048 + qc * TC + TC]
                        s_act(drow, PS[4][64:65, 0:512], AF.Copy)
                        em.free_after(4, "scalar")
                em.wait_now("tensor", chS)
                for qc in range(NCH):
                    em.bank_acquire(5)
                    t_mm(PS[5][:, 0:512], ONESLO[:],
                         SCF[0:1, qc * TC: qc * TC + TC],
                         start=True, stop=False)
                    t_mm(PS[5][:, 0:512], ONESHI[:],
                         SCF[0:1, 2048 + qc * TC: 2048 + qc * TC + TC],
                         start=False, stop=True,
                         apl=[em.mark_ready(5, chT)])
                    em.wait_ready("scalar", 5)
                    rec = SC2[:, 0:512]
                    em.wait_now("scalar", chV)
                    s_act(rec, PS[5][:, 0:512], AF.Ln)
                    em.free_after(5, "scalar")
                    s_act(rec, rec, AF.Exp, scale=-1.0)
                    em.wait_now("vector", chS)
                    ot = A2[:, s * 2048 + qc * TC: s * 2048 + qc * TC + TC]
                    v_op(lambda v, ot=ot: v.tensor_mul(ot, ot, rec))

            em.barrier(("tensor", "vector", "scalar", "sync"))
            wslot = [W1[:, 0:1024], W1[:, 4096:4096 + 1024]]
            owner = [0, 0]
            wo_tok = {}

            def wo_load(et):
                sl = et % 2
                if owner[sl]:
                    em.wait_val("sync", chT, owner[sl])
                dma("sync",
                    wslot[sl].rearrange("p (kt c) -> p kt c", kt=8),
                    dstg["wout_b"][et].rearrange("kt p c -> p kt c"),
                    chD[2 + sl])
                wo_tok[et] = (chD[2 + sl], chD[2 + sl].n)

            wo_load(0)
            for et in range(8):
                sl = et % 2
                if et + 1 < 8:
                    wo_load(et + 1)
                em.wait_tok("tensor", wo_tok[et])
                for c in range(NCH):
                    bank = 4 + c % 2
                    em.bank_acquire(bank)
                    for kt in range(8):
                        last = kt == 7
                        apl = [em.mark_ready(bank, chT)] if last else ()
                        t_mm(PS[bank][:, 0:512],
                             wslot[sl][:, kt * 128: kt * 128 + 128],
                             A2[:, kt * 2048 + c * TC:
                                kt * 2048 + c * TC + TC],
                             start=(kt == 0), stop=last, apl=apl)
                    em.wait_ready("vector", bank)
                    v_op(lambda v, et=et, c=c, bank=bank: v.tensor_add(
                        Rt(et, c), Rt(et, c), PS[bank][:, 0:512]))
                    em.free_after(bank, "vector")
                owner[sl] = chT.n
            em.barrier()


        def phase_ffn(b):
            em.wait_now("sync", chSTALL)
            eslot_guard = [0, 0, 0]
            w2hi_guard = [0]
            for c in range(NCH):
                ln_chunk(c, lambda dt: A3[:, dt * 512: dt * 512 + 512])
                em.bank_acquire(0)
                for kt in range(8):
                    apl = [em.mark_ready(0, chT)] if kt == 7 else ()
                    t_mm(PS[0][0:16, 0:512], GWT[:, kt * 16: kt * 16 + 16],
                         Rt(kt, c), start=(kt == 0), stop=(kt == 7), apl=apl)
                em.wait_ready("scalar", 0)
                GPS = SCF[0:18, 3072:3584]
                s_act(GPS[0:16, :], PS[0][0:16, 0:512], AF.Copy)
                em.free_after(0, "scalar")
                arow = SCF[0:1, 0:512]
                brow = SCF[0:1, 512:1024]
                em.wait_now("gpsimd", chV)
                em.wait_now("gpsimd", chS)
                dma("gpsimd", SCF[16:17, 3072:3584], arow, chG)
                dma("gpsimd", SCF[17:18, 3072:3584], brow, chG)
                em.wait_now("tensor", chG)

                def vsync():
                    ap = em.mk_inc(chV, 1)
                    em.on("vector", lambda v, ap=ap: ap(v.engine_nop()))
                    em.wait_now("vector", chV)

                WGS = []
                for j in range(4):
                    base = 2240 + j * 192
                    W = {}
                    o = SCF[:, base:base + 192]
                    W["gt"] = o[:, 0:18]
                    W["t0"] = o[:, 18:34]
                    W["sc"] = o[:, 34:50]
                    W["p0"] = o[:, 50:66]
                    W["s2"] = o[:, 66:82]
                    W["gm"] = o[:, 82:90]
                    W["gs"] = o[:, 90:94]
                    W["keep"] = o[:, 94:98]
                    W["ke"] = o[:, 98:114]
                    W["s2k"] = o[:, 114:130]
                    W["e1"] = o[:, 130:146]
                    W["e2"] = o[:, 146:162]
                    W["eqg"] = o[:, 162:166]
                    W["tmp"] = o[:, 166:182]
                    W["nmx"] = o[:, 182:183]
                    W["sume"] = o[:, 183:184]
                    W["rcp"] = o[:, 184:185]
                    W["r1"] = o[:, 185:186]
                    W["r2"] = o[:, 186:187]
                    W["m1"] = o[:, 187:188]
                    W["m2"] = o[:, 188:189]
                    WGS.append(W)
                for j in range(4):
                    em.bank_acquire("ptf")
                    em.wait_now("tensor", chS)
                    t_tr(PTF[:, 0:18], GPS[0:18, j * 128: j * 128 + 128],
                         IDF[0:18, 0:18], apl=[em.mark_ready("ptf", chT)])
                    em.wait_ready("vector", "ptf")
                    v_op(lambda v, j=j: v.tensor_copy(WGS[j]["gt"],
                                                      PTF[:, 0:18]))
                    em.free_after("ptf", "vector")
                vsync()
                for j in range(4):
                    W = WGS[j]
                    v_op(lambda v, W=W: v.tensor_scalar(
                        W["t0"], W["gt"][:, 0:16], W["gt"][:, 16:17], None,
                        op0=ALU.mult))
                vsync()
                for j in range(4):
                    W = WGS[j]
                    v_op(lambda v, W=W: v.scalar_tensor_tensor(
                        W["sc"], CS[:], W["gt"][:, 17:18], W["t0"],
                        op0=ALU.mult, op1=ALU.add))
                vsync()
                for j in range(4):
                    W = WGS[j]
                    v_op(lambda v, W=W: v.reduce_max(W["nmx"], W["sc"],
                                                     axis=AX.X, negate=True))
                vsync()
                em.wait_now("scalar", chV)
                for j in range(4):
                    W = WGS[j]
                    s_act(W["p0"], W["sc"], AF.Exp, bias=W["nmx"],
                          accum=W["sume"])
                for j in range(4):
                    W = WGS[j]
                    s_act(W["rcp"], W["sume"], AF.Ln)
                for j in range(4):
                    W = WGS[j]
                    s_act(W["rcp"], W["rcp"], AF.Exp, scale=-1.0)
                em.wait_now("vector", chS)
                for j in range(4):
                    W = WGS[j]
                    v_op(lambda v, W=W: v.tensor_scalar(
                        W["p0"], W["p0"], W["rcp"], None, op0=ALU.mult))
                vsync()
                for j in range(4):
                    W = WGS[j]
                    v_op(lambda v, W=W: v.tensor_add(W["s2"], W["p0"], GB[:]))
                vsync()
                for gi in range(4):
                    for j in range(4):
                        W = WGS[j]
                        v_op(lambda v, W=W, gi=gi: v.reduce_max(
                            W["gm"][:, gi:gi + 1],
                            W["s2"][:, 4 * gi:4 * gi + 4], axis=AX.X))
                vsync()
                for gi in range(4):
                    for j in range(4):
                        W = WGS[j]
                        v_op(lambda v, W=W, gi=gi: v.tensor_scalar(
                            W["eqg"], W["s2"][:, 4 * gi:4 * gi + 4],
                            W["gm"][:, gi:gi + 1], None, op0=ALU.is_equal))
                    vsync()
                    for j in range(4):
                        W = WGS[j]
                        v_op(lambda v, W=W, gi=gi: v.scalar_tensor_tensor(
                            W["tmp"][:, 0:4], W["eqg"], -1e30,
                            W["s2"][:, 4 * gi:4 * gi + 4],
                            op0=ALU.mult, op1=ALU.add))
                    vsync()
                    for j in range(4):
                        W = WGS[j]
                        v_op(lambda v, W=W, gi=gi: v.reduce_max(
                            W["gm"][:, 4 + gi:5 + gi], W["tmp"][:, 0:4],
                            axis=AX.X))
                    vsync()
                for j in range(4):
                    W = WGS[j]
                    v_op(lambda v, W=W: v.tensor_add(
                        W["gs"], W["gm"][:, 0:4], W["gm"][:, 4:8]))
                vsync()
                for j in range(4):
                    W = WGS[j]
                    v_op(lambda v, W=W: v.reduce_max(W["r1"], W["gs"],
                                                     axis=AX.X))
                vsync()
                for j in range(4):
                    W = WGS[j]
                    v_op(lambda v, W=W: v.tensor_scalar(
                        W["eqg"], W["gs"], W["r1"], None, op0=ALU.is_equal))
                vsync()
                for j in range(4):
                    W = WGS[j]
                    v_op(lambda v, W=W: v.scalar_tensor_tensor(
                        W["tmp"][:, 0:4], W["eqg"], -1e30, W["gs"],
                        op0=ALU.mult, op1=ALU.add))
                vsync()
                for j in range(4):
                    W = WGS[j]
                    v_op(lambda v, W=W: v.reduce_max(W["r2"],
                                                     W["tmp"][:, 0:4],
                                                     axis=AX.X))
                vsync()
                for j in range(4):
                    W = WGS[j]
                    v_op(lambda v, W=W: v.tensor_scalar(
                        W["keep"], W["gs"], W["r2"], None, op0=ALU.is_ge))
                vsync()
                for j in range(4):
                    W = WGS[j]
                    v_op(lambda v, W=W: v.memset(W["ke"], 1.0))
                vsync()
                for gi in range(4):
                    for j in range(4):
                        W = WGS[j]
                        v_op(lambda v, W=W, gi=gi: v.tensor_scalar(
                            W["ke"][:, 4 * gi:4 * gi + 4],
                            W["ke"][:, 4 * gi:4 * gi + 4],
                            W["keep"][:, gi:gi + 1], None, op0=ALU.mult))
                    vsync()
                for j in range(4):
                    W = WGS[j]
                    v_op(lambda v, W=W: v.tensor_scalar(
                        W["ke"], W["ke"], 1e30, -1e30,
                        op0=ALU.mult, op1=ALU.add))
                vsync()
                for j in range(4):
                    W = WGS[j]
                    v_op(lambda v, W=W: v.tensor_add(W["s2k"], W["s2"],
                                                     W["ke"]))
                vsync()
                for j in range(4):
                    W = WGS[j]
                    v_op(lambda v, W=W: v.reduce_max(W["m1"], W["s2k"],
                                                     axis=AX.X))
                vsync()
                for j in range(4):
                    W = WGS[j]
                    v_op(lambda v, W=W: v.tensor_scalar(
                        W["e1"], W["s2k"], W["m1"], None, op0=ALU.is_equal))
                vsync()
                for j in range(4):
                    W = WGS[j]
                    v_op(lambda v, W=W: v.scalar_tensor_tensor(
                        W["tmp"], W["e1"], -1e30, W["s2k"],
                        op0=ALU.mult, op1=ALU.add))
                vsync()
                for j in range(4):
                    W = WGS[j]
                    v_op(lambda v, W=W: v.reduce_max(W["m2"], W["tmp"],
                                                     axis=AX.X))
                vsync()
                for j in range(4):
                    W = WGS[j]
                    v_op(lambda v, W=W: v.tensor_scalar(
                        W["e2"], W["tmp"], W["m2"], None, op0=ALU.is_equal))
                vsync()
                for j in range(4):
                    W = WGS[j]
                    v_op(lambda v, W=W: v.tensor_add(W["e1"], W["e1"],
                                                     W["e2"]))
                vsync()
                for j in range(4):
                    W = WGS[j]
                    v_op(lambda v, W=W, j=j: v.tensor_mul(
                        CT[:, j * 16: j * 16 + 16], W["p0"], W["e1"]))
                vsync()
                em.wait_now("tensor", chV)
                for j in range(4):
                    em.bank_acquire("ptf")
                    t_tr(PTF[0:16, 0:128], CT[:, j * 16: j * 16 + 16],
                         IDF[:, 0:128], apl=[em.mark_ready("ptf", chT)])
                    em.wait_ready("scalar", "ptf")
                    s_act(CTROW[0:16, j * 128: j * 128 + 128],
                          PTF[0:16, 0:128], AF.Copy)
                    em.free_after("ptf", "scalar")
                em.wait_now("vector", chS)
                v_op(lambda v: v.tensor_copy(CTB[:], CTROW[:]))

                shtok = {}

                def sh_load(it):
                    sl = it % 2
                    base = 4096 + sl * 1024
                    if it < 2:
                        if w2hi_guard[0]:
                            em.wait_val("sync", chT, w2hi_guard[0])
                    elif shguard[sl]:
                        em.wait_val("sync", chT, shguard[sl])
                    ch = chD[sl]
                    dma("sync", W2[:, base:base + 1024].rearrange(
                        "p (kt c) -> p kt c", kt=8),
                        dstg["ws1_b"][it].rearrange("kt p c -> p kt c"), ch)
                    dma("sync", W2[:, 6144 + sl * 1024: 7168 + sl * 1024]
                        .rearrange("p (kt c) -> p kt c", kt=8),
                        dstg["ws3_b"][it].rearrange("kt p c -> p kt c"), ch)
                    shtok[it] = (ch, ch.n)

                shguard = [0, 0]
                sh_load(0)
                for it in range(8):
                    sl = it % 2
                    base = 4096 + sl * 1024
                    if it + 1 < 8:
                        sh_load(it + 1)
                    em.wait_tok("tensor", shtok[it])
                    b1, b3 = (0, 1) if it % 2 == 0 else (4, 5)
                    em.bank_acquire(b1)
                    em.bank_acquire(b3)
                    for kt in range(8):
                        apl = [em.mark_ready(b1, chT)] if kt == 7 else ()
                        t_mm(PS[b1][:, 0:512],
                             W2[:, base + kt * 128: base + kt * 128 + 128],
                             A3[:, kt * 512: kt * 512 + 512],
                             start=(kt == 0), stop=(kt == 7), apl=apl)
                    for kt in range(8):
                        apl = [em.mark_ready(b3, chT)] if kt == 7 else ()
                        t_mm(PS[b3][:, 0:512],
                             W2[:, 6144 + sl * 1024 + kt * 128:
                                6144 + sl * 1024 + kt * 128 + 128],
                             A3[:, kt * 512: kt * 512 + 512],
                             start=(kt == 0), stop=(kt == 7), apl=apl)
                    shguard[sl] = chT.n
                    em.wait_ready("scalar", b1)
                    sa = SCB[:, (it % 2) * 1024: (it % 2) * 1024 + 512]
                    if scb_ffn_guard[it % 2]:
                        em.wait_val("scalar", chV, scb_ffn_guard[it % 2])
                    if scb_guard[it % 2]:
                        em.wait_val("scalar", chT, scb_guard[it % 2])
                    s_act(sa, PS[b1][:, 0:512], AF.Silu,
                          bias=CC[:, 56 + it:57 + it])
                    em.free_after(b1, "scalar")
                    em.wait_ready("vector", b3)
                    em.wait_now("vector", chS)
                    v_op(lambda v, it=it, sa=sa, b3=b3: v.scalar_tensor_tensor(
                        W2[:, it * 512: it * 512 + 512], PS[b3][:, 0:512],
                        CC[:, 72 + it:73 + it], sa,
                        op0=ALU.add, op1=ALU.mult))
                    em.free_after(b3, "vector")
                    scb_ffn_guard[it % 2] = chV.n
                eslot_guard[2] = max(eslot_guard[2], chT.n)

                slots = [W1[:, 0:4096], W1[:, 4096:8192], W2[:, 4096:8192]]
                etok = {}

                def e_load(li, nm, e2):
                    if eslot_guard[li % 3]:
                        em.wait_val("sync", chT, eslot_guard[li % 3])
                    ch = chD[4 + li % 3]
                    dma("sync", slots[li % 3].rearrange(
                        "p (kt c) -> p kt c", kt=8),
                        dstg[nm][e2].rearrange("kt p c -> p kt c"), ch)
                    etok[li] = (ch, ch.n)

                em.wait_now("tensor", chV)
                e_load(0, "we1_b", 0)
                e_load(1, "we3_b", 0)
                h1banks = (0, 1, 4, 5)
                h3banks = (2, 3)
                for e in range(NE):
                    if e + 1 < NE:
                        e_load(2 * (e + 1), "we1_b", e + 1)
                    em.bank_acquire(2)
                    t_mm(PS[2][:, 0:512],
                         EXHB[:, e * 128: e * 128 + 128],
                         CTB[:, 0:512], start=True, stop=True,
                         apl=[em.mark_ready(2, chT)])
                    em.wait_ready("scalar", 2)
                    bcb = A3[:, 4096:4608]
                    em.wait_now("scalar", chV)
                    s_act(bcb, PS[2][:, 0:512], AF.Copy)
                    em.free_after(2, "scalar")
                    s1 = slots[(2 * e) % 3]
                    s3 = slots[(2 * e + 1) % 3]
                    em.wait_tok("tensor", etok[2 * e])
                    for it in range(4):
                        b1 = h1banks[it]
                        em.bank_acquire(b1)
                        for kt in range(8):
                            apl = [em.mark_ready(b1, chT)] if kt == 7 else ()
                            t_mm(PS[b1][:, 0:512],
                                 s1[:, kt * 512 + it * 128:
                                    kt * 512 + it * 128 + 128],
                                 A3[:, kt * 512: kt * 512 + 512],
                                 start=(kt == 0), stop=(kt == 7), apl=apl)
                    eslot_guard[(2 * e) % 3] = chT.n
                    if e + 1 < NE:
                        e_load(2 * (e + 1) + 1, "we3_b", e + 1)
                    em.wait_tok("tensor", etok[2 * e + 1])
                    for it in range(4):
                        b1 = h1banks[it]
                        b3 = h3banks[it % 2]
                        em.bank_acquire(b3)
                        for kt in range(8):
                            apl = [em.mark_ready(b3, chT)] if kt == 7 else ()
                            t_mm(PS[b3][:, 0:512],
                                 s3[:, kt * 512 + it * 128:
                                    kt * 512 + it * 128 + 128],
                                 A3[:, kt * 512: kt * 512 + 512],
                                 start=(kt == 0), stop=(kt == 7), apl=apl)
                        par2 = it % 2
                        sa = SCB[:, par2 * 1024: par2 * 1024 + 512]
                        sb = SCB[:, par2 * 1024 + 512: par2 * 1024 + 1024]
                        em.wait_ready("scalar", b1)
                        if scb_ffn_guard[par2]:
                            em.wait_val("scalar", chV, scb_ffn_guard[par2])
                        if scb_guard[par2]:
                            em.wait_val("scalar", chT, scb_guard[par2])
                        ei = e * 4 + it
                        s_act(sa, PS[b1][:, 0:512], AF.Silu,
                              bias=CC[:, 80 + ei:81 + ei])
                        em.free_after(b1, "scalar")
                        em.wait_ready("scalar", b3)
                        s_act(sb, PS[b3][:, 0:512], AF.Identity,
                              bias=CC[:, 144 + ei:145 + ei])
                        em.free_after(b3, "scalar")
                        em.wait_now("vector", chS)
                        n = e * 4 + it
                        if n < 32:
                            chdst = A1[:, n * 512: n * 512 + 512]
                        else:
                            chdst = A2[:, (n - 32) * 512: (n - 32) * 512 + 512]
                        v_op(lambda v, sa=sa, bcb=bcb: v.tensor_mul(
                            sa, sa, bcb))
                        v_op(lambda v, chdst=chdst, sa=sa, sb=sb:
                             v.tensor_mul(chdst, sa, sb))
                        scb_ffn_guard[par2] = chV.n
                    eslot_guard[(2 * e + 1) % 3] = chT.n

                em.wait_now("tensor", chV)
                ybuf = [W2[:, 4096:8192], W1[:, 2048:6144]]
                ybuf_guard = [eslot_guard[2],
                              max(eslot_guard[0], eslot_guard[1])]
                ws2_guard = [max(eslot_guard[0], eslot_guard[1]), 0]
                ytok = {}
                wtok = {}

                def y_load(s):
                    dt2, half = s // 2, s % 2
                    if ybuf_guard[s % 2]:
                        em.wait_val("sync", chT, ybuf_guard[s % 2])
                    ch = chD[6 + s % 2]
                    dma("sync",
                        ybuf[s % 2].rearrange(
                            "p (e it c) -> p e it c", e=8, it=4),
                        dstg["we2_b"][dt2, 8 * half:8 * half + 8]
                        .rearrange("e it p c -> p e it c"), ch)
                    ytok[s] = (ch, ch.n)

                def ws2_load(dt2):
                    sl = dt2 % 2
                    if ws2_guard[sl]:
                        em.wait_val("sync", chT, ws2_guard[sl])
                    ch = chD[2 + sl]
                    dma("sync", W1[:, sl * 1024: sl * 1024 + 1024]
                        .rearrange("p (it c) -> p it c", it=8),
                        dstg["ws2_b"][dt2].rearrange("it p c -> p it c"), ch)
                    wtok[dt2] = (ch, ch.n)

                y_load(0)
                ws2_load(0)
                y_load(1)
                for dt in range(8):
                    em.bank_acquire(2)
                    for s in (2 * dt, 2 * dt + 1):
                        if s % 2 == 1 and dt + 1 < 8:
                            ws2_load(dt + 1)
                        em.wait_tok("tensor", ytok[s])
                        for n0 in range(32):
                            n = (s % 2) * 32 + n0
                            if n < 32:
                                chsrc = A1[:, n * 512: n * 512 + 512]
                            else:
                                chsrc = A2[:, (n - 32) * 512:
                                           (n - 32) * 512 + 512]
                            apl = [em.mk_inc(chT, 1)] if n0 == 31 else ()
                            t_mm(PS[2][:, 0:512],
                                 ybuf[s % 2][:, n0 * 128: n0 * 128 + 128],
                                 chsrc, start=(s % 2 == 0 and n0 == 0),
                                 stop=False, apl=apl)
                        ybuf_guard[s % 2] = chT.n
                        if s + 2 < 16:
                            y_load(s + 2)
                    t_mm(PS[2][:, 0:512], BE2[:, dt * 128: dt * 128 + 128],
                         CTB[:, 0:512], start=False, stop=False)
                    em.wait_tok("tensor", wtok[dt])
                    for it in range(8):
                        last = it == 7
                        apl = [em.mark_ready(2, chT)] if last else ()
                        t_mm(PS[2][:, 0:512],
                             W1[:, (dt % 2) * 1024 + it * 128:
                                (dt % 2) * 1024 + it * 128 + 128],
                             W2[:, it * 512: it * 512 + 512],
                             start=False, stop=last, apl=apl)
                    ws2_guard[dt % 2] = chT.n
                    em.wait_ready("vector", 2)
                    v_op(lambda v, dt=dt, c=c: v.scalar_tensor_tensor(
                        Rt(dt, c), PS[2][:, 0:512], CC[:, 64 + dt:65 + dt],
                        Rt(dt, c), op0=ALU.add, op1=ALU.add))
                    em.free_after(2, "vector")
                eslot_guard[0] = chT.n
                eslot_guard[1] = chT.n
                eslot_guard[2] = chT.n
                w2hi_guard[0] = chT.n
            em.barrier(("tensor", "vector", "scalar", "sync"))

        def phase_attn_b(b):
            TB = 256
            NCB = T // TB

            def vsync():
                ap = em.mk_inc(chV, 1)
                em.on("vector", lambda v, ap=ap: ap(v.engine_nop()))
                em.wait_now("vector", chV)

            sdummy = SCF[0:1, 2238:2239]

            def ssync():
                ap = em.mk_inc(chS, 1)
                em.on("scalar", lambda s, ap=ap: ap(
                    s.activation(out=sdummy, in_=sdummy, func=AF.Copy)))
                em.wait_now("scalar", chS)

            em.wait_now("sync", chSTB)
            v_op(lambda v: v.memset(CAB[:], 0.0))
            v_op(lambda v: v.memset(SS[:], 0.0))
            v_op(lambda v: v.memset(SQ[:], 0.0))
            vsync()
            for c in range(NCH):
                ln_chunk(c, lambda dt, c=c: A1[:, dt * 2048 + c * TC:
                                               dt * 2048 + c * TC + TC])
            em.barrier(("tensor", "vector", "scalar", "sync", "gpsimd"))

            EK = SCF[:, 0:256]
            A_ = SCF[:, 256:512]
            SA = SCF[:, 512:800]
            SB = SCF[:, 800:1088]
            WREP = SCF[:, 1088:1344]
            NN = SCF[:, 1344:1600]
            DD = SCF[:, 1600:1856]
            RR = SCF[:, 1856:2112]
            ss_tmp = SCF[:, 2232:2233]
            sq_tmp = SCF[:, 2233:2234]

            wslot_guard = [0, 0]
            wtok = {}
            mm_done = {}

            def rkv_load(cb, g):
                sl = g % 2
                base = sl * 3072
                if wslot_guard[sl]:
                    em.wait_val("sync", chT, wslot_guard[sl])
                ch = chD[4 + sl]
                for wi, nm in ((0, "wr_b"), (1, "wk_b"), (2, "wv_b")):
                    dma("sync", W1[:, base + wi * 1024:
                                   base + wi * 1024 + 1024].rearrange(
                                       "p (kt c) -> p kt c", kt=8),
                        dstg[nm][g].rearrange("kt p c -> p kt c"), ch)
                wtok[g] = (ch, ch.n)

            for cb in range(NCB):
                em.wait_now("vector", chT)
                for mi, moff in ((0, 0), (1, 2048), (2, 4096)):
                    for dt in range(NDT):
                        xt = A1[:, dt * 2048 + cb * TB:
                                dt * 2048 + cb * TB + TB]
                        dst = A3[:, moff + dt * TB: moff + dt * TB + TB]
                        v_op(lambda v, xt=xt, dst=dst, mi=mi, dt=dt:
                             v.tensor_scalar(dst, xt,
                                             CC[:, 8 * mi + dt:
                                                8 * mi + dt + 1],
                                             None, op0=ALU.mult))
                for mi, moff in ((0, 0), (1, 2048), (2, 4096)):
                    for dt in range(NDT):
                        dst = A3[:, moff + dt * TB: moff + dt * TB + TB]
                        if cb == 0:
                            xp = A1[:, dt * 2048: dt * 2048 + TB - 1]
                            v_op(lambda v, xp=xp, dst=dst, mi=mi, dt=dt:
                                 v.scalar_tensor_tensor(
                                     dst[:, 1:TB], xp,
                                     DC[:, 24 + 8 * mi + dt:
                                        25 + 8 * mi + dt],
                                     dst[:, 1:TB],
                                     op0=ALU.mult, op1=ALU.add))
                        else:
                            xp = A1[:, dt * 2048 + cb * TB - 1:
                                    dt * 2048 + cb * TB + TB - 1]
                            v_op(lambda v, xp=xp, dst=dst, mi=mi, dt=dt:
                                 v.scalar_tensor_tensor(
                                     dst, xp,
                                     DC[:, 24 + 8 * mi + dt:
                                        25 + 8 * mi + dt],
                                     dst, op0=ALU.mult, op1=ALU.add))
                ap_mix = em.mk_inc(chV, 1)
                em.on("vector", lambda v, ap=ap_mix: ap(v.engine_nop()))
                mix_tok = chV.n

                def emit_mms(g):
                    banks = (0, 1, 2) if g % 2 == 0 else (3, 4, 5)
                    sl = g % 2
                    base = sl * 3072
                    em.wait_tok("tensor", wtok[g])
                    em.wait_val("tensor", chV, mix_tok)
                    for bi2, (wi, src_off) in enumerate(
                            ((0, 0), (1, 2048), (2, 4096))):
                        bank = banks[bi2]
                        em.bank_acquire(bank)
                        for kt in range(8):
                            apl = ([em.mark_ready(bank, chT)]
                                   if kt == 7 else ())
                            t_mm(PS[bank][:, 0:TB],
                                 W1[:, base + wi * 1024 + kt * 128:
                                    base + wi * 1024 + kt * 128 + 128],
                                 A3[:, src_off + kt * TB:
                                    src_off + kt * TB + TB],
                                 start=(kt == 0), stop=(kt == 7), apl=apl)
                    wslot_guard[sl] = chT.n
                    mm_done[g] = chT.n

                def emit_chain(g):
                    banks = (0, 1, 2) if g % 2 == 0 else (3, 4, 5)
                    bR, bK, bV = banks
                    em.wait_ready("scalar", bK)
                    em.wait_now("scalar", chV)
                    s_act(EK, PS[bK][:, 0:TB], AF.Exp)
                    em.free_after(bK, "scalar")
                    em.wait_ready("scalar", bR)
                    s_act(RR, PS[bR][:, 0:TB], AF.Exp, scale=-1.0)
                    em.free_after(bR, "scalar")
                    s_act(RR, RR, AF.Ln, bias=1.0)
                    s_act(RR, RR, AF.Exp, scale=-1.0)
                    em.wait_ready("vector", bV)
                    em.wait_now("vector", chS)
                    v_op(lambda v: v.memset(WREP, 1.0))
                    v_op(lambda v, g=g, bV=bV: v.tensor_mul(
                        A_, EK, PS[bV][:, 0:TB]))
                    em.free_after(bV, "vector")
                    v_op(lambda v, g=g: v.tensor_scalar(
                        WREP, WREP, DC[:, g:g + 1], None, op0=ALU.mult))
                    vsync()
                    v_op(lambda v, g=g: v.tensor_tensor_scan(
                        SA[:, 1:257], WREP, A_, CAB[:, g:g + 1],
                        op0=ALU.mult, op1=ALU.add))
                    v_op(lambda v, g=g: v.tensor_tensor_scan(
                        SB[:, 1:257], WREP, EK, CAB[:, 8 + g:9 + g],
                        op0=ALU.mult, op1=ALU.add))
                    v_op(lambda v, g=g: v.tensor_copy(SA[:, 0:1],
                                                      CAB[:, g:g + 1]))
                    v_op(lambda v, g=g: v.tensor_copy(SB[:, 0:1],
                                                      CAB[:, 8 + g:9 + g]))
                    vsync()
                    v_op(lambda v, g=g: v.tensor_copy(CAB[:, g:g + 1],
                                                      SA[:, 256:257]))
                    v_op(lambda v, g=g: v.tensor_copy(CAB[:, 8 + g:9 + g],
                                                      SB[:, 256:257]))
                    v_op(lambda v, g=g: v.tensor_scalar(
                        EK, SA[:, 1:257], DC[:, 8 + g:9 + g], None,
                        op0=ALU.mult))
                    v_op(lambda v, g=g: v.tensor_scalar(
                        A_, SB[:, 1:257], DC[:, 8 + g:9 + g], 1e-9,
                        op0=ALU.mult, op1=ALU.add))
                    vsync()
                    v_op(lambda v, g=g: v.scalar_tensor_tensor(
                        NN, SA[:, 0:256], DC[:, 16 + g:17 + g], EK,
                        op0=ALU.mult, op1=ALU.add))
                    v_op(lambda v, g=g: v.scalar_tensor_tensor(
                        DD, SB[:, 0:256], DC[:, 16 + g:17 + g], A_,
                        op0=ALU.mult, op1=ALU.add))
                    vsync()
                    em.wait_now("scalar", chV)
                    s_act(DD, DD, AF.Ln)
                    s_act(DD, DD, AF.Exp, scale=-1.0)
                    em.wait_now("vector", chS)
                    v_op(lambda v: v.tensor_mul(NN, NN, DD))
                    vsync()
                    v_op(lambda v: v.tensor_mul(A_, NN, RR))
                    vsync()
                    odst = A2[:, g * 2048 + cb * TB:
                              g * 2048 + cb * TB + TB]
                    v_op(lambda v, odst=odst: v.tensor_scalar(
                        odst, A_, 1.0, None, op0=ALU.mult, op1=ALU.add,
                        accum_out=ss_tmp))
                    v_op(lambda v: v.scalar_tensor_tensor(
                        DD, A_, 1.0, A_, op0=ALU.mult, op1=ALU.mult,
                        accum_out=sq_tmp))
                    vsync()
                    v_op(lambda v, g=g: v.tensor_add(
                        SS[:, g:g + 1], SS[:, g:g + 1], ss_tmp))
                    v_op(lambda v, g=g: v.tensor_add(
                        SQ[:, g:g + 1], SQ[:, g:g + 1], sq_tmp))
                    vsync()

                if cb == 0:
                    rkv_load(cb, 0)
                emit_mms(0)
                rkv_load(cb, 1)
                for g in range(8):
                    if g + 1 < 8:
                        emit_mms(g + 1)
                        if g + 2 < 8:
                            rkv_load(cb, g + 2)
                        elif cb + 1 < NCB:
                            rkv_load(cb + 1, 0)
                    emit_chain(g)

            em.barrier(("tensor", "vector", "scalar", "sync"))
            em.bank_acquire("ptf")
            t_mm(PTF[0:2, 0:8], HSEL[:], SS[:], start=True, stop=True)
            t_mm(PTF[0:2, 8:16], HSEL[:], SQ[:], start=True, stop=True,
                 apl=[em.mark_ready("ptf", chT)])
            em.wait_ready("scalar", "ptf")
            HSROW = SCF[0:2, 2240:2256]
            s_act(HSROW, PTF[0:2, 0:16], AF.Copy)
            em.free_after("ptf", "scalar")
            hm = SCF[0:2, 2256:2264]
            hv = SCF[0:2, 2264:2272]
            em.wait_now("vector", chS)
            NINV = 1.0 / (64.0 * 2048.0)
            v_op(lambda v: v.tensor_scalar(hm, HSROW[0:2, 0:8], NINV, None,
                                           op0=ALU.mult))
            v_op(lambda v: v.tensor_scalar(hv, HSROW[0:2, 8:16], NINV, None,
                                           op0=ALU.mult))
            vsync()
            hm2 = SCF[0:2, 2272:2280]
            v_op(lambda v: v.tensor_mul(hm2, hm, hm))
            vsync()
            v_op(lambda v: v.tensor_sub(hv, hv, hm2))
            vsync()
            em.wait_now("scalar", chV)
            s_act(hv, hv, AF.Ln, bias=CC[0:2, 208:209])
            ssync()
            s_act(hv, hv, AF.Exp, scale=-0.5)
            em.wait_now("tensor", chS)
            em.bank_acquire("ptf")
            t_mm(PTF[:, 0:8], HSELT[:], hm, start=True, stop=True)
            t_mm(PTF[:, 8:16], HSELT[:], hv, start=True, stop=True,
                 apl=[em.mark_ready("ptf", chT)])
            em.wait_ready("vector", "ptf")
            v_op(lambda v: v.tensor_mul(SCL[:], PTF[:, 8:16],
                                        CC[:, 40:48]))
            em.free_after("ptf", "vector")
            vsync()
            tb8 = SCF[:, 2280:2288]
            v_op(lambda v: v.tensor_mul(tb8, PTF[:, 0:8], SCL[:]))
            vsync()
            v_op(lambda v: v.tensor_sub(BIA[:], CC[:, 48:56], tb8))
            vsync()
            for g in range(8):
                for c in range(NCH):
                    src_t = A2[:, g * 2048 + c * TC: g * 2048 + c * TC + TC]
                    dst_t = A1[:, g * 2048 + c * TC: g * 2048 + c * TC + TC]
                    v_op(lambda v, src_t=src_t, dst_t=dst_t, g=g:
                         v.tensor_scalar(dst_t, src_t, SCL[:, g:g + 1],
                                         BIA[:, g:g + 1], op0=ALU.mult,
                                         op1=ALU.add))
            vsync()
            woguard = [0, 0]
            for et in range(8):
                sl = et % 2
                if woguard[sl]:
                    em.wait_val("sync", chT, woguard[sl])
                ch = chD[2 + sl]
                dma("sync", W2[:, sl * 1024: sl * 1024 + 1024].rearrange(
                    "p (kt c) -> p kt c", kt=8),
                    dstg["wo_b"][et].rearrange("kt p c -> p kt c"), ch)
                em.wait_now("tensor", ch)
                em.wait_now("tensor", chV)
                for c in range(NCH):
                    bank = 4 + c % 2
                    em.bank_acquire(bank)
                    for kt in range(8):
                        last = kt == 7
                        apl = [em.mark_ready(bank, chT)] if last else ()
                        t_mm(PS[bank][:, 0:512],
                             W2[:, sl * 1024 + kt * 128:
                                sl * 1024 + kt * 128 + 128],
                             A1[:, kt * 2048 + c * TC:
                                kt * 2048 + c * TC + TC],
                             start=(kt == 0), stop=last, apl=apl)
                    em.wait_ready("vector", bank)
                    v_op(lambda v, et=et, c=c, bank=bank: v.tensor_add(
                        Rt(et, c), Rt(et, c), PS[bank][:, 0:512]))
                    em.free_after(bank, "vector")
                woguard[sl] = chT.n
            em.barrier()

        for b in range(NB):
            if b == 0:
                for e in ("scalar", "vector", "tensor"):
                    em.wait_now(e, chSX)
            else:
                dma("gpsimd", R[:].rearrange("p (dt t) -> p dt t", dt=8),
                    x_in[b].rearrange("(dt p) t -> p dt t", p=128), chG)
                em.barrier()
            if "a" in phases:
                phase_attn_a(b)
            if "f1" in phases:
                phase_ffn(b)
            if "b" in phases:
                phase_attn_b(b)
            if "f2" in phases:
                phase_ffn(b)
            ap = em.mk_inc(chO, 16)
            dstp = y_out[b].rearrange("(dt p) t -> p dt t", p=128)
            em.on("sync", lambda e, d=dstp, ap=ap: ap(
                e.dma_start(out=d,
                            in_=R[:].rearrange("p (dt t) -> p dt t", dt=8))))
            em.barrier()

        for nm, out_ap in dump_outs.items():
            buf = {"R": R, "A1": A1, "A2": A2, "A3": A3, "SCF": SCF,
                   "CT": CT, "SS": SS, "SQ": SQ, "DC": DC}[nm]
            ap = em.mk_inc(chO, 16)
            em.on("sync", lambda e, bu=buf, o=out_ap, ap=ap: ap(
                e.dma_start(out=o, in_=bu[:])))
        em.wait_now("sync", chO)

        em.run(block)
    return nc



def _prep_shared_inputs(kw):
    f = np.float32
    out = {}
    wqkv = np.asarray(kw["L0_attn_a_w_qkv"], f)
    cols = []
    for s in range(8):
        for part in range(3):
            for hh in (2 * s, 2 * s + 1):
                base = part * 1024 + hh * 64
                cols.append(np.arange(base, base + 64))
    perm = np.concatenate(cols)
    wq = wqkv[:, perm].reshape(8, 128, 8, 384)
    out["wqkv"] = np.ascontiguousarray(wq.transpose(2, 0, 1, 3))

    def tile_oi(w):
        return np.ascontiguousarray(
            w.reshape(8, 128, 8, 128).transpose(2, 0, 1, 3))

    out["wout"] = tile_oi(np.asarray(kw["L0_attn_a_w_out"], f))
    out["wr"] = tile_oi(np.asarray(kw["L0_attn_b_W_r"], f).T)
    out["wk"] = tile_oi(np.asarray(kw["L0_attn_b_W_k"], f).T)
    out["wv"] = tile_oi(np.asarray(kw["L0_attn_b_W_v"], f).T)
    out["wo"] = tile_oi(np.asarray(kw["L0_attn_b_W_o"], f).T)
    out["we1"] = np.ascontiguousarray(
        np.asarray(kw["L0_ffn_We1"], f).reshape(16, 8, 128, 512))
    out["we3"] = np.ascontiguousarray(
        np.asarray(kw["L0_ffn_We3"], f).reshape(16, 8, 128, 512))
    out["we2"] = np.ascontiguousarray(
        np.asarray(kw["L0_ffn_We2"], f).reshape(16, 4, 128, 8, 128)
        .transpose(3, 0, 1, 2, 4))
    out["ws1"] = tile_oi(np.asarray(kw["L0_ffn_ws1"], f))
    out["ws3"] = tile_oi(np.asarray(kw["L0_ffn_ws3"], f))
    out["ws2"] = tile_oi(np.asarray(kw["L0_ffn_ws2"], f))
    gwt = np.asarray(kw["L0_ffn_gate_w"], f).T
    out["gwt"] = np.ascontiguousarray(
        gwt.reshape(8, 128, 16).transpose(1, 0, 2).reshape(128, 128))
    out["be2m"] = np.asarray(kw["L0_ffn_be2"], f)
    out["gateb"] = np.asarray(kw["L0_ffn_gate_b"], f).reshape(1, 16)

    cc = np.zeros((128, 224), f)

    def pack8(vec):
        return np.asarray(vec, f).reshape(8, 128).T

    cc[:, 0:8] = pack8(kw["L0_attn_b_time_mix_r"])
    cc[:, 8:16] = pack8(kw["L0_attn_b_time_mix_k"])
    cc[:, 16:24] = pack8(kw["L0_attn_b_time_mix_v"])
    cc[:, 24:32] = pack8(np.asarray(kw["L0_attn_b_time_decay"], f).reshape(-1))
    cc[:, 32:40] = pack8(np.asarray(kw["L0_attn_b_time_first"], f).reshape(-1))
    cc[:, 40:48] = pack8(kw["L0_attn_b_gn_weight"])
    cc[:, 48:56] = pack8(kw["L0_attn_b_gn_bias"])
    cc[:, 56:64] = pack8(kw["L0_ffn_bs1"])
    cc[:, 64:72] = pack8(kw["L0_ffn_bs2"])
    cc[:, 72:80] = pack8(kw["L0_ffn_bs3"])
    be1 = np.asarray(kw["L0_ffn_be1"], f)
    be3 = np.asarray(kw["L0_ffn_be3"], f)
    cc[:, 80:144] = be1.reshape(16, 4, 128).transpose(2, 0, 1).reshape(128, 64)
    cc[:, 144:208] = be3.reshape(16, 4, 128).transpose(2, 0, 1).reshape(128, 64)
    cc[:, 208] = 1e-5
    cc[:, 209] = 1e-9
    out["cc"] = cc

    out["idf"] = np.eye(128, dtype=f)
    out["ones128"] = np.ones((128, 128), f)
    out["onesrow"] = np.ones((1, 128), f)
    lo = np.zeros((1, 128), f)
    lo[0, 0:64] = 1.0
    out["oneslo"] = lo
    hi = np.zeros((1, 128), f)
    hi[0, 64:128] = 1.0
    out["oneshi"] = hi
    out["stair"] = (np.arange(896)[None, :] >=
                    (np.arange(128)[:, None] + 384)).astype(f)
    out["exh"] = np.kron(np.eye(16, dtype=f), np.ones((1, 128), f))
    hsel = np.zeros((128, 2), f)
    hsel[0:64, 0] = 1.0
    hsel[64:128, 1] = 1.0
    out["hsel"] = hsel
    out["hselt"] = np.ascontiguousarray(hsel.T)
    return out


def _run(inputs, phases=("a", "f1", "b", "f2"), dumps=(), cores=8):
    x = np.asarray(inputs["L0_attn_a_x"], np.float32)
    shared = _prep_shared_inputs(inputs)
    nc = build_program(phases=phases, dumps=dumps)
    in_maps = []
    for core in range(cores):
        m = dict(shared)
        m["x"] = np.ascontiguousarray(
            x[core * NB:(core + 1) * NB].transpose(0, 2, 1))
        in_maps.append(m)
    res = run_bass_kernel_spmd(nc, in_maps, core_ids=list(range(cores)))
    return res


def kernel(**inputs):
    res = _run(inputs)
    outs = []
    for core in range(8):
        y = np.asarray(res.results[core]["y"])
        outs.append(np.ascontiguousarray(y.transpose(0, 2, 1)))
    return np.concatenate(outs, axis=0)


# version 08 (final)
# speedup vs baseline: 1.6105x; 1.0280x over previous
import contextlib
import numpy as np

import concourse.bass as bass
import concourse.mybir as mybir
from concourse.bass_utils import run_bass_kernel_spmd

F32 = mybir.dt.float32
BF16 = mybir.dt.bfloat16
AF = mybir.ActivationFunctionType
ALU = mybir.AluOpType
AX = mybir.AxisListType

D = 1024
T = 2048
H = 16
HD = 64
NE = 16
INTER = 512
NB = 2
NDT = 8
TC = 512
NCH = T // TC
EPS = 1e-5

ENGINES = ("sync", "gpsimd", "vector", "scalar", "tensor")


class Ch:
    def __init__(self, handle):
        self.h = handle
        self.n = 0


class Emitter:

    def __init__(self, sems):
        self.q = {e: [] for e in ENGINES}
        self.ch = {name: Ch(h) for name, h in sems.items()}
        self.bar = {e: self.ch["bar_" + e] for e in ENGINES}
        self.bank_free = {}
        self.bank_ready = {}

    def on(self, eng, fn):
        self.q[eng].append(fn)

    def mk_inc(self, ch, k=1):
        ch.n += k
        h = ch.h

        def apply(instr):
            instr.then_inc(h, k)
            return instr
        return apply

    def tok(self, ch):
        ap = self.mk_inc(ch, 1)
        return ap, (ch, ch.n)

    def wait_tok(self, eng, tok):
        ch, v = tok
        self.wait_val(eng, ch, v)

    def wait_now(self, eng, ch):
        self.wait_val(eng, ch, ch.n)

    def wait_val(self, eng, ch, v):
        h = ch.h
        if v > 0:
            self.on(eng, lambda e: e.wait_ge(h, v))

    def barrier(self, engines=ENGINES):
        dma_chs = {"sync": tuple(["w", "o"] + ["dma%d" % i
                                                for i in range(8)]),
                   "gpsimd": ("g", "sx", "sta", "stb", "stall")}
        for e in engines:
            for cn in dma_chs.get(e, ()):
                self.wait_now(e, self.ch[cn])
        toks = {}
        for e in engines:
            ap = self.mk_inc(self.bar[e], 1)
            self.on(e, lambda eng, ap=ap: ap(eng.nop()))
            toks[e] = self.bar[e].n
        for e in engines:
            for o in engines:
                if o != e:
                    self.wait_val(e, self.bar[o], toks[o])

    def bank_acquire(self, bank):
        if bank in self.bank_free:
            self.wait_tok("tensor", self.bank_free[bank])

    def mark_ready(self, bank, ch):
        ap, tk = self.tok(ch)
        self.bank_ready[bank] = tk
        return ap

    def wait_ready(self, eng, bank):
        self.wait_tok(eng, self.bank_ready[bank])

    def mark_free(self, bank, eng):
        chmap = {"vector": "v", "scalar": "s", "gpsimd": "g", "sync": "w"}
        ap, tk = self.tok(self.ch[chmap[eng]])
        self.bank_free[bank] = tk
        return ap

    def free_after(self, bank, eng):
        chmap = {"vector": "v", "scalar": "s", "gpsimd": "g", "sync": "w"}
        ch = self.ch[chmap[eng]]
        self.bank_free[bank] = (ch, ch.n)

    def run(self, block):
        for e in ENGINES:
            fns = self.q[e]

            def body(eng, fns=fns):
                for fn in fns:
                    fn(eng)
            getattr(block, e)(body)


def build_program(phases=("a", "f1", "b", "f2"), dumps=()):
    nc = bass.Bass("TRN2", target_bir_lowering=False)

    din = {}

    def I(name, shape, dt=F32):
        din[name] = nc.dram_tensor(name, list(shape), dt,
                                   kind="ExternalInput").ap()
        return din[name]

    x_in = I("x", (NB, D, T))
    I("wqkv", (8, 8, 128, 384))
    I("wout", (8, 8, 128, 128))
    for nm in ("wr", "wk", "wv", "wo"):
        I(nm, (8, 8, 128, 128))
    I("we1", (NE, 8, 128, 512))
    I("we3", (NE, 8, 128, 512))
    I("we2", (8, NE, 4, 128, 128))
    I("ws1", (8, 8, 128, 128))
    I("ws3", (8, 8, 128, 128))
    I("ws2", (8, 8, 128, 128))
    I("gwt", (128, 128))
    I("be2m", (16, 1024))
    I("gateb", (1, 16))
    I("cc", (128, 224))
    I("idf", (128, 128))
    I("ones128", (128, 128))
    I("onesrow", (1, 128))
    I("oneslo", (1, 128))
    I("oneshi", (1, 128))
    I("stair", (128, 896))
    I("hsel", (128, 2))
    I("exh", (16, 2048))
    I("hselt", (2, 128))

    y_out = nc.dram_tensor("y", [NB, D, T], F32, kind="ExternalOutput").ap()

    dstg = {}
    for nm, shp in (
        ("wqkv_b", (8, 8, 128, 384)), ("wout_b", (8, 8, 128, 128)),
        ("wr_b", (8, 8, 128, 128)), ("wk_b", (8, 8, 128, 128)),
        ("wv_b", (8, 8, 128, 128)), ("wo_b", (8, 8, 128, 128)),
        ("we1_b", (NE, 8, 128, 512)), ("we3_b", (NE, 8, 128, 512)),
        ("we2_b", (8, NE, 4, 128, 128)),
        ("ws1_b", (8, 8, 128, 128)), ("ws3_b", (8, 8, 128, 128)),
        ("ws2_b", (8, 8, 128, 128)),
    ):
        dstg[nm] = nc.dram_tensor(nm, list(shp), BF16, kind="Internal").ap()

    dump_outs = {}
    for nm in dumps:
        shp, dt = {"R": ([128, 16384], F32), "A1": ([128, 16384], BF16),
                   "A2": ([128, 16384], BF16), "A3": ([128, 6144], BF16),
                   "SCF": ([128, 4096], F32), "CT": ([128, 64], F32),
                   "SS": ([128, 8], F32), "SQ": ([128, 8], F32),
                   "DC": ([128, 64], F32)}[nm]
        dump_outs[nm] = nc.dram_tensor("dump_" + nm, shp, dt,
                                       kind="ExternalOutput").ap()

    es = contextlib.ExitStack()
    with es:
        sbt = lambda n, p, f, dt: es.enter_context(
            nc.sbuf_tensor(n, [p, f], dt))
        R = sbt("R", 128, 16384, F32)
        A1 = sbt("A1", 128, 16384, BF16)
        A2 = sbt("A2", 128, 16384, BF16)
        A3 = sbt("A3", 128, 6144, BF16)
        W1 = sbt("W1", 128, 8192, BF16)
        W2 = sbt("W2", 128, 8192, BF16)
        SCF = sbt("SCF", 128, 4096, F32)
        SC2 = sbt("SC2", 128, 512, F32)
        SCB = sbt("SCB", 128, 2048, BF16)
        GP = SCF[0:18, 3072:3584]
        GT = SCF[:, 2112:2184]
        CT = SCF[:, 2048:2112]
        CTROW = SCF[0:16, 3584:4096]
        CTB = sbt("CTB", 16, 512, BF16)
        CC = sbt("CC", 128, 224, F32)
        DC = sbt("DC", 128, 64, F32)
        GWT = sbt("GWT", 128, 128, F32)
        CS = sbt("CS", 128, 16, F32)
        BE2 = sbt("BE2", 16, 1024, BF16)
        GB = sbt("GB", 128, 16, F32)
        IDF = sbt("IDF", 128, 128, F32)
        IDB = sbt("IDB", 128, 128, BF16)
        ONES128B = sbt("ONES128B", 128, 128, BF16)
        ONESROWF = sbt("ONESROWF", 1, 128, F32)
        ONESROWB = sbt("ONESROWB", 1, 128, BF16)
        ONESLO = sbt("ONESLO", 1, 128, F32)
        ONESHI = sbt("ONESHI", 1, 128, F32)
        STAIRB = sbt("STAIRB", 128, 896, BF16)
        HSEL = sbt("HSEL", 128, 2, F32)
        EXHB = sbt("EXHB", 16, 2048, BF16)
        HSELT = sbt("HSELT", 2, 128, F32)
        SS = SCF[:, 2184:2192]
        SQ = SCF[:, 2192:2200]
        CAB = SCF[:, 2200:2216]
        SCL = SCF[:, 2216:2224]
        BIA = SCF[:, 2224:2232]

        PS = [es.enter_context(nc.psum_tensor("PB%d" % i, [128, 512], F32))
              for i in range(6)]
        PTB = es.enter_context(nc.psum_tensor("PTB", [128, 128], BF16))
        PTF = es.enter_context(nc.psum_tensor("PTF", [128, 128], F32))

        sems = {}
        for nm in (["w", "g", "t", "s", "v", "o", "sx", "sta", "stb",
                    "stall"] + ["bar_" + e for e in ENGINES]
                   + ["dma%d" % i for i in range(8)]):
            sems[nm] = es.enter_context(nc.semaphore("sem_" + nm))

        block = es.enter_context(nc.Block())
        em = Emitter(sems)
        chW, chG, chT, chS, chV, chO = (em.ch[k] for k in
                                        ("w", "g", "t", "s", "v", "o"))
        chD = [em.ch["dma%d" % i] for i in range(8)]

        def dma(eng, dst, src, ch):
            ap = em.mk_inc(ch, 16)
            em.on(eng, lambda e: ap(e.dma_start(out=dst, in_=src)))

        def s_act(out, in_, func, bias=0.0, scale=1.0, accum=None, apl=()):
            ap = em.mk_inc(chS, 1)

            def f(s):
                if accum is not None:
                    ins = s.activation(out=out, in_=in_, func=func, bias=bias,
                                       scale=scale, accum_out=accum)
                else:
                    ins = s.activation(out=out, in_=in_, func=func, bias=bias,
                                       scale=scale)
                ap(ins)
                for a in apl:
                    a(ins)
                return ins
            em.on("scalar", f)

        def v_op(fn, apl=()):
            ap = em.mk_inc(chV, 1)

            def f(v):
                ins = fn(v)
                ap(ins)
                for a in apl:
                    a(ins)
                return ins
            em.on("vector", f)

        import traceback

        def t_mm(out, lhsT, rhs, start, stop, apl=()):
            site = traceback.extract_stack(limit=3)[0]

            def f(t):
                try:
                    ins = t.matmul(out, lhsT, rhs, start=start, stop=stop)
                except Exception:
                    print("t_mm failed from %s:%d" % (site.filename,
                                                      site.lineno))
                    raise
                for a in apl:
                    a(ins)
                return ins
            em.on("tensor", f)

        def t_tr(out, in_, ident, apl=()):
            def f(t):
                ins = t.transpose(out, in_, ident)
                for a in apl:
                    a(ins)
                return ins
            em.on("tensor", f)

        def Rt(dt, c):
            return R[:, dt * 2048 + c * TC: dt * 2048 + c * TC + TC]

        dma("sync", CC[:], din["cc"][:], chW)
        dma("sync", GWT[:], din["gwt"][:], chW)
        dma("sync", IDF[:], din["idf"][:], chW)
        dma("sync", ONESROWF[:], din["onesrow"][:], chW)
        dma("sync", ONESLO[:], din["oneslo"][:], chW)
        dma("sync", ONESHI[:], din["oneshi"][:], chW)
        dma("sync", HSEL[:], din["hsel"][:], chW)
        dma("sync", HSELT[:], din["hselt"][:], chW)
        dma("sync", GB[:], din["gateb"].broadcast_to([128, 16]), chW)
        dma("gpsimd", STAIRB[:], din["stair"][:], chG)
        dma("gpsimd", ONES128B[:], din["ones128"][:], chG)
        dma("gpsimd", IDB[:], din["idf"][:], chG)
        dma("gpsimd", ONESROWB[:], din["onesrow"][:], chG)
        dma("gpsimd", BE2[:], din["be2m"][:], chG)
        dma("gpsimd", EXHB[:], din["exh"][:], chG)

        for e in ("scalar", "vector", "tensor"):
            em.wait_now(e, chW)
            em.wait_now(e, chG)

        s_act(DC[:, 0:8], CC[:, 24:32], AF.Exp)
        s_act(DC[:, 0:8], DC[:, 0:8], AF.Exp, scale=-1.0)
        s_act(DC[:, 8:16], CC[:, 32:40], AF.Exp)
        em.wait_now("vector", chS)

        def vsync0():
            ap = em.mk_inc(chV, 1)
            em.on("vector", lambda v, ap=ap: ap(v.engine_nop()))
            em.wait_now("vector", chV)

        v_op(lambda v: v.tensor_mul(DC[:, 16:24], DC[:, 0:8], DC[:, 8:16]))
        vsync0()
        v_op(lambda v: v.tensor_scalar(DC[:, 16:24], DC[:, 16:24], -1.0, 1.0,
                                       op0=ALU.mult, op1=ALU.add))
        vsync0()
        for j in range(3):
            v_op(lambda v, j=j: v.tensor_scalar(
                DC[:, 24 + 8 * j:32 + 8 * j], CC[:, 8 * j:8 * j + 8],
                -1.0, 1.0, op0=ALU.mult, op1=ALU.add))
        v_op(lambda v: v.memset(SCF[:, 0:1], 1.0))
        em.wait_now("tensor", chV)
        for kt in range(8):
            apl = [em.mark_ready(0, chT)] if kt == 7 else ()
            t_mm(PS[0][0:1, 0:16], SCF[:, 0:1], GWT[:, kt * 16:kt * 16 + 16],
                 start=(kt == 0), stop=(kt == 7), apl=apl)
        em.wait_ready("vector", 0)
        v_op(lambda v: v.tensor_copy(SCF[0:1, 4:20], PS[0][0:1, 0:16]))
        em.free_after(0, "vector")
        em.wait_now("tensor", chV)
        t_mm(PTF[:, 0:16], ONESROWF[:], SCF[0:1, 4:20], True, True,
             apl=[em.mark_ready(6, chT)])
        em.wait_ready("vector", 6)
        v_op(lambda v: v.tensor_copy(CS[:], PTF[:, 0:16]))
        em.free_after(6, "vector")

        chSX, chSTA, chSTB, chSTALL = (em.ch[k] for k in
                                       ("sx", "sta", "stb", "stall"))
        xsrc0 = x_in[0].rearrange("(dt p) t -> p dt t", p=128)
        rdst0 = R[:].rearrange("p (dt t) -> p dt t", dt=8)
        dma("gpsimd", rdst0[:, :, 0:TC], xsrc0[:, :, 0:TC], chSX)
        dma("gpsimd", dstg["wqkv_b"][:], din["wqkv"][:], chSTA)
        for c in range(1, NCH):
            dma("gpsimd", rdst0[:, :, c * TC:(c + 1) * TC],
                xsrc0[:, :, c * TC:(c + 1) * TC], chSX)
        dma("gpsimd", dstg["wout_b"][:], din["wout"][:], chSTA)
        for nm in ("wr", "wk", "wv", "wo"):
            dma("gpsimd", dstg[nm + "_b"][:], din[nm][:], chSTB)
        for nm in ("ws1", "ws3", "ws2", "we1", "we3", "we2"):
            dma("gpsimd", dstg[nm + "_b"][:], din[nm][:], chSTALL)


        scb_guard = [0, 0]
        scb_ffn_guard = [0, 0]

        def ln_chunk(c, dest_tile_fn):
            for bank in (0, 1):
                em.bank_acquire(bank)
            for dt in range(NDT):
                par = dt % 2
                xb = SCB[:, par * 1024: par * 1024 + 512]
                x2b = SCB[:, par * 1024 + 512: par * 1024 + 1024]
                if scb_guard[par]:
                    em.wait_val("scalar", chT, scb_guard[par])
                if scb_ffn_guard[par]:
                    em.wait_val("scalar", chV, scb_ffn_guard[par])
                s_act(xb, Rt(dt, c), AF.Copy)
                s_act(x2b, Rt(dt, c), AF.Square)
                em.wait_now("tensor", chS)
                apl0 = [em.mark_ready(0, chT)] if dt == NDT - 1 else ()
                t_mm(PS[0][0:1, 0:512], ONES128B[:, 0:1], xb,
                     start=(dt == 0), stop=(dt == NDT - 1), apl=apl0)
                if dt == NDT - 1:
                    apl1 = [em.mark_ready(1, chT)]
                else:
                    apl1 = [em.mk_inc(chT, 1)]
                t_mm(PS[1][0:1, 0:512], ONES128B[:, 0:1], x2b,
                     start=(dt == 0), stop=(dt == NDT - 1), apl=apl1)
                scb_guard[par] = chT.n
            em.wait_ready("vector", 0)
            em.wait_ready("vector", 1)
            arow = SCF[0:1, 0:512]
            brow = SCF[0:1, 512:1024]
            vrow = SCF[0:1, 1024:1536]
            mrow = SCF[0:1, 1536:2048]
            v_op(lambda v: v.tensor_scalar_mul(mrow, PS[0][0:1, 0:512],
                                               1.0 / D))
            em.free_after(0, "vector")
            v_op(lambda v: v.tensor_scalar_mul(vrow, PS[1][0:1, 0:512],
                                               1.0 / D))
            em.free_after(1, "vector")
            v_op(lambda v: v.tensor_mul(arow, mrow, mrow))
            v_op(lambda v: v.tensor_sub(vrow, vrow, arow))
            em.wait_now("scalar", chV)
            s_act(arow, vrow, AF.Ln, bias=CC[0:1, 208:209])
            s_act(arow, arow, AF.Exp, scale=-0.5)
            em.wait_now("vector", chS)
            v_op(lambda v: v.tensor_mul(brow, mrow, arow))
            v_op(lambda v: v.tensor_scalar_mul(brow, brow, -1.0))
            em.bank_acquire(4)
            em.bank_acquire(5)
            em.wait_now("tensor", chV)
            t_mm(PS[4][:, 0:512], ONESROWF[:], arow, True, True,
                 apl=[em.mark_ready(4, chT)])
            t_mm(PS[5][:, 0:512], ONESROWF[:], brow, True, True,
                 apl=[em.mark_ready(5, chT)])
            em.wait_ready("vector", 4)
            em.wait_ready("vector", 5)
            t1 = SC2[:, 0:512]
            for dt in range(NDT):
                v_op(lambda v, dt=dt: v.tensor_mul(t1, Rt(dt, c),
                                                   PS[4][:, 0:512]))
                v_op(lambda v, dt=dt: v.tensor_add(dest_tile_fn(dt), t1,
                                                   PS[5][:, 0:512]))
                if dt == NDT - 1:
                    em.free_after(4, "vector")
                    em.free_after(5, "vector")

        def phase_attn_a(b):
            for c in range(NCH):
                if b == 0:
                    em.wait_val("scalar", chSX, 16 * (c + 1))
                    em.wait_val("vector", chSX, 16 * (c + 1))
                ln_chunk(c, lambda dt, c=c: A1[:, dt * 2048 + c * TC:
                                               dt * 2048 + c * TC + TC])
            em.barrier(("tensor", "vector", "scalar", "sync"))

            VT = [W2[:, 0:1040], W2[:, 1040:2080]]
            PT = [W2[:, 2080 + i * 512: 2080 + (i + 1) * 512]
                  for i in range(4)]
            for vb in range(2):
                for kt in range(16):
                    v_op(lambda v, vb=vb, kt=kt: v.memset(
                        VT[vb][:, kt * 65 + 64: kt * 65 + 65], 1.0))
            wslab = [W1[:, 0:3072], W1[:, 4096:4096 + 3072]]
            slab_owner = [0, 0]
            wq_tok = {}

            def wq_load(s):
                slot = s % 2
                if slab_owner[slot]:
                    em.wait_val("sync", chT, slab_owner[slot])
                dma("sync",
                    wslab[slot].rearrange("p (kt c) -> p kt c", kt=8),
                    dstg["wqkv_b"][s].rearrange("kt p c -> p kt c"),
                    chD[slot])
                wq_tok[s] = (chD[slot], chD[slot].n)

            vt_done = {}

            def tr_gen(h):
                s_, par = h // 2, h % 2
                rows = slice(par * 64, par * 64 + 64)
                vtb = VT[h % 2]
                for kt in range(16):
                    em.bank_acquire("ptb")
                    t_tr(PTB[:, 0:64],
                         A3[rows, 2 * 2048 + kt * 128:
                            2 * 2048 + kt * 128 + 128],
                         IDB[rows, par * 64: par * 64 + 64],
                         apl=[em.mark_ready("ptb", chT)])
                    em.wait_ready("vector", "ptb")
                    v_op(lambda v, kt=kt, vtb=vtb: v.tensor_copy(
                        vtb[:, kt * 65: kt * 65 + 64], PTB[:, 0:64]))
                    em.free_after("ptb", "vector")
                    yield
                vt_done[h] = chV.n

            def drain(gen):
                if gen is not None:
                    for _ in gen:
                        pass

            em.wait_val("sync", chSTA, 16)
            wq_load(0)
            pend = None
            pt_guard = [0, 0, 0, 0]
            for s in range(8):
                slot = s % 2
                if s + 1 < 8:
                    wq_load(s + 1)
                em.wait_tok("tensor", wq_tok[s])
                qkv_banks = (2, 3, 4, 5)
                bi = 0
                for mt in (2, 0, 1):
                    for c in range(NCH):
                        if mt == 0 and c == 0:
                            em.wait_now("tensor", chS)
                            pend = tr_gen(2 * s)
                        bank = qkv_banks[bi % 4]
                        bi += 1
                        em.bank_acquire(bank)
                        for kt in range(8):
                            last = kt == 7
                            apl = ([em.mark_ready(bank, chT)]
                                   if last else ())
                            t_mm(PS[bank][:, 0:512],
                                 wslab[slot][:, kt * 384 + mt * 128:
                                             kt * 384 + mt * 128 + 128],
                                 A1[:, kt * 2048 + c * TC:
                                    kt * 2048 + c * TC + TC],
                                 start=(kt == 0), stop=last, apl=apl)
                        if pend is not None and mt != 2:
                            next(pend, None)
                            next(pend, None)
                        em.wait_ready("scalar", bank)
                        dst = A3[:, mt * 2048 + c * TC:
                                 mt * 2048 + c * TC + TC]
                        s_act(dst, PS[bank][:, 0:512], AF.Copy)
                        em.free_after(bank, "scalar")
                slab_owner[slot] = chT.n
                drain(pend)
                pend = None
                for par in range(2):
                    h = 2 * s + par
                    if par == 0:
                        em.wait_now("tensor", chS)
                        pend = tr_gen(2 * s + 1)
                    else:
                        drain(pend)
                        pend = None
                    em.wait_val("tensor", chV, vt_done[h])
                    rows = slice(par * 64, par * 64 + 64)
                    vtb = VT[h % 2]
                    for qc in range(NCH):
                        nkt = 4 * (qc + 1)
                        em.bank_acquire(4)
                        exp_tok = {}

                        def emit_pv(kt, qc=qc, nkt=nkt, vtb=vtb):
                            cn, vv = exp_tok[kt]
                            em.wait_val("tensor", em.ch[cn], vv)
                            if kt == nkt - 1:
                                apl = [em.mark_ready(4, chT)]
                            else:
                                apl = [em.mk_inc(chT, 1)]
                            t_mm(PS[4][0:65, 0:512],
                                 vtb[:, kt * 65: kt * 65 + 65], PT[kt % 4],
                                 start=(kt == 0), stop=(kt == nkt - 1),
                                 apl=apl)
                            pt_guard[kt % 4] = chT.n

                        for kt in range(nkt):
                            sbank = 2 + kt % 2
                            em.bank_acquire(sbank)
                            t_mm(PS[sbank][:, 0:512],
                                 A3[rows, 1 * 2048 + kt * 128:
                                    1 * 2048 + kt * 128 + 128],
                                 A3[rows, 0 * 2048 + qc * TC:
                                    0 * 2048 + qc * TC + TC],
                                 start=True, stop=True,
                                 apl=[em.mark_ready(sbank, chT)])
                            if kt >= 2:
                                emit_pv(kt - 2)
                            em.wait_ready("scalar", sbank)
                            pt = PT[kt % 4]
                            if pt_guard[kt % 4]:
                                em.wait_val("scalar", chT, pt_guard[kt % 4])
                            s_act(pt, PS[sbank][:, 0:512], AF.Exp,
                                  scale=0.125)
                            em.free_after(sbank, "scalar")
                            dd = kt - 4 * qc
                            if dd >= 0:
                                off = 384 - 128 * dd
                                em.wait_now("vector", chS)
                                v_op(lambda v, pt=pt, off=off: v.tensor_mul(
                                    pt, pt, STAIRB[:, off:off + 512]))
                                exp_tok[kt] = ("v", chV.n)
                            else:
                                exp_tok[kt] = ("s", chS.n)
                            if pend is not None:
                                next(pend, None)
                        emit_pv(nkt - 2)
                        emit_pv(nkt - 1)
                        em.wait_ready("scalar", 4)
                        odst = A2[rows, s * 2048 + qc * TC:
                                  s * 2048 + qc * TC + TC]
                        s_act(odst, PS[4][0:64, 0:512], AF.Copy)
                        drow = SCF[0:1, par * 2048 + qc * TC:
                                   par * 2048 + qc * TC + TC]
                        s_act(drow, PS[4][64:65, 0:512], AF.Copy)
                        em.free_after(4, "scalar")
                em.wait_now("tensor", chS)
                for qc in range(NCH):
                    em.bank_acquire(5)
                    t_mm(PS[5][:, 0:512], ONESLO[:],
                         SCF[0:1, qc * TC: qc * TC + TC],
                         start=True, stop=False)
                    t_mm(PS[5][:, 0:512], ONESHI[:],
                         SCF[0:1, 2048 + qc * TC: 2048 + qc * TC + TC],
                         start=False, stop=True,
                         apl=[em.mark_ready(5, chT)])
                    em.wait_ready("scalar", 5)
                    rec = SC2[:, 0:512]
                    em.wait_now("scalar", chV)
                    s_act(rec, PS[5][:, 0:512], AF.Ln)
                    em.free_after(5, "scalar")
                    s_act(rec, rec, AF.Exp, scale=-1.0)
                    em.wait_now("vector", chS)
                    ot = A2[:, s * 2048 + qc * TC: s * 2048 + qc * TC + TC]
                    v_op(lambda v, ot=ot: v.tensor_mul(ot, ot, rec))

            em.barrier(("tensor", "vector", "scalar", "sync"))
            wslot = [W1[:, 0:1024], W1[:, 4096:4096 + 1024]]
            owner = [0, 0]
            wo_tok = {}

            def wo_load(et):
                sl = et % 2
                if owner[sl]:
                    em.wait_val("sync", chT, owner[sl])
                dma("sync",
                    wslot[sl].rearrange("p (kt c) -> p kt c", kt=8),
                    dstg["wout_b"][et].rearrange("kt p c -> p kt c"),
                    chD[2 + sl])
                wo_tok[et] = (chD[2 + sl], chD[2 + sl].n)

            em.wait_now("sync", chSTA)
            wo_load(0)
            for et in range(8):
                sl = et % 2
                if et + 1 < 8:
                    wo_load(et + 1)
                em.wait_tok("tensor", wo_tok[et])
                for c in range(NCH):
                    bank = 4 + c % 2
                    em.bank_acquire(bank)
                    for kt in range(8):
                        last = kt == 7
                        apl = [em.mark_ready(bank, chT)] if last else ()
                        t_mm(PS[bank][:, 0:512],
                             wslot[sl][:, kt * 128: kt * 128 + 128],
                             A2[:, kt * 2048 + c * TC:
                                kt * 2048 + c * TC + TC],
                             start=(kt == 0), stop=last, apl=apl)
                    em.wait_ready("vector", bank)
                    v_op(lambda v, et=et, c=c, bank=bank: v.tensor_add(
                        Rt(et, c), Rt(et, c), PS[bank][:, 0:512]))
                    em.free_after(bank, "vector")
                owner[sl] = chT.n
            em.barrier()


        def phase_ffn(b):
            em.wait_now("sync", chSTALL)
            eslot_guard = [0, 0, 0]
            w2hi_guard = [0]
            for c in range(NCH):
                ln_chunk(c, lambda dt: A3[:, dt * 512: dt * 512 + 512])
                em.bank_acquire(0)
                for kt in range(8):
                    apl = [em.mark_ready(0, chT)] if kt == 7 else ()
                    t_mm(PS[0][0:16, 0:512], GWT[:, kt * 16: kt * 16 + 16],
                         Rt(kt, c), start=(kt == 0), stop=(kt == 7), apl=apl)
                em.wait_ready("scalar", 0)
                GPS = SCF[0:18, 3072:3584]
                s_act(GPS[0:16, :], PS[0][0:16, 0:512], AF.Copy)
                em.free_after(0, "scalar")
                arow = SCF[0:1, 0:512]
                brow = SCF[0:1, 512:1024]
                em.wait_now("gpsimd", chV)
                em.wait_now("gpsimd", chS)
                dma("gpsimd", SCF[16:17, 3072:3584], arow, chG)
                dma("gpsimd", SCF[17:18, 3072:3584], brow, chG)
                em.wait_now("tensor", chG)

                def vsync():
                    ap = em.mk_inc(chV, 1)
                    em.on("vector", lambda v, ap=ap: ap(v.engine_nop()))
                    em.wait_now("vector", chV)

                WGS = []
                for j in range(4):
                    base = 2240 + j * 192
                    W = {}
                    o = SCF[:, base:base + 192]
                    W["gt"] = o[:, 0:18]
                    W["t0"] = o[:, 18:34]
                    W["sc"] = o[:, 34:50]
                    W["p0"] = o[:, 50:66]
                    W["s2"] = o[:, 66:82]
                    W["gm"] = o[:, 82:90]
                    W["gs"] = o[:, 90:94]
                    W["keep"] = o[:, 94:98]
                    W["ke"] = o[:, 98:114]
                    W["s2k"] = o[:, 114:130]
                    W["e1"] = o[:, 130:146]
                    W["e2"] = o[:, 146:162]
                    W["eqg"] = o[:, 162:166]
                    W["tmp"] = o[:, 166:182]
                    W["nmx"] = o[:, 182:183]
                    W["sume"] = o[:, 183:184]
                    W["rcp"] = o[:, 184:185]
                    W["r1"] = o[:, 185:186]
                    W["r2"] = o[:, 186:187]
                    W["m1"] = o[:, 187:188]
                    W["m2"] = o[:, 188:189]
                    WGS.append(W)
                for j in range(4):
                    em.bank_acquire("ptf")
                    em.wait_now("tensor", chS)
                    t_tr(PTF[:, 0:18], GPS[0:18, j * 128: j * 128 + 128],
                         IDF[0:18, 0:18], apl=[em.mark_ready("ptf", chT)])
                    em.wait_ready("vector", "ptf")
                    v_op(lambda v, j=j: v.tensor_copy(WGS[j]["gt"],
                                                      PTF[:, 0:18]))
                    em.free_after("ptf", "vector")
                vsync()
                for j in range(4):
                    W = WGS[j]
                    v_op(lambda v, W=W: v.tensor_scalar(
                        W["t0"], W["gt"][:, 0:16], W["gt"][:, 16:17], None,
                        op0=ALU.mult))
                vsync()
                for j in range(4):
                    W = WGS[j]
                    v_op(lambda v, W=W: v.scalar_tensor_tensor(
                        W["sc"], CS[:], W["gt"][:, 17:18], W["t0"],
                        op0=ALU.mult, op1=ALU.add))
                vsync()
                for j in range(4):
                    W = WGS[j]
                    v_op(lambda v, W=W: v.reduce_max(W["nmx"], W["sc"],
                                                     axis=AX.X, negate=True))
                vsync()
                em.wait_now("scalar", chV)
                for j in range(4):
                    W = WGS[j]
                    s_act(W["p0"], W["sc"], AF.Exp, bias=W["nmx"],
                          accum=W["sume"])
                for j in range(4):
                    W = WGS[j]
                    s_act(W["rcp"], W["sume"], AF.Ln)
                for j in range(4):
                    W = WGS[j]
                    s_act(W["rcp"], W["rcp"], AF.Exp, scale=-1.0)
                em.wait_now("vector", chS)
                for j in range(4):
                    W = WGS[j]
                    v_op(lambda v, W=W: v.tensor_scalar(
                        W["p0"], W["p0"], W["rcp"], None, op0=ALU.mult))
                vsync()
                for j in range(4):
                    W = WGS[j]
                    v_op(lambda v, W=W: v.tensor_add(W["s2"], W["p0"], GB[:]))
                vsync()
                for gi in range(4):
                    for j in range(4):
                        W = WGS[j]
                        v_op(lambda v, W=W, gi=gi: v.reduce_max(
                            W["gm"][:, gi:gi + 1],
                            W["s2"][:, 4 * gi:4 * gi + 4], axis=AX.X))
                vsync()
                for gi in range(4):
                    for j in range(4):
                        W = WGS[j]
                        v_op(lambda v, W=W, gi=gi: v.tensor_scalar(
                            W["eqg"], W["s2"][:, 4 * gi:4 * gi + 4],
                            W["gm"][:, gi:gi + 1], None, op0=ALU.is_equal))
                    vsync()
                    for j in range(4):
                        W = WGS[j]
                        v_op(lambda v, W=W, gi=gi: v.scalar_tensor_tensor(
                            W["tmp"][:, 0:4], W["eqg"], -1e30,
                            W["s2"][:, 4 * gi:4 * gi + 4],
                            op0=ALU.mult, op1=ALU.add))
                    vsync()
                    for j in range(4):
                        W = WGS[j]
                        v_op(lambda v, W=W, gi=gi: v.reduce_max(
                            W["gm"][:, 4 + gi:5 + gi], W["tmp"][:, 0:4],
                            axis=AX.X))
                    vsync()
                for j in range(4):
                    W = WGS[j]
                    v_op(lambda v, W=W: v.tensor_add(
                        W["gs"], W["gm"][:, 0:4], W["gm"][:, 4:8]))
                vsync()
                for j in range(4):
                    W = WGS[j]
                    v_op(lambda v, W=W: v.reduce_max(W["r1"], W["gs"],
                                                     axis=AX.X))
                vsync()
                for j in range(4):
                    W = WGS[j]
                    v_op(lambda v, W=W: v.tensor_scalar(
                        W["eqg"], W["gs"], W["r1"], None, op0=ALU.is_equal))
                vsync()
                for j in range(4):
                    W = WGS[j]
                    v_op(lambda v, W=W: v.scalar_tensor_tensor(
                        W["tmp"][:, 0:4], W["eqg"], -1e30, W["gs"],
                        op0=ALU.mult, op1=ALU.add))
                vsync()
                for j in range(4):
                    W = WGS[j]
                    v_op(lambda v, W=W: v.reduce_max(W["r2"],
                                                     W["tmp"][:, 0:4],
                                                     axis=AX.X))
                vsync()
                for j in range(4):
                    W = WGS[j]
                    v_op(lambda v, W=W: v.tensor_scalar(
                        W["keep"], W["gs"], W["r2"], None, op0=ALU.is_ge))
                vsync()
                for j in range(4):
                    W = WGS[j]
                    v_op(lambda v, W=W: v.memset(W["ke"], 1.0))
                vsync()
                for gi in range(4):
                    for j in range(4):
                        W = WGS[j]
                        v_op(lambda v, W=W, gi=gi: v.tensor_scalar(
                            W["ke"][:, 4 * gi:4 * gi + 4],
                            W["ke"][:, 4 * gi:4 * gi + 4],
                            W["keep"][:, gi:gi + 1], None, op0=ALU.mult))
                    vsync()
                for j in range(4):
                    W = WGS[j]
                    v_op(lambda v, W=W: v.tensor_scalar(
                        W["ke"], W["ke"], 1e30, -1e30,
                        op0=ALU.mult, op1=ALU.add))
                vsync()
                for j in range(4):
                    W = WGS[j]
                    v_op(lambda v, W=W: v.tensor_add(W["s2k"], W["s2"],
                                                     W["ke"]))
                vsync()
                for j in range(4):
                    W = WGS[j]
                    v_op(lambda v, W=W: v.reduce_max(W["m1"], W["s2k"],
                                                     axis=AX.X))
                vsync()
                for j in range(4):
                    W = WGS[j]
                    v_op(lambda v, W=W: v.tensor_scalar(
                        W["e1"], W["s2k"], W["m1"], None, op0=ALU.is_equal))
                vsync()
                for j in range(4):
                    W = WGS[j]
                    v_op(lambda v, W=W: v.scalar_tensor_tensor(
                        W["tmp"], W["e1"], -1e30, W["s2k"],
                        op0=ALU.mult, op1=ALU.add))
                vsync()
                for j in range(4):
                    W = WGS[j]
                    v_op(lambda v, W=W: v.reduce_max(W["m2"], W["tmp"],
                                                     axis=AX.X))
                vsync()
                for j in range(4):
                    W = WGS[j]
                    v_op(lambda v, W=W: v.tensor_scalar(
                        W["e2"], W["tmp"], W["m2"], None, op0=ALU.is_equal))
                vsync()
                for j in range(4):
                    W = WGS[j]
                    v_op(lambda v, W=W: v.tensor_add(W["e1"], W["e1"],
                                                     W["e2"]))
                vsync()
                for j in range(4):
                    W = WGS[j]
                    v_op(lambda v, W=W, j=j: v.tensor_mul(
                        CT[:, j * 16: j * 16 + 16], W["p0"], W["e1"]))
                vsync()
                shtok = {}

                def sh_load(it):
                    sl = it % 2
                    base = 4096 + sl * 1024
                    if it < 2:
                        if w2hi_guard[0]:
                            em.wait_val("sync", chT, w2hi_guard[0])
                    elif shguard[sl]:
                        em.wait_val("sync", chT, shguard[sl])
                    ch = chD[sl]
                    dma("sync", W2[:, base:base + 1024].rearrange(
                        "p (kt c) -> p kt c", kt=8),
                        dstg["ws1_b"][it].rearrange("kt p c -> p kt c"), ch)
                    dma("sync", W2[:, 6144 + sl * 1024: 7168 + sl * 1024]
                        .rearrange("p (kt c) -> p kt c", kt=8),
                        dstg["ws3_b"][it].rearrange("kt p c -> p kt c"), ch)
                    shtok[it] = (ch, ch.n)

                shguard = [0, 0]
                sh_load(0)
                for it in range(8):
                    sl = it % 2
                    base = 4096 + sl * 1024
                    if it + 1 < 8:
                        sh_load(it + 1)
                    em.wait_tok("tensor", shtok[it])
                    b1, b3 = (0, 2, 4)[it % 3], (1, 3, 5)[it % 3]
                    em.bank_acquire(b1)
                    em.bank_acquire(b3)
                    for kt in range(8):
                        apl = [em.mark_ready(b1, chT)] if kt == 7 else ()
                        t_mm(PS[b1][:, 0:512],
                             W2[:, base + kt * 128: base + kt * 128 + 128],
                             A3[:, kt * 512: kt * 512 + 512],
                             start=(kt == 0), stop=(kt == 7), apl=apl)
                    for kt in range(8):
                        apl = [em.mark_ready(b3, chT)] if kt == 7 else ()
                        t_mm(PS[b3][:, 0:512],
                             W2[:, 6144 + sl * 1024 + kt * 128:
                                6144 + sl * 1024 + kt * 128 + 128],
                             A3[:, kt * 512: kt * 512 + 512],
                             start=(kt == 0), stop=(kt == 7), apl=apl)
                    shguard[sl] = chT.n
                    em.wait_ready("scalar", b1)
                    s_act(W2[:, it * 512: it * 512 + 512], PS[b1][:, 0:512],
                          AF.Silu, bias=CC[:, 56 + it:57 + it])
                    em.free_after(b1, "scalar")
                    em.wait_ready("scalar", b3)
                    s_act(A1[:, it * 512: it * 512 + 512], PS[b3][:, 0:512],
                          AF.Identity, bias=CC[:, 72 + it:73 + it])
                    em.free_after(b3, "scalar")
                eslot_guard[2] = max(eslot_guard[2], chT.n)

                em.wait_now("vector", chS)
                for it in range(8):
                    v_op(lambda v, it=it: v.tensor_mul(
                        W2[:, it * 512: it * 512 + 512],
                        W2[:, it * 512: it * 512 + 512],
                        A1[:, it * 512: it * 512 + 512]))

                em.wait_now("tensor", chV)
                for j in range(4):
                    em.bank_acquire("ptf")
                    t_tr(PTF[0:16, 0:128], CT[:, j * 16: j * 16 + 16],
                         IDF[:, 0:128], apl=[em.mark_ready("ptf", chT)])
                    em.wait_ready("scalar", "ptf")
                    s_act(CTROW[0:16, j * 128: j * 128 + 128],
                          PTF[0:16, 0:128], AF.Copy)
                    em.free_after("ptf", "scalar")
                em.wait_now("vector", chS)
                v_op(lambda v: v.tensor_copy(CTB[:], CTROW[:]))

                slots = [W1[:, 0:4096], W1[:, 4096:8192], W2[:, 4096:8192]]
                etok = {}

                def e_load(li, nm, e2):
                    if eslot_guard[li % 3]:
                        em.wait_val("sync", chT, eslot_guard[li % 3])
                    ch = chD[4 + li % 3]
                    dma("sync", slots[li % 3].rearrange(
                        "p (kt c) -> p kt c", kt=8),
                        dstg[nm][e2].rearrange("kt p c -> p kt c"), ch)
                    etok[li] = (ch, ch.n)

                em.wait_now("tensor", chV)
                e_load(0, "we1_b", 0)
                e_load(1, "we3_b", 0)
                h1banks = (0, 1, 4, 5)
                h3banks = (2, 3)
                for e in range(NE):
                    if e + 1 < NE:
                        e_load(2 * (e + 1), "we1_b", e + 1)
                    em.bank_acquire(2)
                    t_mm(PS[2][:, 0:512],
                         EXHB[:, e * 128: e * 128 + 128],
                         CTB[:, 0:512], start=True, stop=True,
                         apl=[em.mark_ready(2, chT)])
                    em.wait_ready("scalar", 2)
                    bcb = A3[:, 4096:4608]
                    em.wait_now("scalar", chV)
                    s_act(bcb, PS[2][:, 0:512], AF.Copy)
                    em.free_after(2, "scalar")
                    s1 = slots[(2 * e) % 3]
                    s3 = slots[(2 * e + 1) % 3]
                    em.wait_tok("tensor", etok[2 * e])
                    for it in range(4):
                        b1 = h1banks[it]
                        em.bank_acquire(b1)
                        for kt in range(8):
                            apl = [em.mark_ready(b1, chT)] if kt == 7 else ()
                            t_mm(PS[b1][:, 0:512],
                                 s1[:, kt * 512 + it * 128:
                                    kt * 512 + it * 128 + 128],
                                 A3[:, kt * 512: kt * 512 + 512],
                                 start=(kt == 0), stop=(kt == 7), apl=apl)
                    eslot_guard[(2 * e) % 3] = chT.n
                    if e + 1 < NE:
                        e_load(2 * (e + 1) + 1, "we3_b", e + 1)
                    em.wait_tok("tensor", etok[2 * e + 1])
                    for it in range(4):
                        b1 = h1banks[it]
                        b3 = h3banks[it % 2]
                        em.bank_acquire(b3)
                        for kt in range(8):
                            apl = [em.mark_ready(b3, chT)] if kt == 7 else ()
                            t_mm(PS[b3][:, 0:512],
                                 s3[:, kt * 512 + it * 128:
                                    kt * 512 + it * 128 + 128],
                                 A3[:, kt * 512: kt * 512 + 512],
                                 start=(kt == 0), stop=(kt == 7), apl=apl)
                        par2 = it % 2
                        sa = SCB[:, par2 * 1024: par2 * 1024 + 512]
                        sb = SCB[:, par2 * 1024 + 512: par2 * 1024 + 1024]
                        em.wait_ready("scalar", b1)
                        if scb_ffn_guard[par2]:
                            em.wait_val("scalar", chV, scb_ffn_guard[par2])
                        if scb_guard[par2]:
                            em.wait_val("scalar", chT, scb_guard[par2])
                        ei = e * 4 + it
                        s_act(sa, PS[b1][:, 0:512], AF.Silu,
                              bias=CC[:, 80 + ei:81 + ei])
                        em.free_after(b1, "scalar")
                        em.wait_ready("scalar", b3)
                        s_act(sb, PS[b3][:, 0:512], AF.Identity,
                              bias=CC[:, 144 + ei:145 + ei])
                        em.free_after(b3, "scalar")
                        em.wait_now("vector", chS)
                        n = e * 4 + it
                        if n < 32:
                            chdst = A1[:, n * 512: n * 512 + 512]
                        else:
                            chdst = A2[:, (n - 32) * 512: (n - 32) * 512 + 512]
                        v_op(lambda v, sa=sa, bcb=bcb: v.tensor_mul(
                            sa, sa, bcb))
                        v_op(lambda v, chdst=chdst, sa=sa, sb=sb:
                             v.tensor_mul(chdst, sa, sb))
                        scb_ffn_guard[par2] = chV.n
                    eslot_guard[(2 * e + 1) % 3] = chT.n

                em.wait_now("tensor", chV)
                ybuf = [W2[:, 4096:8192], W1[:, 2048:6144]]
                ybuf_guard = [eslot_guard[2],
                              max(eslot_guard[0], eslot_guard[1])]
                ws2_guard = [max(eslot_guard[0], eslot_guard[1]), 0]
                ytok = {}
                wtok = {}

                def y_load(s):
                    dt2, half = s // 2, s % 2
                    if ybuf_guard[s % 2]:
                        em.wait_val("sync", chT, ybuf_guard[s % 2])
                    ch = chD[6 + s % 2]
                    dma("sync",
                        ybuf[s % 2].rearrange(
                            "p (e it c) -> p e it c", e=8, it=4),
                        dstg["we2_b"][dt2, 8 * half:8 * half + 8]
                        .rearrange("e it p c -> p e it c"), ch)
                    ytok[s] = (ch, ch.n)

                def ws2_load(dt2):
                    sl = dt2 % 2
                    if ws2_guard[sl]:
                        em.wait_val("sync", chT, ws2_guard[sl])
                    ch = chD[2 + sl]
                    dma("sync", W1[:, sl * 1024: sl * 1024 + 1024]
                        .rearrange("p (it c) -> p it c", it=8),
                        dstg["ws2_b"][dt2].rearrange("it p c -> p it c"), ch)
                    wtok[dt2] = (ch, ch.n)

                y_load(0)
                ws2_load(0)
                y_load(1)
                for dt in range(8):
                    em.bank_acquire(2)
                    for s in (2 * dt, 2 * dt + 1):
                        if s % 2 == 1 and dt + 1 < 8:
                            ws2_load(dt + 1)
                        em.wait_tok("tensor", ytok[s])
                        for n0 in range(32):
                            n = (s % 2) * 32 + n0
                            if n < 32:
                                chsrc = A1[:, n * 512: n * 512 + 512]
                            else:
                                chsrc = A2[:, (n - 32) * 512:
                                           (n - 32) * 512 + 512]
                            apl = [em.mk_inc(chT, 1)] if n0 == 31 else ()
                            t_mm(PS[2][:, 0:512],
                                 ybuf[s % 2][:, n0 * 128: n0 * 128 + 128],
                                 chsrc, start=(s % 2 == 0 and n0 == 0),
                                 stop=False, apl=apl)
                        ybuf_guard[s % 2] = chT.n
                        if s + 2 < 16:
                            y_load(s + 2)
                    t_mm(PS[2][:, 0:512], BE2[:, dt * 128: dt * 128 + 128],
                         CTB[:, 0:512], start=False, stop=False)
                    em.wait_tok("tensor", wtok[dt])
                    for it in range(8):
                        last = it == 7
                        apl = [em.mark_ready(2, chT)] if last else ()
                        t_mm(PS[2][:, 0:512],
                             W1[:, (dt % 2) * 1024 + it * 128:
                                (dt % 2) * 1024 + it * 128 + 128],
                             W2[:, it * 512: it * 512 + 512],
                             start=False, stop=last, apl=apl)
                    ws2_guard[dt % 2] = chT.n
                    em.wait_ready("vector", 2)
                    v_op(lambda v, dt=dt, c=c: v.scalar_tensor_tensor(
                        Rt(dt, c), PS[2][:, 0:512], CC[:, 64 + dt:65 + dt],
                        Rt(dt, c), op0=ALU.add, op1=ALU.add))
                    em.free_after(2, "vector")
                eslot_guard[0] = chT.n
                eslot_guard[1] = chT.n
                eslot_guard[2] = chT.n
                w2hi_guard[0] = chT.n
            em.barrier(("tensor", "vector", "scalar", "sync"))

        def phase_attn_b(b):
            TB = 256
            NCB = T // TB

            def vsync():
                ap = em.mk_inc(chV, 1)
                em.on("vector", lambda v, ap=ap: ap(v.engine_nop()))
                em.wait_now("vector", chV)

            sdummy = SCF[0:1, 2238:2239]

            def ssync():
                ap = em.mk_inc(chS, 1)
                em.on("scalar", lambda s, ap=ap: ap(
                    s.activation(out=sdummy, in_=sdummy, func=AF.Copy)))
                em.wait_now("scalar", chS)

            em.wait_now("sync", chSTB)
            v_op(lambda v: v.memset(CAB[:], 0.0))
            v_op(lambda v: v.memset(SS[:], 0.0))
            v_op(lambda v: v.memset(SQ[:], 0.0))
            vsync()
            for c in range(NCH):
                ln_chunk(c, lambda dt, c=c: A1[:, dt * 2048 + c * TC:
                                               dt * 2048 + c * TC + TC])
            em.barrier(("tensor", "vector", "scalar", "sync", "gpsimd"))

            EK = SCF[:, 0:256]
            A_ = SCF[:, 256:512]
            SA = SCF[:, 512:800]
            SB = SCF[:, 800:1088]
            WREP = SCF[:, 1088:1344]
            NN = SCF[:, 1344:1600]
            DD = SCF[:, 1600:1856]
            RR = SCF[:, 1856:2112]
            ss_tmp = SCF[:, 2232:2233]
            sq_tmp = SCF[:, 2233:2234]

            wslot_guard = [0, 0]
            wtok = {}
            mm_done = {}

            def rkv_load(cb, g):
                sl = g % 2
                base = sl * 3072
                if wslot_guard[sl]:
                    em.wait_val("sync", chT, wslot_guard[sl])
                ch = chD[4 + sl]
                for wi, nm in ((0, "wr_b"), (1, "wk_b"), (2, "wv_b")):
                    dma("sync", W1[:, base + wi * 1024:
                                   base + wi * 1024 + 1024].rearrange(
                                       "p (kt c) -> p kt c", kt=8),
                        dstg[nm][g].rearrange("kt p c -> p kt c"), ch)
                wtok[g] = (ch, ch.n)

            for cb in range(NCB):
                em.wait_now("vector", chT)
                for mi, moff in ((0, 0), (1, 2048), (2, 4096)):
                    for dt in range(NDT):
                        xt = A1[:, dt * 2048 + cb * TB:
                                dt * 2048 + cb * TB + TB]
                        dst = A3[:, moff + dt * TB: moff + dt * TB + TB]
                        v_op(lambda v, xt=xt, dst=dst, mi=mi, dt=dt:
                             v.tensor_scalar(dst, xt,
                                             CC[:, 8 * mi + dt:
                                                8 * mi + dt + 1],
                                             None, op0=ALU.mult))
                for mi, moff in ((0, 0), (1, 2048), (2, 4096)):
                    for dt in range(NDT):
                        dst = A3[:, moff + dt * TB: moff + dt * TB + TB]
                        if cb == 0:
                            xp = A1[:, dt * 2048: dt * 2048 + TB - 1]
                            v_op(lambda v, xp=xp, dst=dst, mi=mi, dt=dt:
                                 v.scalar_tensor_tensor(
                                     dst[:, 1:TB], xp,
                                     DC[:, 24 + 8 * mi + dt:
                                        25 + 8 * mi + dt],
                                     dst[:, 1:TB],
                                     op0=ALU.mult, op1=ALU.add))
                        else:
                            xp = A1[:, dt * 2048 + cb * TB - 1:
                                    dt * 2048 + cb * TB + TB - 1]
                            v_op(lambda v, xp=xp, dst=dst, mi=mi, dt=dt:
                                 v.scalar_tensor_tensor(
                                     dst, xp,
                                     DC[:, 24 + 8 * mi + dt:
                                        25 + 8 * mi + dt],
                                     dst, op0=ALU.mult, op1=ALU.add))
                ap_mix = em.mk_inc(chV, 1)
                em.on("vector", lambda v, ap=ap_mix: ap(v.engine_nop()))
                mix_tok = chV.n

                def emit_mms(g):
                    banks = (0, 1, 2) if g % 2 == 0 else (3, 4, 5)
                    sl = g % 2
                    base = sl * 3072
                    em.wait_tok("tensor", wtok[g])
                    em.wait_val("tensor", chV, mix_tok)
                    for bi2, (wi, src_off) in enumerate(
                            ((0, 0), (1, 2048), (2, 4096))):
                        bank = banks[bi2]
                        em.bank_acquire(bank)
                        for kt in range(8):
                            apl = ([em.mark_ready(bank, chT)]
                                   if kt == 7 else ())
                            t_mm(PS[bank][:, 0:TB],
                                 W1[:, base + wi * 1024 + kt * 128:
                                    base + wi * 1024 + kt * 128 + 128],
                                 A3[:, src_off + kt * TB:
                                    src_off + kt * TB + TB],
                                 start=(kt == 0), stop=(kt == 7), apl=apl)
                    wslot_guard[sl] = chT.n
                    mm_done[g] = chT.n

                def emit_chain(g):
                    banks = (0, 1, 2) if g % 2 == 0 else (3, 4, 5)
                    bR, bK, bV = banks
                    em.wait_ready("scalar", bK)
                    em.wait_now("scalar", chV)
                    s_act(EK, PS[bK][:, 0:TB], AF.Exp)
                    em.free_after(bK, "scalar")
                    em.wait_ready("scalar", bR)
                    s_act(RR, PS[bR][:, 0:TB], AF.Exp, scale=-1.0)
                    em.free_after(bR, "scalar")
                    s_act(RR, RR, AF.Ln, bias=1.0)
                    s_act(RR, RR, AF.Exp, scale=-1.0)
                    em.wait_ready("vector", bV)
                    em.wait_now("vector", chS)
                    v_op(lambda v: v.memset(WREP, 1.0))
                    v_op(lambda v, g=g, bV=bV: v.tensor_mul(
                        A_, EK, PS[bV][:, 0:TB]))
                    em.free_after(bV, "vector")
                    v_op(lambda v, g=g: v.tensor_scalar(
                        WREP, WREP, DC[:, g:g + 1], None, op0=ALU.mult))
                    vsync()
                    v_op(lambda v, g=g: v.tensor_tensor_scan(
                        SA[:, 1:257], WREP, A_, CAB[:, g:g + 1],
                        op0=ALU.mult, op1=ALU.add))
                    v_op(lambda v, g=g: v.tensor_tensor_scan(
                        SB[:, 1:257], WREP, EK, CAB[:, 8 + g:9 + g],
                        op0=ALU.mult, op1=ALU.add))
                    v_op(lambda v, g=g: v.tensor_copy(SA[:, 0:1],
                                                      CAB[:, g:g + 1]))
                    v_op(lambda v, g=g: v.tensor_copy(SB[:, 0:1],
                                                      CAB[:, 8 + g:9 + g]))
                    vsync()
                    v_op(lambda v, g=g: v.tensor_copy(CAB[:, g:g + 1],
                                                      SA[:, 256:257]))
                    v_op(lambda v, g=g: v.tensor_copy(CAB[:, 8 + g:9 + g],
                                                      SB[:, 256:257]))
                    v_op(lambda v, g=g: v.tensor_scalar(
                        EK, SA[:, 1:257], DC[:, 8 + g:9 + g], None,
                        op0=ALU.mult))
                    v_op(lambda v, g=g: v.tensor_scalar(
                        A_, SB[:, 1:257], DC[:, 8 + g:9 + g], 1e-9,
                        op0=ALU.mult, op1=ALU.add))
                    vsync()
                    v_op(lambda v, g=g: v.scalar_tensor_tensor(
                        NN, SA[:, 0:256], DC[:, 16 + g:17 + g], EK,
                        op0=ALU.mult, op1=ALU.add))
                    v_op(lambda v, g=g: v.scalar_tensor_tensor(
                        DD, SB[:, 0:256], DC[:, 16 + g:17 + g], A_,
                        op0=ALU.mult, op1=ALU.add))
                    vsync()
                    em.wait_now("scalar", chV)
                    s_act(DD, DD, AF.Ln)
                    s_act(DD, DD, AF.Exp, scale=-1.0)
                    em.wait_now("vector", chS)
                    v_op(lambda v: v.tensor_mul(NN, NN, DD))
                    vsync()
                    v_op(lambda v: v.tensor_mul(A_, NN, RR))
                    vsync()
                    odst = A2[:, g * 2048 + cb * TB:
                              g * 2048 + cb * TB + TB]
                    v_op(lambda v, odst=odst: v.tensor_scalar(
                        odst, A_, 1.0, None, op0=ALU.mult, op1=ALU.add,
                        accum_out=ss_tmp))
                    v_op(lambda v: v.scalar_tensor_tensor(
                        DD, A_, 1.0, A_, op0=ALU.mult, op1=ALU.mult,
                        accum_out=sq_tmp))
                    vsync()
                    v_op(lambda v, g=g: v.tensor_add(
                        SS[:, g:g + 1], SS[:, g:g + 1], ss_tmp))
                    v_op(lambda v, g=g: v.tensor_add(
                        SQ[:, g:g + 1], SQ[:, g:g + 1], sq_tmp))
                    vsync()

                if cb == 0:
                    rkv_load(cb, 0)
                emit_mms(0)
                rkv_load(cb, 1)
                for g in range(8):
                    if g + 1 < 8:
                        emit_mms(g + 1)
                        if g + 2 < 8:
                            rkv_load(cb, g + 2)
                        elif cb + 1 < NCB:
                            rkv_load(cb + 1, 0)
                    emit_chain(g)

            em.barrier(("tensor", "vector", "scalar", "sync"))
            em.bank_acquire("ptf")
            t_mm(PTF[0:2, 0:8], HSEL[:], SS[:], start=True, stop=True)
            t_mm(PTF[0:2, 8:16], HSEL[:], SQ[:], start=True, stop=True,
                 apl=[em.mark_ready("ptf", chT)])
            em.wait_ready("scalar", "ptf")
            HSROW = SCF[0:2, 2240:2256]
            s_act(HSROW, PTF[0:2, 0:16], AF.Copy)
            em.free_after("ptf", "scalar")
            hm = SCF[0:2, 2256:2264]
            hv = SCF[0:2, 2264:2272]
            em.wait_now("vector", chS)
            NINV = 1.0 / (64.0 * 2048.0)
            v_op(lambda v: v.tensor_scalar(hm, HSROW[0:2, 0:8], NINV, None,
                                           op0=ALU.mult))
            v_op(lambda v: v.tensor_scalar(hv, HSROW[0:2, 8:16], NINV, None,
                                           op0=ALU.mult))
            vsync()
            hm2 = SCF[0:2, 2272:2280]
            v_op(lambda v: v.tensor_mul(hm2, hm, hm))
            vsync()
            v_op(lambda v: v.tensor_sub(hv, hv, hm2))
            vsync()
            em.wait_now("scalar", chV)
            s_act(hv, hv, AF.Ln, bias=CC[0:2, 208:209])
            ssync()
            s_act(hv, hv, AF.Exp, scale=-0.5)
            em.wait_now("tensor", chS)
            em.bank_acquire("ptf")
            t_mm(PTF[:, 0:8], HSELT[:], hm, start=True, stop=True)
            t_mm(PTF[:, 8:16], HSELT[:], hv, start=True, stop=True,
                 apl=[em.mark_ready("ptf", chT)])
            em.wait_ready("vector", "ptf")
            v_op(lambda v: v.tensor_mul(SCL[:], PTF[:, 8:16],
                                        CC[:, 40:48]))
            em.free_after("ptf", "vector")
            vsync()
            tb8 = SCF[:, 2280:2288]
            v_op(lambda v: v.tensor_mul(tb8, PTF[:, 0:8], SCL[:]))
            vsync()
            v_op(lambda v: v.tensor_sub(BIA[:], CC[:, 48:56], tb8))
            vsync()
            for g in range(8):
                for c in range(NCH):
                    src_t = A2[:, g * 2048 + c * TC: g * 2048 + c * TC + TC]
                    dst_t = A1[:, g * 2048 + c * TC: g * 2048 + c * TC + TC]
                    v_op(lambda v, src_t=src_t, dst_t=dst_t, g=g:
                         v.tensor_scalar(dst_t, src_t, SCL[:, g:g + 1],
                                         BIA[:, g:g + 1], op0=ALU.mult,
                                         op1=ALU.add))
            vsync()
            woguard = [0, 0]
            for et in range(8):
                sl = et % 2
                if woguard[sl]:
                    em.wait_val("sync", chT, woguard[sl])
                ch = chD[2 + sl]
                dma("sync", W2[:, sl * 1024: sl * 1024 + 1024].rearrange(
                    "p (kt c) -> p kt c", kt=8),
                    dstg["wo_b"][et].rearrange("kt p c -> p kt c"), ch)
                em.wait_now("tensor", ch)
                em.wait_now("tensor", chV)
                for c in range(NCH):
                    bank = 4 + c % 2
                    em.bank_acquire(bank)
                    for kt in range(8):
                        last = kt == 7
                        apl = [em.mark_ready(bank, chT)] if last else ()
                        t_mm(PS[bank][:, 0:512],
                             W2[:, sl * 1024 + kt * 128:
                                sl * 1024 + kt * 128 + 128],
                             A1[:, kt * 2048 + c * TC:
                                kt * 2048 + c * TC + TC],
                             start=(kt == 0), stop=last, apl=apl)
                    em.wait_ready("vector", bank)
                    v_op(lambda v, et=et, c=c, bank=bank: v.tensor_add(
                        Rt(et, c), Rt(et, c), PS[bank][:, 0:512]))
                    em.free_after(bank, "vector")
                woguard[sl] = chT.n
            em.barrier()

        for b in range(NB):
            if b == 0:
                pass
            else:
                dma("gpsimd", R[:].rearrange("p (dt t) -> p dt t", dt=8),
                    x_in[b].rearrange("(dt p) t -> p dt t", p=128), chG)
                em.barrier()
            if "a" in phases:
                phase_attn_a(b)
            if "f1" in phases:
                phase_ffn(b)
            if "b" in phases:
                phase_attn_b(b)
            if "f2" in phases:
                phase_ffn(b)
            ap = em.mk_inc(chO, 16)
            dstp = y_out[b].rearrange("(dt p) t -> p dt t", p=128)
            em.on("sync", lambda e, d=dstp, ap=ap: ap(
                e.dma_start(out=d,
                            in_=R[:].rearrange("p (dt t) -> p dt t", dt=8))))
            em.barrier()

        for nm, out_ap in dump_outs.items():
            buf = {"R": R, "A1": A1, "A2": A2, "A3": A3, "SCF": SCF,
                   "CT": CT, "SS": SS, "SQ": SQ, "DC": DC}[nm]
            ap = em.mk_inc(chO, 16)
            em.on("sync", lambda e, bu=buf, o=out_ap, ap=ap: ap(
                e.dma_start(out=o, in_=bu[:])))
        em.wait_now("sync", chO)

        em.run(block)
    return nc



def _prep_shared_inputs(kw):
    f = np.float32
    out = {}
    wqkv = np.asarray(kw["L0_attn_a_w_qkv"], f)
    cols = []
    for s in range(8):
        for part in range(3):
            for hh in (2 * s, 2 * s + 1):
                base = part * 1024 + hh * 64
                cols.append(np.arange(base, base + 64))
    perm = np.concatenate(cols)
    wq = wqkv[:, perm].reshape(8, 128, 8, 384)
    out["wqkv"] = np.ascontiguousarray(wq.transpose(2, 0, 1, 3))

    def tile_oi(w):
        return np.ascontiguousarray(
            w.reshape(8, 128, 8, 128).transpose(2, 0, 1, 3))

    out["wout"] = tile_oi(np.asarray(kw["L0_attn_a_w_out"], f))
    out["wr"] = tile_oi(np.asarray(kw["L0_attn_b_W_r"], f).T)
    out["wk"] = tile_oi(np.asarray(kw["L0_attn_b_W_k"], f).T)
    out["wv"] = tile_oi(np.asarray(kw["L0_attn_b_W_v"], f).T)
    out["wo"] = tile_oi(np.asarray(kw["L0_attn_b_W_o"], f).T)
    out["we1"] = np.ascontiguousarray(
        np.asarray(kw["L0_ffn_We1"], f).reshape(16, 8, 128, 512))
    out["we3"] = np.ascontiguousarray(
        np.asarray(kw["L0_ffn_We3"], f).reshape(16, 8, 128, 512))
    out["we2"] = np.ascontiguousarray(
        np.asarray(kw["L0_ffn_We2"], f).reshape(16, 4, 128, 8, 128)
        .transpose(3, 0, 1, 2, 4))
    out["ws1"] = tile_oi(np.asarray(kw["L0_ffn_ws1"], f))
    out["ws3"] = tile_oi(np.asarray(kw["L0_ffn_ws3"], f))
    out["ws2"] = tile_oi(np.asarray(kw["L0_ffn_ws2"], f))
    gwt = np.asarray(kw["L0_ffn_gate_w"], f).T
    out["gwt"] = np.ascontiguousarray(
        gwt.reshape(8, 128, 16).transpose(1, 0, 2).reshape(128, 128))
    out["be2m"] = np.asarray(kw["L0_ffn_be2"], f)
    out["gateb"] = np.asarray(kw["L0_ffn_gate_b"], f).reshape(1, 16)

    cc = np.zeros((128, 224), f)

    def pack8(vec):
        return np.asarray(vec, f).reshape(8, 128).T

    cc[:, 0:8] = pack8(kw["L0_attn_b_time_mix_r"])
    cc[:, 8:16] = pack8(kw["L0_attn_b_time_mix_k"])
    cc[:, 16:24] = pack8(kw["L0_attn_b_time_mix_v"])
    cc[:, 24:32] = pack8(np.asarray(kw["L0_attn_b_time_decay"], f).reshape(-1))
    cc[:, 32:40] = pack8(np.asarray(kw["L0_attn_b_time_first"], f).reshape(-1))
    cc[:, 40:48] = pack8(kw["L0_attn_b_gn_weight"])
    cc[:, 48:56] = pack8(kw["L0_attn_b_gn_bias"])
    cc[:, 56:64] = pack8(kw["L0_ffn_bs1"])
    cc[:, 64:72] = pack8(kw["L0_ffn_bs2"])
    cc[:, 72:80] = pack8(kw["L0_ffn_bs3"])
    be1 = np.asarray(kw["L0_ffn_be1"], f)
    be3 = np.asarray(kw["L0_ffn_be3"], f)
    cc[:, 80:144] = be1.reshape(16, 4, 128).transpose(2, 0, 1).reshape(128, 64)
    cc[:, 144:208] = be3.reshape(16, 4, 128).transpose(2, 0, 1).reshape(128, 64)
    cc[:, 208] = 1e-5
    cc[:, 209] = 1e-9
    out["cc"] = cc

    out["idf"] = np.eye(128, dtype=f)
    out["ones128"] = np.ones((128, 128), f)
    out["onesrow"] = np.ones((1, 128), f)
    lo = np.zeros((1, 128), f)
    lo[0, 0:64] = 1.0
    out["oneslo"] = lo
    hi = np.zeros((1, 128), f)
    hi[0, 64:128] = 1.0
    out["oneshi"] = hi
    out["stair"] = (np.arange(896)[None, :] >=
                    (np.arange(128)[:, None] + 384)).astype(f)
    out["exh"] = np.kron(np.eye(16, dtype=f), np.ones((1, 128), f))
    hsel = np.zeros((128, 2), f)
    hsel[0:64, 0] = 1.0
    hsel[64:128, 1] = 1.0
    out["hsel"] = hsel
    out["hselt"] = np.ascontiguousarray(hsel.T)
    return out


def _run(inputs, phases=("a", "f1", "b", "f2"), dumps=(), cores=8):
    x = np.asarray(inputs["L0_attn_a_x"], np.float32)
    shared = _prep_shared_inputs(inputs)
    nc = build_program(phases=phases, dumps=dumps)
    in_maps = []
    for core in range(cores):
        m = dict(shared)
        m["x"] = np.ascontiguousarray(
            x[core * NB:(core + 1) * NB].transpose(0, 2, 1))
        in_maps.append(m)
    res = run_bass_kernel_spmd(nc, in_maps, core_ids=list(range(cores)))
    return res


def kernel(**inputs):
    res = _run(inputs)
    outs = []
    for core in range(8):
        y = np.asarray(res.results[core]["y"])
        outs.append(np.ascontiguousarray(y.transpose(0, 2, 1)))
    return np.concatenate(outs, axis=0)


# version 10
# speedup vs baseline: 1.6262x; 1.0098x over previous
import contextlib
import numpy as np

import concourse.bass as bass
import concourse.mybir as mybir
from concourse.bass_utils import run_bass_kernel_spmd

F32 = mybir.dt.float32
BF16 = mybir.dt.bfloat16
AF = mybir.ActivationFunctionType
ALU = mybir.AluOpType
AX = mybir.AxisListType

D = 1024
T = 2048
H = 16
HD = 64
NE = 16
INTER = 512
NB = 2
NDT = 8
TC = 512
NCH = T // TC
EPS = 1e-5

ENGINES = ("sync", "gpsimd", "vector", "scalar", "tensor")


class Ch:
    def __init__(self, handle):
        self.h = handle
        self.n = 0


class Emitter:

    def __init__(self, sems):
        self.q = {e: [] for e in ENGINES}
        self.ch = {name: Ch(h) for name, h in sems.items()}
        self.bar = {e: self.ch["bar_" + e] for e in ENGINES}
        self.bank_free = {}
        self.bank_ready = {}

    def on(self, eng, fn):
        self.q[eng].append(fn)

    def mk_inc(self, ch, k=1):
        ch.n += k
        h = ch.h

        def apply(instr):
            instr.then_inc(h, k)
            return instr
        return apply

    def tok(self, ch):
        ap = self.mk_inc(ch, 1)
        return ap, (ch, ch.n)

    def wait_tok(self, eng, tok):
        ch, v = tok
        self.wait_val(eng, ch, v)

    def wait_now(self, eng, ch):
        self.wait_val(eng, ch, ch.n)

    def wait_val(self, eng, ch, v):
        h = ch.h
        if v > 0:
            self.on(eng, lambda e: e.wait_ge(h, v))

    def barrier(self, engines=ENGINES):
        dma_chs = {"sync": tuple(["w", "o"] + ["dma%d" % i
                                                for i in range(8)]),
                   "gpsimd": ("g", "sx", "sta", "stb", "stall")}
        for e in engines:
            for cn in dma_chs.get(e, ()):
                self.wait_now(e, self.ch[cn])
        toks = {}
        for e in engines:
            ap = self.mk_inc(self.bar[e], 1)
            self.on(e, lambda eng, ap=ap: ap(eng.nop()))
            toks[e] = self.bar[e].n
        for e in engines:
            for o in engines:
                if o != e:
                    self.wait_val(e, self.bar[o], toks[o])

    def bank_acquire(self, bank):
        if bank in self.bank_free:
            self.wait_tok("tensor", self.bank_free[bank])

    def mark_ready(self, bank, ch):
        ap, tk = self.tok(ch)
        self.bank_ready[bank] = tk
        return ap

    def wait_ready(self, eng, bank):
        self.wait_tok(eng, self.bank_ready[bank])

    def mark_free(self, bank, eng):
        chmap = {"vector": "v", "scalar": "s", "gpsimd": "g", "sync": "w"}
        ap, tk = self.tok(self.ch[chmap[eng]])
        self.bank_free[bank] = tk
        return ap

    def free_after(self, bank, eng):
        chmap = {"vector": "v", "scalar": "s", "gpsimd": "g", "sync": "w"}
        ch = self.ch[chmap[eng]]
        self.bank_free[bank] = (ch, ch.n)

    def run(self, block):
        for e in ENGINES:
            fns = self.q[e]

            def body(eng, fns=fns):
                for fn in fns:
                    fn(eng)
            getattr(block, e)(body)


def build_program(phases=("a", "f1", "b", "f2"), dumps=()):
    nc = bass.Bass("TRN2", target_bir_lowering=False)

    din = {}

    def I(name, shape, dt=F32):
        din[name] = nc.dram_tensor(name, list(shape), dt,
                                   kind="ExternalInput").ap()
        return din[name]

    x_in = I("x", (NB, D, T))
    I("wqkv", (8, 8, 128, 384))
    I("wout", (8, 8, 128, 128))
    for nm in ("wr", "wk", "wv", "wo"):
        I(nm, (8, 8, 128, 128))
    I("we1", (NE, 8, 128, 512))
    I("we3", (NE, 8, 128, 512))
    I("we2", (8, NE, 4, 128, 128))
    I("ws1", (8, 8, 128, 128))
    I("ws3", (8, 8, 128, 128))
    I("ws2", (8, 8, 128, 128))
    I("gwt", (128, 128))
    I("be2m", (16, 1024))
    I("gateb", (1, 16))
    I("cc", (128, 224))
    I("idf", (128, 128))
    I("ones128", (128, 128))
    I("onesrow", (1, 128))
    I("oneslo", (1, 128))
    I("oneshi", (1, 128))
    I("stair", (128, 896))
    I("hsel", (128, 2))
    I("exh", (16, 2048))
    I("hselt", (2, 128))

    y_out = nc.dram_tensor("y", [NB, D, T], F32, kind="ExternalOutput").ap()

    dstg = {}
    for nm, shp in (
        ("wqkv_b", (8, 8, 128, 384)), ("wout_b", (8, 8, 128, 128)),
        ("wr_b", (8, 8, 128, 128)), ("wk_b", (8, 8, 128, 128)),
        ("wv_b", (8, 8, 128, 128)), ("wo_b", (8, 8, 128, 128)),
        ("we1_b", (NE, 8, 128, 512)), ("we3_b", (NE, 8, 128, 512)),
        ("we2_b", (8, NE, 4, 128, 128)),
        ("ws1_b", (8, 8, 128, 128)), ("ws3_b", (8, 8, 128, 128)),
        ("ws2_b", (8, 8, 128, 128)),
    ):
        dstg[nm] = nc.dram_tensor(nm, list(shp), BF16, kind="Internal").ap()

    dump_outs = {}
    for nm in dumps:
        shp, dt = {"R": ([128, 16384], F32), "A1": ([128, 16384], BF16),
                   "A2": ([128, 16384], BF16), "A3": ([128, 6144], BF16),
                   "SCF": ([128, 4096], F32), "CT": ([128, 64], F32),
                   "SS": ([128, 8], F32), "SQ": ([128, 8], F32),
                   "DC": ([128, 64], F32)}[nm]
        dump_outs[nm] = nc.dram_tensor("dump_" + nm, shp, dt,
                                       kind="ExternalOutput").ap()

    es = contextlib.ExitStack()
    with es:
        sbt = lambda n, p, f, dt: es.enter_context(
            nc.sbuf_tensor(n, [p, f], dt))
        R = sbt("R", 128, 16384, F32)
        A1 = sbt("A1", 128, 16384, BF16)
        A2 = sbt("A2", 128, 16384, BF16)
        A3 = sbt("A3", 128, 6144, BF16)
        W1 = sbt("W1", 128, 8192, BF16)
        W2 = sbt("W2", 128, 8192, BF16)
        SCF = sbt("SCF", 128, 4096, F32)
        SC2 = sbt("SC2", 128, 512, F32)
        SCB = sbt("SCB", 128, 2048, BF16)
        GP = SCF[0:18, 3072:3584]
        GT = SCF[:, 2112:2184]
        CT = SCF[:, 2048:2112]
        CTROW = SCF[0:16, 3584:4096]
        CTB = sbt("CTB", 16, 512, BF16)
        CC = sbt("CC", 128, 224, F32)
        DC = sbt("DC", 128, 64, F32)
        GWT = sbt("GWT", 128, 128, F32)
        CS = sbt("CS", 128, 16, F32)
        BE2 = sbt("BE2", 16, 1024, BF16)
        GB = sbt("GB", 128, 16, F32)
        IDF = sbt("IDF", 128, 128, F32)
        IDB = sbt("IDB", 128, 128, BF16)
        ONES128B = sbt("ONES128B", 128, 128, BF16)
        ONESROWF = sbt("ONESROWF", 1, 128, F32)
        ONESROWB = sbt("ONESROWB", 1, 128, BF16)
        ONESLO = sbt("ONESLO", 1, 128, F32)
        ONESHI = sbt("ONESHI", 1, 128, F32)
        STAIRB = sbt("STAIRB", 128, 896, BF16)
        HSEL = sbt("HSEL", 128, 2, F32)
        EXHB = sbt("EXHB", 16, 2048, BF16)
        HSELT = sbt("HSELT", 2, 128, F32)
        SS = SCF[:, 2184:2192]
        SQ = SCF[:, 2192:2200]
        CAB = SCF[:, 2200:2216]
        SCL = SCF[:, 2216:2224]
        BIA = SCF[:, 2224:2232]

        PS = [es.enter_context(nc.psum_tensor("PB%d" % i, [128, 512], F32))
              for i in range(6)]
        PTB = es.enter_context(nc.psum_tensor("PTB", [128, 128], BF16))
        PTF = es.enter_context(nc.psum_tensor("PTF", [128, 128], F32))

        sems = {}
        for nm in (["w", "g", "t", "s", "v", "o", "sx", "sta", "stb",
                    "stall"] + ["bar_" + e for e in ENGINES]
                   + ["dma%d" % i for i in range(8)]):
            sems[nm] = es.enter_context(nc.semaphore("sem_" + nm))

        block = es.enter_context(nc.Block())
        em = Emitter(sems)
        chW, chG, chT, chS, chV, chO = (em.ch[k] for k in
                                        ("w", "g", "t", "s", "v", "o"))
        chD = [em.ch["dma%d" % i] for i in range(8)]

        def dma(eng, dst, src, ch):
            ap = em.mk_inc(ch, 16)
            em.on(eng, lambda e: ap(e.dma_start(out=dst, in_=src)))

        def s_act(out, in_, func, bias=0.0, scale=1.0, accum=None, apl=()):
            ap = em.mk_inc(chS, 1)

            def f(s):
                if accum is not None:
                    ins = s.activation(out=out, in_=in_, func=func, bias=bias,
                                       scale=scale, accum_out=accum)
                else:
                    ins = s.activation(out=out, in_=in_, func=func, bias=bias,
                                       scale=scale)
                ap(ins)
                for a in apl:
                    a(ins)
                return ins
            em.on("scalar", f)

        def v_op(fn, apl=()):
            ap = em.mk_inc(chV, 1)

            def f(v):
                ins = fn(v)
                ap(ins)
                for a in apl:
                    a(ins)
                return ins
            em.on("vector", f)

        def g_op(fn, apl=()):
            ap = em.mk_inc(chG, 1)

            def f(g):
                ins = fn(g)
                ap(ins)
                for a in apl:
                    a(ins)
                return ins
            em.on("gpsimd", f)

        import traceback

        def t_mm(out, lhsT, rhs, start, stop, apl=()):
            site = traceback.extract_stack(limit=3)[0]

            def f(t):
                try:
                    ins = t.matmul(out, lhsT, rhs, start=start, stop=stop)
                except Exception:
                    print("t_mm failed from %s:%d" % (site.filename,
                                                      site.lineno))
                    raise
                for a in apl:
                    a(ins)
                return ins
            em.on("tensor", f)

        def t_tr(out, in_, ident, apl=()):
            def f(t):
                ins = t.transpose(out, in_, ident)
                for a in apl:
                    a(ins)
                return ins
            em.on("tensor", f)

        def Rt(dt, c):
            return R[:, dt * 2048 + c * TC: dt * 2048 + c * TC + TC]

        dma("sync", CC[:], din["cc"][:], chW)
        dma("sync", GWT[:], din["gwt"][:], chW)
        dma("sync", IDF[:], din["idf"][:], chW)
        dma("sync", ONESROWF[:], din["onesrow"][:], chW)
        dma("sync", ONESLO[:], din["oneslo"][:], chW)
        dma("sync", ONESHI[:], din["oneshi"][:], chW)
        dma("sync", HSEL[:], din["hsel"][:], chW)
        dma("sync", HSELT[:], din["hselt"][:], chW)
        dma("sync", GB[:], din["gateb"].broadcast_to([128, 16]), chW)
        dma("gpsimd", STAIRB[:], din["stair"][:], chG)
        dma("gpsimd", ONES128B[:], din["ones128"][:], chG)
        dma("gpsimd", IDB[:], din["idf"][:], chG)
        dma("gpsimd", ONESROWB[:], din["onesrow"][:], chG)
        dma("gpsimd", BE2[:], din["be2m"][:], chG)
        dma("gpsimd", EXHB[:], din["exh"][:], chG)

        for e in ("scalar", "vector", "tensor"):
            em.wait_now(e, chW)
            em.wait_now(e, chG)

        s_act(DC[:, 0:8], CC[:, 24:32], AF.Exp)
        s_act(DC[:, 0:8], DC[:, 0:8], AF.Exp, scale=-1.0)
        s_act(DC[:, 8:16], CC[:, 32:40], AF.Exp)
        em.wait_now("vector", chS)

        def vsync0():
            ap = em.mk_inc(chV, 1)
            em.on("vector", lambda v, ap=ap: ap(v.engine_nop()))
            em.wait_now("vector", chV)

        v_op(lambda v: v.tensor_mul(DC[:, 16:24], DC[:, 0:8], DC[:, 8:16]))
        vsync0()
        v_op(lambda v: v.tensor_scalar(DC[:, 16:24], DC[:, 16:24], -1.0, 1.0,
                                       op0=ALU.mult, op1=ALU.add))
        vsync0()
        for j in range(3):
            v_op(lambda v, j=j: v.tensor_scalar(
                DC[:, 24 + 8 * j:32 + 8 * j], CC[:, 8 * j:8 * j + 8],
                -1.0, 1.0, op0=ALU.mult, op1=ALU.add))
        v_op(lambda v: v.memset(SCF[:, 0:1], 1.0))
        em.wait_now("tensor", chV)
        for kt in range(8):
            apl = [em.mark_ready(0, chT)] if kt == 7 else ()
            t_mm(PS[0][0:1, 0:16], SCF[:, 0:1], GWT[:, kt * 16:kt * 16 + 16],
                 start=(kt == 0), stop=(kt == 7), apl=apl)
        em.wait_ready("vector", 0)
        v_op(lambda v: v.tensor_copy(SCF[0:1, 4:20], PS[0][0:1, 0:16]))
        em.free_after(0, "vector")
        em.wait_now("tensor", chV)
        t_mm(PTF[:, 0:16], ONESROWF[:], SCF[0:1, 4:20], True, True,
             apl=[em.mark_ready(6, chT)])
        em.wait_ready("vector", 6)
        v_op(lambda v: v.tensor_copy(CS[:], PTF[:, 0:16]))
        em.free_after(6, "vector")

        chSX, chSTA, chSTB, chSTALL = (em.ch[k] for k in
                                       ("sx", "sta", "stb", "stall"))
        xsrc0 = x_in[0].rearrange("(dt p) t -> p dt t", p=128)
        rdst0 = R[:].rearrange("p (dt t) -> p dt t", dt=8)
        dma("gpsimd", rdst0[:, :, 0:TC], xsrc0[:, :, 0:TC], chSX)
        dma("gpsimd", W1[:, 0:3072].rearrange("p (kt c) -> p kt c", kt=8),
            din["wqkv"][0].rearrange("kt p c -> p kt c"), chSTA)
        wq0_tok = (chSTA, chSTA.n)
        for c in range(1, NCH):
            dma("gpsimd", rdst0[:, :, c * TC:(c + 1) * TC],
                xsrc0[:, :, c * TC:(c + 1) * TC], chSX)
        dma("gpsimd", dstg["wqkv_b"][:], din["wqkv"][:], chSTA)
        dma("gpsimd", dstg["wout_b"][:], din["wout"][:], chSTA)
        for nm in ("wr", "wk", "wv", "wo"):
            dma("gpsimd", dstg[nm + "_b"][:], din[nm][:], chSTB)
        for nm in ("ws1", "ws3", "ws2", "we1", "we3", "we2"):
            dma("gpsimd", dstg[nm + "_b"][:], din[nm][:], chSTALL)


        scb_guard = [0, 0]
        scb_ffn_guard = [0, 0]

        def ln_chunk(c, dest_tile_fn):
            for bank in (0, 1):
                em.bank_acquire(bank)
            for dt in range(NDT):
                par = dt % 2
                xb = SCB[:, par * 1024: par * 1024 + 512]
                x2b = SCB[:, par * 1024 + 512: par * 1024 + 1024]
                if scb_guard[par]:
                    em.wait_val("scalar", chT, scb_guard[par])
                if scb_ffn_guard[par]:
                    em.wait_val("scalar", chV, scb_ffn_guard[par])
                s_act(xb, Rt(dt, c), AF.Copy)
                s_act(x2b, Rt(dt, c), AF.Square)
                em.wait_now("tensor", chS)
                apl0 = [em.mark_ready(0, chT)] if dt == NDT - 1 else ()
                t_mm(PS[0][0:1, 0:512], ONES128B[:, 0:1], xb,
                     start=(dt == 0), stop=(dt == NDT - 1), apl=apl0)
                if dt == NDT - 1:
                    apl1 = [em.mark_ready(1, chT)]
                else:
                    apl1 = [em.mk_inc(chT, 1)]
                t_mm(PS[1][0:1, 0:512], ONES128B[:, 0:1], x2b,
                     start=(dt == 0), stop=(dt == NDT - 1), apl=apl1)
                scb_guard[par] = chT.n
            em.wait_ready("vector", 0)
            em.wait_ready("vector", 1)
            arow = SCF[0:1, 0:512]
            brow = SCF[0:1, 512:1024]
            vrow = SCF[0:1, 1024:1536]
            mrow = SCF[0:1, 1536:2048]
            v_op(lambda v: v.tensor_scalar_mul(mrow, PS[0][0:1, 0:512],
                                               1.0 / D))
            em.free_after(0, "vector")
            v_op(lambda v: v.tensor_scalar_mul(vrow, PS[1][0:1, 0:512],
                                               1.0 / D))
            em.free_after(1, "vector")
            v_op(lambda v: v.tensor_mul(arow, mrow, mrow))
            v_op(lambda v: v.tensor_sub(vrow, vrow, arow))
            em.wait_now("scalar", chV)
            s_act(arow, vrow, AF.Ln, bias=CC[0:1, 208:209])
            s_act(arow, arow, AF.Exp, scale=-0.5)
            em.wait_now("vector", chS)
            v_op(lambda v: v.tensor_mul(brow, mrow, arow))
            v_op(lambda v: v.tensor_scalar_mul(brow, brow, -1.0))
            em.bank_acquire(4)
            em.bank_acquire(5)
            em.wait_now("tensor", chV)
            t_mm(PS[4][:, 0:512], ONESROWF[:], arow, True, True,
                 apl=[em.mark_ready(4, chT)])
            t_mm(PS[5][:, 0:512], ONESROWF[:], brow, True, True,
                 apl=[em.mark_ready(5, chT)])
            em.wait_ready("vector", 4)
            em.wait_ready("vector", 5)
            t1 = SC2[:, 0:512]
            for dt in range(NDT):
                v_op(lambda v, dt=dt: v.tensor_mul(t1, Rt(dt, c),
                                                   PS[4][:, 0:512]))
                v_op(lambda v, dt=dt: v.tensor_add(dest_tile_fn(dt), t1,
                                                   PS[5][:, 0:512]))
                if dt == NDT - 1:
                    em.free_after(4, "vector")
                    em.free_after(5, "vector")

        def phase_attn_a(b):
            for c in range(NCH):
                if b == 0:
                    em.wait_val("scalar", chSX, 16 * (c + 1))
                    em.wait_val("vector", chSX, 16 * (c + 1))
                ln_chunk(c, lambda dt, c=c: A1[:, dt * 2048 + c * TC:
                                               dt * 2048 + c * TC + TC])
            em.barrier(("tensor", "vector", "scalar", "sync"))

            VT = [W2[:, 0:1040], W2[:, 1040:2080]]
            PT = [W2[:, 2080 + i * 512: 2080 + (i + 1) * 512]
                  for i in range(4)]
            for vb in range(2):
                for kt in range(16):
                    v_op(lambda v, vb=vb, kt=kt: v.memset(
                        VT[vb][:, kt * 65 + 64: kt * 65 + 65], 1.0))
            wslab = [W1[:, 0:3072], W1[:, 4096:4096 + 3072]]
            slab_owner = [0, 0]
            wq_tok = {}

            def wq_load(s):
                slot = s % 2
                if slab_owner[slot]:
                    em.wait_val("sync", chT, slab_owner[slot])
                dma("sync",
                    wslab[slot].rearrange("p (kt c) -> p kt c", kt=8),
                    dstg["wqkv_b"][s].rearrange("kt p c -> p kt c"),
                    chD[slot])
                wq_tok[s] = (chD[slot], chD[slot].n)

            vt_done = {}

            def tr_gen(h):
                s_, par = h // 2, h % 2
                rows = slice(par * 64, par * 64 + 64)
                vtb = VT[h % 2]
                for kt in range(16):
                    em.bank_acquire("ptb")
                    t_tr(PTB[:, 0:64],
                         A3[rows, 2 * 2048 + kt * 128:
                            2 * 2048 + kt * 128 + 128],
                         IDB[rows, par * 64: par * 64 + 64],
                         apl=[em.mark_ready("ptb", chT)])
                    em.wait_ready("vector", "ptb")
                    v_op(lambda v, kt=kt, vtb=vtb: v.tensor_copy(
                        vtb[:, kt * 65: kt * 65 + 64], PTB[:, 0:64]))
                    em.free_after("ptb", "vector")
                    yield
                vt_done[h] = chV.n

            def drain(gen):
                if gen is not None:
                    for _ in gen:
                        pass

            em.wait_val("sync", chSTA, 32)
            if b == 0:
                wq_tok[0] = wq0_tok
            else:
                wq_load(0)
            pend = None
            pt_guard = [0, 0, 0, 0]
            for s in range(8):
                slot = s % 2
                if s + 1 < 8:
                    wq_load(s + 1)
                em.wait_tok("tensor", wq_tok[s])
                qkv_banks = (2, 3, 4, 5)
                bi = 0
                for mt in (2, 0, 1):
                    for c in range(NCH):
                        if mt == 0 and c == 0:
                            em.wait_now("tensor", chS)
                            pend = tr_gen(2 * s)
                        bank = qkv_banks[bi % 4]
                        bi += 1
                        em.bank_acquire(bank)
                        for kt in range(8):
                            last = kt == 7
                            apl = ([em.mark_ready(bank, chT)]
                                   if last else ())
                            t_mm(PS[bank][:, 0:512],
                                 wslab[slot][:, kt * 384 + mt * 128:
                                             kt * 384 + mt * 128 + 128],
                                 A1[:, kt * 2048 + c * TC:
                                    kt * 2048 + c * TC + TC],
                                 start=(kt == 0), stop=last, apl=apl)
                        if pend is not None and mt != 2:
                            next(pend, None)
                            next(pend, None)
                        em.wait_ready("scalar", bank)
                        dst = A3[:, mt * 2048 + c * TC:
                                 mt * 2048 + c * TC + TC]
                        s_act(dst, PS[bank][:, 0:512], AF.Copy)
                        em.free_after(bank, "scalar")
                slab_owner[slot] = chT.n
                drain(pend)
                pend = None
                for par in range(2):
                    h = 2 * s + par
                    if par == 0:
                        em.wait_now("tensor", chS)
                        pend = tr_gen(2 * s + 1)
                    else:
                        drain(pend)
                        pend = None
                    em.wait_val("tensor", chV, vt_done[h])
                    rows = slice(par * 64, par * 64 + 64)
                    vtb = VT[h % 2]
                    for qc in range(NCH):
                        nkt = 4 * (qc + 1)
                        em.bank_acquire(4)
                        exp_tok = {}

                        def emit_pv(kt, qc=qc, nkt=nkt, vtb=vtb):
                            cn, vv = exp_tok[kt]
                            em.wait_val("tensor", em.ch[cn], vv)
                            if kt == nkt - 1:
                                apl = [em.mark_ready(4, chT)]
                            else:
                                apl = [em.mk_inc(chT, 1)]
                            t_mm(PS[4][0:65, 0:512],
                                 vtb[:, kt * 65: kt * 65 + 65], PT[kt % 4],
                                 start=(kt == 0), stop=(kt == nkt - 1),
                                 apl=apl)
                            pt_guard[kt % 4] = chT.n

                        for kt in range(nkt):
                            sbank = 2 + kt % 2
                            em.bank_acquire(sbank)
                            t_mm(PS[sbank][:, 0:512],
                                 A3[rows, 1 * 2048 + kt * 128:
                                    1 * 2048 + kt * 128 + 128],
                                 A3[rows, 0 * 2048 + qc * TC:
                                    0 * 2048 + qc * TC + TC],
                                 start=True, stop=True,
                                 apl=[em.mark_ready(sbank, chT)])
                            if kt >= 2:
                                emit_pv(kt - 2)
                            em.wait_ready("scalar", sbank)
                            pt = PT[kt % 4]
                            if pt_guard[kt % 4]:
                                em.wait_val("scalar", chT, pt_guard[kt % 4])
                            s_act(pt, PS[sbank][:, 0:512], AF.Exp,
                                  scale=0.125)
                            em.free_after(sbank, "scalar")
                            dd = kt - 4 * qc
                            if dd >= 0:
                                off = 384 - 128 * dd
                                em.wait_now("vector", chS)
                                v_op(lambda v, pt=pt, off=off: v.tensor_mul(
                                    pt, pt, STAIRB[:, off:off + 512]))
                                exp_tok[kt] = ("v", chV.n)
                            else:
                                exp_tok[kt] = ("s", chS.n)
                            if pend is not None:
                                next(pend, None)
                        emit_pv(nkt - 2)
                        emit_pv(nkt - 1)
                        em.wait_ready("scalar", 4)
                        odst = A2[rows, s * 2048 + qc * TC:
                                  s * 2048 + qc * TC + TC]
                        s_act(odst, PS[4][0:64, 0:512], AF.Copy)
                        drow = SCF[0:1, par * 2048 + qc * TC:
                                   par * 2048 + qc * TC + TC]
                        s_act(drow, PS[4][64:65, 0:512], AF.Copy)
                        em.free_after(4, "scalar")
                em.wait_now("tensor", chS)
                for qc in range(NCH):
                    em.bank_acquire(5)
                    t_mm(PS[5][:, 0:512], ONESLO[:],
                         SCF[0:1, qc * TC: qc * TC + TC],
                         start=True, stop=False)
                    t_mm(PS[5][:, 0:512], ONESHI[:],
                         SCF[0:1, 2048 + qc * TC: 2048 + qc * TC + TC],
                         start=False, stop=True,
                         apl=[em.mark_ready(5, chT)])
                    em.wait_ready("scalar", 5)
                    rec = SC2[:, 0:512]
                    em.wait_now("scalar", chV)
                    s_act(rec, PS[5][:, 0:512], AF.Ln)
                    em.free_after(5, "scalar")
                    s_act(rec, rec, AF.Exp, scale=-1.0)
                    em.wait_now("vector", chS)
                    ot = A2[:, s * 2048 + qc * TC: s * 2048 + qc * TC + TC]
                    v_op(lambda v, ot=ot: v.tensor_mul(ot, ot, rec))

            em.barrier(("tensor", "vector", "scalar", "sync"))
            wslot = [W1[:, 0:1024], W1[:, 4096:4096 + 1024]]
            owner = [0, 0]
            wo_tok = {}

            def wo_load(et):
                sl = et % 2
                if owner[sl]:
                    em.wait_val("sync", chT, owner[sl])
                dma("sync",
                    wslot[sl].rearrange("p (kt c) -> p kt c", kt=8),
                    dstg["wout_b"][et].rearrange("kt p c -> p kt c"),
                    chD[2 + sl])
                wo_tok[et] = (chD[2 + sl], chD[2 + sl].n)

            em.wait_now("sync", chSTA)
            wo_load(0)
            for et in range(8):
                sl = et % 2
                if et + 1 < 8:
                    wo_load(et + 1)
                em.wait_tok("tensor", wo_tok[et])
                for c in range(NCH):
                    bank = 4 + c % 2
                    em.bank_acquire(bank)
                    for kt in range(8):
                        last = kt == 7
                        apl = [em.mark_ready(bank, chT)] if last else ()
                        t_mm(PS[bank][:, 0:512],
                             wslot[sl][:, kt * 128: kt * 128 + 128],
                             A2[:, kt * 2048 + c * TC:
                                kt * 2048 + c * TC + TC],
                             start=(kt == 0), stop=last, apl=apl)
                    em.wait_ready("vector", bank)
                    v_op(lambda v, et=et, c=c, bank=bank: v.tensor_add(
                        Rt(et, c), Rt(et, c), PS[bank][:, 0:512]))
                    em.free_after(bank, "vector")
                owner[sl] = chT.n
            em.barrier()


        def phase_ffn(b):
            em.wait_now("sync", chSTALL)
            eslot_guard = [0, 0, 0]
            w2hi_guard = [0]
            for c in range(NCH):
                ln_chunk(c, lambda dt: A3[:, dt * 512: dt * 512 + 512])
                em.bank_acquire(0)
                for kt in range(8):
                    apl = [em.mark_ready(0, chT)] if kt == 7 else ()
                    t_mm(PS[0][0:16, 0:512], GWT[:, kt * 16: kt * 16 + 16],
                         Rt(kt, c), start=(kt == 0), stop=(kt == 7), apl=apl)
                em.wait_ready("scalar", 0)
                GPS = SCF[0:18, 3072:3584]
                s_act(GPS[0:16, :], PS[0][0:16, 0:512], AF.Copy)
                em.free_after(0, "scalar")
                arow = SCF[0:1, 0:512]
                brow = SCF[0:1, 512:1024]
                em.wait_now("gpsimd", chV)
                em.wait_now("gpsimd", chS)
                dma("gpsimd", SCF[16:17, 3072:3584], arow, chG)
                dma("gpsimd", SCF[17:18, 3072:3584], brow, chG)
                em.wait_now("tensor", chG)

                def vsync():
                    ap = em.mk_inc(chV, 1)
                    em.on("vector", lambda v, ap=ap: ap(v.engine_nop()))
                    em.wait_now("vector", chV)

                WGS = []
                for j in range(4):
                    base = 2240 + j * 192
                    W = {}
                    o = SCF[:, base:base + 192]
                    W["gt"] = o[:, 0:18]
                    W["t0"] = o[:, 18:34]
                    W["sc"] = o[:, 34:50]
                    W["p0"] = o[:, 50:66]
                    W["s2"] = o[:, 66:82]
                    W["gm"] = o[:, 82:90]
                    W["gs"] = o[:, 90:94]
                    W["keep"] = o[:, 94:98]
                    W["ke"] = o[:, 98:114]
                    W["s2k"] = o[:, 114:130]
                    W["e1"] = o[:, 130:146]
                    W["e2"] = o[:, 146:162]
                    W["eqg"] = o[:, 162:166]
                    W["tmp"] = o[:, 166:182]
                    W["nmx"] = o[:, 182:183]
                    W["sume"] = o[:, 183:184]
                    W["rcp"] = o[:, 184:185]
                    W["r1"] = o[:, 185:186]
                    W["r2"] = o[:, 186:187]
                    W["m1"] = o[:, 187:188]
                    W["m2"] = o[:, 188:189]
                    WGS.append(W)
                for j in range(4):
                    em.bank_acquire("ptf")
                    em.wait_now("tensor", chS)
                    t_tr(PTF[:, 0:18], GPS[0:18, j * 128: j * 128 + 128],
                         IDF[0:18, 0:18], apl=[em.mark_ready("ptf", chT)])
                    em.wait_ready("vector", "ptf")
                    v_op(lambda v, j=j: v.tensor_copy(WGS[j]["gt"],
                                                      PTF[:, 0:18]))
                    em.free_after("ptf", "vector")
                vsync()
                for j in range(4):
                    W = WGS[j]
                    v_op(lambda v, W=W: v.tensor_scalar(
                        W["t0"], W["gt"][:, 0:16], W["gt"][:, 16:17], None,
                        op0=ALU.mult))
                vsync()
                for j in range(4):
                    W = WGS[j]
                    v_op(lambda v, W=W: v.scalar_tensor_tensor(
                        W["sc"], CS[:], W["gt"][:, 17:18], W["t0"],
                        op0=ALU.mult, op1=ALU.add))
                vsync()
                for j in range(4):
                    W = WGS[j]
                    v_op(lambda v, W=W: v.reduce_max(W["nmx"], W["sc"],
                                                     axis=AX.X, negate=True))
                vsync()
                em.wait_now("scalar", chV)
                for j in range(4):
                    W = WGS[j]
                    s_act(W["p0"], W["sc"], AF.Exp, bias=W["nmx"],
                          accum=W["sume"])
                for j in range(4):
                    W = WGS[j]
                    s_act(W["rcp"], W["sume"], AF.Ln)
                for j in range(4):
                    W = WGS[j]
                    s_act(W["rcp"], W["rcp"], AF.Exp, scale=-1.0)
                em.wait_now("vector", chS)
                for j in range(4):
                    W = WGS[j]
                    v_op(lambda v, W=W: v.tensor_scalar(
                        W["p0"], W["p0"], W["rcp"], None, op0=ALU.mult))
                vsync()
                for j in range(4):
                    W = WGS[j]
                    v_op(lambda v, W=W: v.tensor_add(W["s2"], W["p0"], GB[:]))
                vsync()
                for gi in range(4):
                    for j in range(4):
                        W = WGS[j]
                        v_op(lambda v, W=W, gi=gi: v.reduce_max(
                            W["gm"][:, gi:gi + 1],
                            W["s2"][:, 4 * gi:4 * gi + 4], axis=AX.X))
                vsync()
                for gi in range(4):
                    for j in range(4):
                        W = WGS[j]
                        v_op(lambda v, W=W, gi=gi: v.tensor_scalar(
                            W["eqg"], W["s2"][:, 4 * gi:4 * gi + 4],
                            W["gm"][:, gi:gi + 1], None, op0=ALU.is_equal))
                    vsync()
                    for j in range(4):
                        W = WGS[j]
                        v_op(lambda v, W=W, gi=gi: v.scalar_tensor_tensor(
                            W["tmp"][:, 0:4], W["eqg"], -1e30,
                            W["s2"][:, 4 * gi:4 * gi + 4],
                            op0=ALU.mult, op1=ALU.add))
                    vsync()
                    for j in range(4):
                        W = WGS[j]
                        v_op(lambda v, W=W, gi=gi: v.reduce_max(
                            W["gm"][:, 4 + gi:5 + gi], W["tmp"][:, 0:4],
                            axis=AX.X))
                    vsync()
                for j in range(4):
                    W = WGS[j]
                    v_op(lambda v, W=W: v.tensor_add(
                        W["gs"], W["gm"][:, 0:4], W["gm"][:, 4:8]))
                vsync()
                for j in range(4):
                    W = WGS[j]
                    v_op(lambda v, W=W: v.reduce_max(W["r1"], W["gs"],
                                                     axis=AX.X))
                vsync()
                for j in range(4):
                    W = WGS[j]
                    v_op(lambda v, W=W: v.tensor_scalar(
                        W["eqg"], W["gs"], W["r1"], None, op0=ALU.is_equal))
                vsync()
                for j in range(4):
                    W = WGS[j]
                    v_op(lambda v, W=W: v.scalar_tensor_tensor(
                        W["tmp"][:, 0:4], W["eqg"], -1e30, W["gs"],
                        op0=ALU.mult, op1=ALU.add))
                vsync()
                for j in range(4):
                    W = WGS[j]
                    v_op(lambda v, W=W: v.reduce_max(W["r2"],
                                                     W["tmp"][:, 0:4],
                                                     axis=AX.X))
                vsync()
                for j in range(4):
                    W = WGS[j]
                    v_op(lambda v, W=W: v.tensor_scalar(
                        W["keep"], W["gs"], W["r2"], None, op0=ALU.is_ge))
                vsync()
                for j in range(4):
                    W = WGS[j]
                    v_op(lambda v, W=W: v.memset(W["ke"], 1.0))
                vsync()
                for gi in range(4):
                    for j in range(4):
                        W = WGS[j]
                        v_op(lambda v, W=W, gi=gi: v.tensor_scalar(
                            W["ke"][:, 4 * gi:4 * gi + 4],
                            W["ke"][:, 4 * gi:4 * gi + 4],
                            W["keep"][:, gi:gi + 1], None, op0=ALU.mult))
                    vsync()
                for j in range(4):
                    W = WGS[j]
                    v_op(lambda v, W=W: v.tensor_scalar(
                        W["ke"], W["ke"], 1e30, -1e30,
                        op0=ALU.mult, op1=ALU.add))
                vsync()
                for j in range(4):
                    W = WGS[j]
                    v_op(lambda v, W=W: v.tensor_add(W["s2k"], W["s2"],
                                                     W["ke"]))
                vsync()
                for j in range(4):
                    W = WGS[j]
                    v_op(lambda v, W=W: v.reduce_max(W["m1"], W["s2k"],
                                                     axis=AX.X))
                vsync()
                for j in range(4):
                    W = WGS[j]
                    v_op(lambda v, W=W: v.tensor_scalar(
                        W["e1"], W["s2k"], W["m1"], None, op0=ALU.is_equal))
                vsync()
                for j in range(4):
                    W = WGS[j]
                    v_op(lambda v, W=W: v.scalar_tensor_tensor(
                        W["tmp"], W["e1"], -1e30, W["s2k"],
                        op0=ALU.mult, op1=ALU.add))
                vsync()
                for j in range(4):
                    W = WGS[j]
                    v_op(lambda v, W=W: v.reduce_max(W["m2"], W["tmp"],
                                                     axis=AX.X))
                vsync()
                for j in range(4):
                    W = WGS[j]
                    v_op(lambda v, W=W: v.tensor_scalar(
                        W["e2"], W["tmp"], W["m2"], None, op0=ALU.is_equal))
                vsync()
                for j in range(4):
                    W = WGS[j]
                    v_op(lambda v, W=W: v.tensor_add(W["e1"], W["e1"],
                                                     W["e2"]))
                vsync()
                for j in range(4):
                    W = WGS[j]
                    v_op(lambda v, W=W, j=j: v.tensor_mul(
                        CT[:, j * 16: j * 16 + 16], W["p0"], W["e1"]))
                vsync()
                shtok = {}

                def sh_load(it):
                    sl = it % 2
                    base = 4096 + sl * 1024
                    if it < 2:
                        if w2hi_guard[0]:
                            em.wait_val("sync", chT, w2hi_guard[0])
                    elif shguard[sl]:
                        em.wait_val("sync", chT, shguard[sl])
                    ch = chD[sl]
                    dma("sync", W2[:, base:base + 1024].rearrange(
                        "p (kt c) -> p kt c", kt=8),
                        dstg["ws1_b"][it].rearrange("kt p c -> p kt c"), ch)
                    dma("sync", W2[:, 6144 + sl * 1024: 7168 + sl * 1024]
                        .rearrange("p (kt c) -> p kt c", kt=8),
                        dstg["ws3_b"][it].rearrange("kt p c -> p kt c"), ch)
                    shtok[it] = (ch, ch.n)

                shguard = [0, 0]
                sh_load(0)
                for it in range(8):
                    sl = it % 2
                    base = 4096 + sl * 1024
                    if it + 1 < 8:
                        sh_load(it + 1)
                    em.wait_tok("tensor", shtok[it])
                    b1, b3 = (0, 2, 4)[it % 3], (1, 3, 5)[it % 3]
                    em.bank_acquire(b1)
                    em.bank_acquire(b3)
                    for kt in range(8):
                        apl = [em.mark_ready(b1, chT)] if kt == 7 else ()
                        t_mm(PS[b1][:, 0:512],
                             W2[:, base + kt * 128: base + kt * 128 + 128],
                             A3[:, kt * 512: kt * 512 + 512],
                             start=(kt == 0), stop=(kt == 7), apl=apl)
                    for kt in range(8):
                        apl = [em.mark_ready(b3, chT)] if kt == 7 else ()
                        t_mm(PS[b3][:, 0:512],
                             W2[:, 6144 + sl * 1024 + kt * 128:
                                6144 + sl * 1024 + kt * 128 + 128],
                             A3[:, kt * 512: kt * 512 + 512],
                             start=(kt == 0), stop=(kt == 7), apl=apl)
                    shguard[sl] = chT.n
                    em.wait_ready("scalar", b1)
                    s_act(W2[:, it * 512: it * 512 + 512], PS[b1][:, 0:512],
                          AF.Silu, bias=CC[:, 56 + it:57 + it])
                    em.free_after(b1, "scalar")
                    em.wait_ready("scalar", b3)
                    s_act(A1[:, it * 512: it * 512 + 512], PS[b3][:, 0:512],
                          AF.Identity, bias=CC[:, 72 + it:73 + it])
                    em.free_after(b3, "scalar")
                eslot_guard[2] = max(eslot_guard[2], chT.n)

                em.wait_now("vector", chS)
                for it in range(8):
                    v_op(lambda v, it=it: v.tensor_mul(
                        W2[:, it * 512: it * 512 + 512],
                        W2[:, it * 512: it * 512 + 512],
                        A1[:, it * 512: it * 512 + 512]))

                em.wait_now("tensor", chV)
                for j in range(4):
                    em.bank_acquire("ptf")
                    t_tr(PTF[0:16, 0:128], CT[:, j * 16: j * 16 + 16],
                         IDF[:, 0:128], apl=[em.mark_ready("ptf", chT)])
                    em.wait_ready("scalar", "ptf")
                    s_act(CTROW[0:16, j * 128: j * 128 + 128],
                          PTF[0:16, 0:128], AF.Copy)
                    em.free_after("ptf", "scalar")
                em.wait_now("vector", chS)
                v_op(lambda v: v.tensor_copy(CTB[:], CTROW[:]))

                slots = [W1[:, 0:4096], W1[:, 4096:8192], W2[:, 4096:8192]]
                etok = {}

                def e_load(li, nm, e2):
                    if eslot_guard[li % 3]:
                        em.wait_val("sync", chT, eslot_guard[li % 3])
                    ch = chD[4 + li % 3]
                    dma("sync", slots[li % 3].rearrange(
                        "p (kt c) -> p kt c", kt=8),
                        dstg[nm][e2].rearrange("kt p c -> p kt c"), ch)
                    etok[li] = (ch, ch.n)

                em.wait_now("tensor", chV)
                e_load(0, "we1_b", 0)
                e_load(1, "we3_b", 0)
                h1banks = (0, 1, 4, 5)
                h3banks = (2, 3)
                for e in range(NE):
                    if e + 1 < NE:
                        e_load(2 * (e + 1), "we1_b", e + 1)
                    em.bank_acquire(2)
                    t_mm(PS[2][:, 0:512],
                         EXHB[:, e * 128: e * 128 + 128],
                         CTB[:, 0:512], start=True, stop=True,
                         apl=[em.mark_ready(2, chT)])
                    em.wait_ready("scalar", 2)
                    bcb = A3[:, 4096:4608]
                    em.wait_now("scalar", chV)
                    s_act(bcb, PS[2][:, 0:512], AF.Copy)
                    em.free_after(2, "scalar")
                    s1 = slots[(2 * e) % 3]
                    s3 = slots[(2 * e + 1) % 3]
                    em.wait_tok("tensor", etok[2 * e])
                    for it in range(4):
                        b1 = h1banks[it]
                        em.bank_acquire(b1)
                        for kt in range(8):
                            apl = [em.mark_ready(b1, chT)] if kt == 7 else ()
                            t_mm(PS[b1][:, 0:512],
                                 s1[:, kt * 512 + it * 128:
                                    kt * 512 + it * 128 + 128],
                                 A3[:, kt * 512: kt * 512 + 512],
                                 start=(kt == 0), stop=(kt == 7), apl=apl)
                    eslot_guard[(2 * e) % 3] = chT.n
                    if e + 1 < NE:
                        e_load(2 * (e + 1) + 1, "we3_b", e + 1)
                    em.wait_tok("tensor", etok[2 * e + 1])
                    for it in range(4):
                        b1 = h1banks[it]
                        b3 = h3banks[it % 2]
                        em.bank_acquire(b3)
                        for kt in range(8):
                            apl = [em.mark_ready(b3, chT)] if kt == 7 else ()
                            t_mm(PS[b3][:, 0:512],
                                 s3[:, kt * 512 + it * 128:
                                    kt * 512 + it * 128 + 128],
                                 A3[:, kt * 512: kt * 512 + 512],
                                 start=(kt == 0), stop=(kt == 7), apl=apl)
                        par2 = it % 2
                        sa = SCB[:, par2 * 1024: par2 * 1024 + 512]
                        sb = SCB[:, par2 * 1024 + 512: par2 * 1024 + 1024]
                        em.wait_ready("scalar", b1)
                        if scb_ffn_guard[par2]:
                            em.wait_val("scalar", chV, scb_ffn_guard[par2])
                        if scb_guard[par2]:
                            em.wait_val("scalar", chT, scb_guard[par2])
                        ei = e * 4 + it
                        s_act(sa, PS[b1][:, 0:512], AF.Silu,
                              bias=CC[:, 80 + ei:81 + ei])
                        em.free_after(b1, "scalar")
                        em.wait_ready("scalar", b3)
                        s_act(sb, PS[b3][:, 0:512], AF.Identity,
                              bias=CC[:, 144 + ei:145 + ei])
                        em.free_after(b3, "scalar")
                        em.wait_now("vector", chS)
                        n = e * 4 + it
                        if n < 32:
                            chdst = A1[:, n * 512: n * 512 + 512]
                        else:
                            chdst = A2[:, (n - 32) * 512: (n - 32) * 512 + 512]
                        v_op(lambda v, sa=sa, bcb=bcb: v.tensor_mul(
                            sa, sa, bcb))
                        v_op(lambda v, chdst=chdst, sa=sa, sb=sb:
                             v.tensor_mul(chdst, sa, sb))
                        scb_ffn_guard[par2] = chV.n
                    eslot_guard[(2 * e + 1) % 3] = chT.n

                em.wait_now("tensor", chV)
                ybuf = [W2[:, 4096:8192], W1[:, 2048:6144]]
                ybuf_guard = [eslot_guard[2],
                              max(eslot_guard[0], eslot_guard[1])]
                ws2_guard = [max(eslot_guard[0], eslot_guard[1]), 0]
                ytok = {}
                wtok = {}

                def y_load(s):
                    dt2, half = s // 2, s % 2
                    if ybuf_guard[s % 2]:
                        em.wait_val("sync", chT, ybuf_guard[s % 2])
                    ch = chD[6 + s % 2]
                    dma("sync",
                        ybuf[s % 2].rearrange(
                            "p (e it c) -> p e it c", e=8, it=4),
                        dstg["we2_b"][dt2, 8 * half:8 * half + 8]
                        .rearrange("e it p c -> p e it c"), ch)
                    ytok[s] = (ch, ch.n)

                def ws2_load(dt2):
                    sl = dt2 % 2
                    if ws2_guard[sl]:
                        em.wait_val("sync", chT, ws2_guard[sl])
                    ch = chD[2 + sl]
                    dma("sync", W1[:, sl * 1024: sl * 1024 + 1024]
                        .rearrange("p (it c) -> p it c", it=8),
                        dstg["ws2_b"][dt2].rearrange("it p c -> p it c"), ch)
                    wtok[dt2] = (ch, ch.n)

                y_load(0)
                ws2_load(0)
                y_load(1)
                for dt in range(8):
                    em.bank_acquire(2)
                    for s in (2 * dt, 2 * dt + 1):
                        if s % 2 == 1 and dt + 1 < 8:
                            ws2_load(dt + 1)
                        em.wait_tok("tensor", ytok[s])
                        for n0 in range(32):
                            n = (s % 2) * 32 + n0
                            if n < 32:
                                chsrc = A1[:, n * 512: n * 512 + 512]
                            else:
                                chsrc = A2[:, (n - 32) * 512:
                                           (n - 32) * 512 + 512]
                            apl = [em.mk_inc(chT, 1)] if n0 == 31 else ()
                            t_mm(PS[2][:, 0:512],
                                 ybuf[s % 2][:, n0 * 128: n0 * 128 + 128],
                                 chsrc, start=(s % 2 == 0 and n0 == 0),
                                 stop=False, apl=apl)
                        ybuf_guard[s % 2] = chT.n
                        if s + 2 < 16:
                            y_load(s + 2)
                    t_mm(PS[2][:, 0:512], BE2[:, dt * 128: dt * 128 + 128],
                         CTB[:, 0:512], start=False, stop=False)
                    em.wait_tok("tensor", wtok[dt])
                    for it in range(8):
                        last = it == 7
                        apl = [em.mark_ready(2, chT)] if last else ()
                        t_mm(PS[2][:, 0:512],
                             W1[:, (dt % 2) * 1024 + it * 128:
                                (dt % 2) * 1024 + it * 128 + 128],
                             W2[:, it * 512: it * 512 + 512],
                             start=False, stop=last, apl=apl)
                    ws2_guard[dt % 2] = chT.n
                    em.wait_ready("vector", 2)
                    v_op(lambda v, dt=dt, c=c: v.scalar_tensor_tensor(
                        Rt(dt, c), PS[2][:, 0:512], CC[:, 64 + dt:65 + dt],
                        Rt(dt, c), op0=ALU.add, op1=ALU.add))
                    em.free_after(2, "vector")
                eslot_guard[0] = chT.n
                eslot_guard[1] = chT.n
                eslot_guard[2] = chT.n
                w2hi_guard[0] = chT.n
            em.barrier(("tensor", "vector", "scalar", "sync"))

        def phase_attn_b(b):
            TB = 256
            NCB = T // TB

            def vsync():
                ap = em.mk_inc(chV, 1)
                em.on("vector", lambda v, ap=ap: ap(v.engine_nop()))
                em.wait_now("vector", chV)

            sdummy = SCF[0:1, 2238:2239]

            def ssync():
                ap = em.mk_inc(chS, 1)
                em.on("scalar", lambda s, ap=ap: ap(
                    s.activation(out=sdummy, in_=sdummy, func=AF.Copy)))
                em.wait_now("scalar", chS)

            em.wait_now("sync", chSTB)
            v_op(lambda v: v.memset(CAB[:], 0.0))
            v_op(lambda v: v.memset(SS[:], 0.0))
            v_op(lambda v: v.memset(SQ[:], 0.0))
            vsync()
            for c in range(NCH):
                ln_chunk(c, lambda dt, c=c: A1[:, dt * 2048 + c * TC:
                                               dt * 2048 + c * TC + TC])
            em.barrier(("tensor", "vector", "scalar", "sync", "gpsimd"))

            EK = SCF[:, 0:256]
            A_ = SCF[:, 256:512]
            SA = SCF[:, 512:800]
            SB = SCF[:, 800:1088]
            WREP = SCF[:, 1088:1344]
            NN = SCF[:, 1344:1600]
            DD = SCF[:, 1600:1856]
            RR = SCF[:, 1856:2112]
            ss_tmp = SCF[:, 2232:2233]
            sq_tmp = SCF[:, 2233:2234]

            wslot_guard = [0, 0]
            wtok = {}
            mm_done = {}

            def rkv_load(cb, g):
                sl = g % 2
                base = sl * 3072
                if wslot_guard[sl]:
                    em.wait_val("sync", chT, wslot_guard[sl])
                ch = chD[4 + sl]
                for wi, nm in ((0, "wr_b"), (1, "wk_b"), (2, "wv_b")):
                    dma("sync", W1[:, base + wi * 1024:
                                   base + wi * 1024 + 1024].rearrange(
                                       "p (kt c) -> p kt c", kt=8),
                        dstg[nm][g].rearrange("kt p c -> p kt c"), ch)
                wtok[g] = (ch, ch.n)

            for cb in range(NCB):
                em.wait_now("gpsimd", chT)
                for mi, moff in ((0, 0), (1, 2048), (2, 4096)):
                    for dt in range(NDT):
                        xt = A1[:, dt * 2048 + cb * TB:
                                dt * 2048 + cb * TB + TB]
                        dst = A3[:, moff + dt * TB: moff + dt * TB + TB]
                        ccb = CC[:, 8 * mi + dt: 8 * mi + dt + 1]\
                            .broadcast_to([128, TB])
                        g_op(lambda g, xt=xt, dst=dst, ccb=ccb:
                             g.tensor_tensor(dst, xt, ccb, op=ALU.mult))
                for mi, moff in ((0, 0), (1, 2048), (2, 4096)):
                    for dt in range(NDT):
                        st = mi * 8 + dt
                        tmp = W2[:, 2048 + st * 256: 2048 + st * 256 + TB]
                        omb = DC[:, 24 + 8 * mi + dt: 25 + 8 * mi + dt]\
                            .broadcast_to([128, TB])
                        if cb == 0:
                            xp = A1[:, dt * 2048: dt * 2048 + TB - 1]
                            g_op(lambda g, xp=xp, tmp=tmp, omb=omb:
                                 g.tensor_tensor(tmp[:, 1:TB], xp,
                                                 omb[:, 1:TB], op=ALU.mult))
                        else:
                            xp = A1[:, dt * 2048 + cb * TB - 1:
                                    dt * 2048 + cb * TB + TB - 1]
                            g_op(lambda g, xp=xp, tmp=tmp, omb=omb:
                                 g.tensor_tensor(tmp, xp, omb, op=ALU.mult))
                for mi, moff in ((0, 0), (1, 2048), (2, 4096)):
                    for dt in range(NDT):
                        st = mi * 8 + dt
                        dst = A3[:, moff + dt * TB: moff + dt * TB + TB]
                        tmp = W2[:, 2048 + st * 256: 2048 + st * 256 + TB]
                        if cb == 0:
                            g_op(lambda g, dst=dst, tmp=tmp:
                                 g.tensor_tensor(dst[:, 1:TB], dst[:, 1:TB],
                                                 tmp[:, 1:TB], op=ALU.add))
                        else:
                            g_op(lambda g, dst=dst, tmp=tmp:
                                 g.tensor_tensor(dst, dst, tmp, op=ALU.add))
                mix_tok = chG.n

                def emit_mms(g):
                    banks = (0, 1, 2) if g % 2 == 0 else (3, 4, 5)
                    sl = g % 2
                    base = sl * 3072
                    em.wait_tok("tensor", wtok[g])
                    em.wait_val("tensor", chG, mix_tok)
                    for bi2, (wi, src_off) in enumerate(
                            ((0, 0), (1, 2048), (2, 4096))):
                        bank = banks[bi2]
                        em.bank_acquire(bank)
                        for kt in range(8):
                            apl = ([em.mark_ready(bank, chT)]
                                   if kt == 7 else ())
                            t_mm(PS[bank][:, 0:TB],
                                 W1[:, base + wi * 1024 + kt * 128:
                                    base + wi * 1024 + kt * 128 + 128],
                                 A3[:, src_off + kt * TB:
                                    src_off + kt * TB + TB],
                                 start=(kt == 0), stop=(kt == 7), apl=apl)
                    wslot_guard[sl] = chT.n
                    mm_done[g] = chT.n

                def emit_chain(g):
                    banks = (0, 1, 2) if g % 2 == 0 else (3, 4, 5)
                    bR, bK, bV = banks
                    em.wait_ready("scalar", bK)
                    em.wait_now("scalar", chV)
                    s_act(EK, PS[bK][:, 0:TB], AF.Exp)
                    em.free_after(bK, "scalar")
                    em.wait_ready("scalar", bR)
                    s_act(RR, PS[bR][:, 0:TB], AF.Exp, scale=-1.0)
                    em.free_after(bR, "scalar")
                    s_act(RR, RR, AF.Ln, bias=1.0)
                    s_act(RR, RR, AF.Exp, scale=-1.0)
                    em.wait_ready("vector", bV)
                    em.wait_now("vector", chS)
                    v_op(lambda v: v.memset(WREP, 1.0))
                    v_op(lambda v, g=g, bV=bV: v.tensor_mul(
                        A_, EK, PS[bV][:, 0:TB]))
                    em.free_after(bV, "vector")
                    v_op(lambda v, g=g: v.tensor_scalar(
                        WREP, WREP, DC[:, g:g + 1], None, op0=ALU.mult))
                    vsync()
                    v_op(lambda v, g=g: v.tensor_tensor_scan(
                        SA[:, 1:257], WREP, A_, CAB[:, g:g + 1],
                        op0=ALU.mult, op1=ALU.add))
                    v_op(lambda v, g=g: v.tensor_tensor_scan(
                        SB[:, 1:257], WREP, EK, CAB[:, 8 + g:9 + g],
                        op0=ALU.mult, op1=ALU.add))
                    v_op(lambda v, g=g: v.tensor_copy(SA[:, 0:1],
                                                      CAB[:, g:g + 1]))
                    v_op(lambda v, g=g: v.tensor_copy(SB[:, 0:1],
                                                      CAB[:, 8 + g:9 + g]))
                    vsync()
                    v_op(lambda v, g=g: v.tensor_copy(CAB[:, g:g + 1],
                                                      SA[:, 256:257]))
                    v_op(lambda v, g=g: v.tensor_copy(CAB[:, 8 + g:9 + g],
                                                      SB[:, 256:257]))
                    v_op(lambda v, g=g: v.tensor_scalar(
                        EK, SA[:, 1:257], DC[:, 8 + g:9 + g], None,
                        op0=ALU.mult))
                    v_op(lambda v, g=g: v.tensor_scalar(
                        A_, SB[:, 1:257], DC[:, 8 + g:9 + g], 1e-9,
                        op0=ALU.mult, op1=ALU.add))
                    vsync()
                    v_op(lambda v, g=g: v.scalar_tensor_tensor(
                        NN, SA[:, 0:256], DC[:, 16 + g:17 + g], EK,
                        op0=ALU.mult, op1=ALU.add))
                    v_op(lambda v, g=g: v.scalar_tensor_tensor(
                        DD, SB[:, 0:256], DC[:, 16 + g:17 + g], A_,
                        op0=ALU.mult, op1=ALU.add))
                    vsync()
                    em.wait_now("scalar", chV)
                    s_act(DD, DD, AF.Ln)
                    s_act(DD, DD, AF.Exp, scale=-1.0)
                    em.wait_now("vector", chS)
                    v_op(lambda v: v.tensor_mul(NN, NN, DD))
                    vsync()
                    v_op(lambda v: v.tensor_mul(A_, NN, RR))
                    vsync()
                    odst = A2[:, g * 2048 + cb * TB:
                              g * 2048 + cb * TB + TB]
                    v_op(lambda v, odst=odst: v.tensor_scalar(
                        odst, A_, 1.0, None, op0=ALU.mult, op1=ALU.add,
                        accum_out=ss_tmp))
                    v_op(lambda v: v.scalar_tensor_tensor(
                        DD, A_, 1.0, A_, op0=ALU.mult, op1=ALU.mult,
                        accum_out=sq_tmp))
                    vsync()
                    v_op(lambda v, g=g: v.tensor_add(
                        SS[:, g:g + 1], SS[:, g:g + 1], ss_tmp))
                    v_op(lambda v, g=g: v.tensor_add(
                        SQ[:, g:g + 1], SQ[:, g:g + 1], sq_tmp))
                    vsync()

                if cb == 0:
                    rkv_load(cb, 0)
                emit_mms(0)
                rkv_load(cb, 1)
                for g in range(8):
                    if g + 1 < 8:
                        emit_mms(g + 1)
                        if g + 2 < 8:
                            rkv_load(cb, g + 2)
                        elif cb + 1 < NCB:
                            rkv_load(cb + 1, 0)
                    emit_chain(g)

            em.barrier(("tensor", "vector", "scalar", "sync"))
            em.bank_acquire("ptf")
            t_mm(PTF[0:2, 0:8], HSEL[:], SS[:], start=True, stop=True)
            t_mm(PTF[0:2, 8:16], HSEL[:], SQ[:], start=True, stop=True,
                 apl=[em.mark_ready("ptf", chT)])
            em.wait_ready("scalar", "ptf")
            HSROW = SCF[0:2, 2240:2256]
            s_act(HSROW, PTF[0:2, 0:16], AF.Copy)
            em.free_after("ptf", "scalar")
            hm = SCF[0:2, 2256:2264]
            hv = SCF[0:2, 2264:2272]
            em.wait_now("vector", chS)
            NINV = 1.0 / (64.0 * 2048.0)
            v_op(lambda v: v.tensor_scalar(hm, HSROW[0:2, 0:8], NINV, None,
                                           op0=ALU.mult))
            v_op(lambda v: v.tensor_scalar(hv, HSROW[0:2, 8:16], NINV, None,
                                           op0=ALU.mult))
            vsync()
            hm2 = SCF[0:2, 2272:2280]
            v_op(lambda v: v.tensor_mul(hm2, hm, hm))
            vsync()
            v_op(lambda v: v.tensor_sub(hv, hv, hm2))
            vsync()
            em.wait_now("scalar", chV)
            s_act(hv, hv, AF.Ln, bias=CC[0:2, 208:209])
            ssync()
            s_act(hv, hv, AF.Exp, scale=-0.5)
            em.wait_now("tensor", chS)
            em.bank_acquire("ptf")
            t_mm(PTF[:, 0:8], HSELT[:], hm, start=True, stop=True)
            t_mm(PTF[:, 8:16], HSELT[:], hv, start=True, stop=True,
                 apl=[em.mark_ready("ptf", chT)])
            em.wait_ready("vector", "ptf")
            v_op(lambda v: v.tensor_mul(SCL[:], PTF[:, 8:16],
                                        CC[:, 40:48]))
            em.free_after("ptf", "vector")
            vsync()
            tb8 = SCF[:, 2280:2288]
            v_op(lambda v: v.tensor_mul(tb8, PTF[:, 0:8], SCL[:]))
            vsync()
            v_op(lambda v: v.tensor_sub(BIA[:], CC[:, 48:56], tb8))
            vsync()
            for g in range(8):
                for c in range(NCH):
                    src_t = A2[:, g * 2048 + c * TC: g * 2048 + c * TC + TC]
                    dst_t = A1[:, g * 2048 + c * TC: g * 2048 + c * TC + TC]
                    v_op(lambda v, src_t=src_t, dst_t=dst_t, g=g:
                         v.tensor_scalar(dst_t, src_t, SCL[:, g:g + 1],
                                         BIA[:, g:g + 1], op0=ALU.mult,
                                         op1=ALU.add))
            vsync()
            woguard = [0, 0]
            for et in range(8):
                sl = et % 2
                if woguard[sl]:
                    em.wait_val("sync", chT, woguard[sl])
                ch = chD[2 + sl]
                dma("sync", W2[:, sl * 1024: sl * 1024 + 1024].rearrange(
                    "p (kt c) -> p kt c", kt=8),
                    dstg["wo_b"][et].rearrange("kt p c -> p kt c"), ch)
                em.wait_now("tensor", ch)
                em.wait_now("tensor", chV)
                for c in range(NCH):
                    bank = 4 + c % 2
                    em.bank_acquire(bank)
                    for kt in range(8):
                        last = kt == 7
                        apl = [em.mark_ready(bank, chT)] if last else ()
                        t_mm(PS[bank][:, 0:512],
                             W2[:, sl * 1024 + kt * 128:
                                sl * 1024 + kt * 128 + 128],
                             A1[:, kt * 2048 + c * TC:
                                kt * 2048 + c * TC + TC],
                             start=(kt == 0), stop=last, apl=apl)
                    em.wait_ready("vector", bank)
                    v_op(lambda v, et=et, c=c, bank=bank: v.tensor_add(
                        Rt(et, c), Rt(et, c), PS[bank][:, 0:512]))
                    em.free_after(bank, "vector")
                woguard[sl] = chT.n
            em.barrier()

        for b in range(NB):
            if b == 0:
                pass
            else:
                dma("gpsimd", R[:].rearrange("p (dt t) -> p dt t", dt=8),
                    x_in[b].rearrange("(dt p) t -> p dt t", p=128), chG)
                em.barrier()
            if "a" in phases:
                phase_attn_a(b)
            if "f1" in phases:
                phase_ffn(b)
            if "b" in phases:
                phase_attn_b(b)
            if "f2" in phases:
                phase_ffn(b)
            ap = em.mk_inc(chO, 16)
            dstp = y_out[b].rearrange("(dt p) t -> p dt t", p=128)
            em.on("sync", lambda e, d=dstp, ap=ap: ap(
                e.dma_start(out=d,
                            in_=R[:].rearrange("p (dt t) -> p dt t", dt=8))))
            em.barrier()

        for nm, out_ap in dump_outs.items():
            buf = {"R": R, "A1": A1, "A2": A2, "A3": A3, "SCF": SCF,
                   "CT": CT, "SS": SS, "SQ": SQ, "DC": DC}[nm]
            ap = em.mk_inc(chO, 16)
            em.on("sync", lambda e, bu=buf, o=out_ap, ap=ap: ap(
                e.dma_start(out=o, in_=bu[:])))
        em.wait_now("sync", chO)

        em.run(block)
    return nc



def _prep_shared_inputs(kw):
    f = np.float32
    out = {}
    wqkv = np.asarray(kw["L0_attn_a_w_qkv"], f)
    cols = []
    for s in range(8):
        for part in range(3):
            for hh in (2 * s, 2 * s + 1):
                base = part * 1024 + hh * 64
                cols.append(np.arange(base, base + 64))
    perm = np.concatenate(cols)
    wq = wqkv[:, perm].reshape(8, 128, 8, 384)
    out["wqkv"] = np.ascontiguousarray(wq.transpose(2, 0, 1, 3))

    def tile_oi(w):
        return np.ascontiguousarray(
            w.reshape(8, 128, 8, 128).transpose(2, 0, 1, 3))

    out["wout"] = tile_oi(np.asarray(kw["L0_attn_a_w_out"], f))
    out["wr"] = tile_oi(np.asarray(kw["L0_attn_b_W_r"], f).T)
    out["wk"] = tile_oi(np.asarray(kw["L0_attn_b_W_k"], f).T)
    out["wv"] = tile_oi(np.asarray(kw["L0_attn_b_W_v"], f).T)
    out["wo"] = tile_oi(np.asarray(kw["L0_attn_b_W_o"], f).T)
    out["we1"] = np.ascontiguousarray(
        np.asarray(kw["L0_ffn_We1"], f).reshape(16, 8, 128, 512))
    out["we3"] = np.ascontiguousarray(
        np.asarray(kw["L0_ffn_We3"], f).reshape(16, 8, 128, 512))
    out["we2"] = np.ascontiguousarray(
        np.asarray(kw["L0_ffn_We2"], f).reshape(16, 4, 128, 8, 128)
        .transpose(3, 0, 1, 2, 4))
    out["ws1"] = tile_oi(np.asarray(kw["L0_ffn_ws1"], f))
    out["ws3"] = tile_oi(np.asarray(kw["L0_ffn_ws3"], f))
    out["ws2"] = tile_oi(np.asarray(kw["L0_ffn_ws2"], f))
    gwt = np.asarray(kw["L0_ffn_gate_w"], f).T
    out["gwt"] = np.ascontiguousarray(
        gwt.reshape(8, 128, 16).transpose(1, 0, 2).reshape(128, 128))
    out["be2m"] = np.asarray(kw["L0_ffn_be2"], f)
    out["gateb"] = np.asarray(kw["L0_ffn_gate_b"], f).reshape(1, 16)

    cc = np.zeros((128, 224), f)

    def pack8(vec):
        return np.asarray(vec, f).reshape(8, 128).T

    cc[:, 0:8] = pack8(kw["L0_attn_b_time_mix_r"])
    cc[:, 8:16] = pack8(kw["L0_attn_b_time_mix_k"])
    cc[:, 16:24] = pack8(kw["L0_attn_b_time_mix_v"])
    cc[:, 24:32] = pack8(np.asarray(kw["L0_attn_b_time_decay"], f).reshape(-1))
    cc[:, 32:40] = pack8(np.asarray(kw["L0_attn_b_time_first"], f).reshape(-1))
    cc[:, 40:48] = pack8(kw["L0_attn_b_gn_weight"])
    cc[:, 48:56] = pack8(kw["L0_attn_b_gn_bias"])
    cc[:, 56:64] = pack8(kw["L0_ffn_bs1"])
    cc[:, 64:72] = pack8(kw["L0_ffn_bs2"])
    cc[:, 72:80] = pack8(kw["L0_ffn_bs3"])
    be1 = np.asarray(kw["L0_ffn_be1"], f)
    be3 = np.asarray(kw["L0_ffn_be3"], f)
    cc[:, 80:144] = be1.reshape(16, 4, 128).transpose(2, 0, 1).reshape(128, 64)
    cc[:, 144:208] = be3.reshape(16, 4, 128).transpose(2, 0, 1).reshape(128, 64)
    cc[:, 208] = 1e-5
    cc[:, 209] = 1e-9
    out["cc"] = cc

    out["idf"] = np.eye(128, dtype=f)
    out["ones128"] = np.ones((128, 128), f)
    out["onesrow"] = np.ones((1, 128), f)
    lo = np.zeros((1, 128), f)
    lo[0, 0:64] = 1.0
    out["oneslo"] = lo
    hi = np.zeros((1, 128), f)
    hi[0, 64:128] = 1.0
    out["oneshi"] = hi
    out["stair"] = (np.arange(896)[None, :] >=
                    (np.arange(128)[:, None] + 384)).astype(f)
    out["exh"] = np.kron(np.eye(16, dtype=f), np.ones((1, 128), f))
    hsel = np.zeros((128, 2), f)
    hsel[0:64, 0] = 1.0
    hsel[64:128, 1] = 1.0
    out["hsel"] = hsel
    out["hselt"] = np.ascontiguousarray(hsel.T)
    return out


def _run(inputs, phases=("a", "f1", "b", "f2"), dumps=(), cores=8):
    x = np.asarray(inputs["L0_attn_a_x"], np.float32)
    shared = _prep_shared_inputs(inputs)
    nc = build_program(phases=phases, dumps=dumps)
    in_maps = []
    for core in range(cores):
        m = dict(shared)
        m["x"] = np.ascontiguousarray(
            x[core * NB:(core + 1) * NB].transpose(0, 2, 1))
        in_maps.append(m)
    res = run_bass_kernel_spmd(nc, in_maps, core_ids=list(range(cores)))
    return res


def kernel(**inputs):
    res = _run(inputs)
    outs = []
    for core in range(8):
        y = np.asarray(res.results[core]["y"])
        outs.append(np.ascontiguousarray(y.transpose(0, 2, 1)))
    return np.concatenate(outs, axis=0)
